# version 03
# baseline (speedup 1.0000x reference)
import contextlib
import numpy as np
import ml_dtypes
import concourse.bass as bass
import concourse.bacc as bacc
import concourse.mybir as mybir
from concourse.bass_utils import run_bass_kernel_spmd

F32 = mybir.dt.float32
BF16 = mybir.dt.bfloat16
I16 = mybir.dt.int16
I32 = mybir.dt.int32
AF = mybir.ActivationFunctionType
ALU = mybir.AluOpType

T = 2048
D = 4096
KT = 32
NH = 32
NKV = 8
W = 64
EPS = 1e-5
NEG = -1e30
SQDI = float(1.0 / np.sqrt(128.0))


class Sem:
    def __init__(self, nc, name):
        self.h = nc.alloc_semaphore(name)
        self.n = 0

    def inc(self, inst, k=1):
        inst.then_inc(self.h, k)
        self.n += k
        return self.n


def make_consts():
    c = {}
    c["idn32"] = np.eye(128, dtype=np.float32)
    c["idn16"] = np.eye(128, dtype=ml_dtypes.bfloat16)
    c["ones_k"] = np.ones((128, 1), dtype=np.float32)
    c["ones_m"] = np.ones((1, 128), dtype=np.float32)
    half = np.arange(64, dtype=np.float64)
    inv = 1.0 / (10000.0 ** (2.0 * half / 128.0))
    pos = np.arange(T, dtype=np.float64)
    fr = pos[None, :] * inv[:, None]
    emb = np.concatenate([fr, fr], axis=0)
    c["ropec"] = np.cos(emb).astype(np.float32)
    c["ropes"] = np.sin(emb).astype(np.float32)
    R = np.zeros((128, 128), dtype=np.float32)
    for m in range(64):
        R[m + 64, m] = -1.0
        R[m, m + 64] = 1.0
    c["rotT"] = R
    c["ltmask"] = (np.arange(128)[:, None] <= np.arange(128)[None, :]).astype(np.float32)
    a = np.arange(W)[:, None]
    cc = np.arange(2 * W)[None, :]
    rel = a + W - cc
    base = (rel >= 0) & (rel <= W)
    mb = np.where(base, 0.0, NEG).astype(np.float32)
    c["swam"] = mb
    mb0 = mb.copy()
    mb0[:, :W] = NEG
    c["swam0"] = mb0
    c["sh16"] = np.tile(np.arange(16, dtype=np.int16), (128, 1))
    return c


def build(taps=(), upto="full", dbg=False):
    nc = bacc.Bacc("TRN2", target_bir_lowering=False)
    dk = {"kind": "ExternalOutput"} if dbg else {}

    x_in = nc.dram_tensor("x_in", [T, D], F32, kind="ExternalInput")
    wq_d = nc.dram_tensor("wq", [D, D], F32, kind="ExternalInput")
    wk_d = nc.dram_tensor("wk", [D, 1024], F32, kind="ExternalInput")
    wv_d = nc.dram_tensor("wv", [D, 1024], F32, kind="ExternalInput")
    wo_d = nc.dram_tensor("wo", [D, D], F32, kind="ExternalInput")
    qw_d = nc.dram_tensor("qw", [8, D, 128], I32, kind="ExternalInput")
    lut4_d = nc.dram_tensor("lut4", [D, 16], F32, kind="ExternalInput")
    lut8_d = nc.dram_tensor("lut8", [D, 256], F32, kind="ExternalInput")
    bias_d = nc.dram_tensor("biasr", [128, 32], F32, kind="ExternalInput")
    idn32_d = nc.dram_tensor("idn32c", [128, 128], F32, kind="ExternalInput")
    idn16_d = nc.dram_tensor("idn16c", [128, 128], BF16, kind="ExternalInput")
    onesk_d = nc.dram_tensor("ones_kc", [128, 1], F32, kind="ExternalInput")
    onesm_d = nc.dram_tensor("ones_mc", [1, 128], F32, kind="ExternalInput")
    ropec_d = nc.dram_tensor("ropecc", [128, T], F32, kind="ExternalInput")
    ropes_d = nc.dram_tensor("ropesc", [128, T], F32, kind="ExternalInput")
    rotT_d = nc.dram_tensor("rotTc", [128, 128], F32, kind="ExternalInput")
    ltm_d = nc.dram_tensor("ltmaskc", [128, 128], F32, kind="ExternalInput")
    swam_d = nc.dram_tensor("swamc", [W, 2 * W], F32, kind="ExternalInput")
    swam0_d = nc.dram_tensor("swam0c", [W, 2 * W], F32, kind="ExternalInput")
    sh16_d = nc.dram_tensor("sh16c", [128, 16], I16, kind="ExternalInput")

    out_d = nc.dram_tensor("out", [T, D], F32, kind="ExternalOutput")
    tap_d = {name: nc.dram_tensor(f"tap_{name}", [T, D], F32, kind="ExternalOutput")
             for name in taps}

    xA = nc.dram_tensor("xA", [KT, 128, T], F32)
    xB = nc.dram_tensor("xB", [KT, 128, T], F32)
    xC = nc.dram_tensor("xC", [KT, 128, T], F32)
    wpk = nc.dram_tensor("wpk", [2, 32, 128, D], F32)
    wf = nc.dram_tensor("wf", [4, 32, 128, D], BF16)
    wqs = nc.dram_tensor("wqs", [2, 32, 128, D], BF16)
    wks = nc.dram_tensor("wks", [2, 8, 128, D], BF16)
    wvs = nc.dram_tensor("wvs", [2, 8, 128, D], BF16)
    wos = nc.dram_tensor("wos", [2, 32, 128, D], BF16)
    qs_s = nc.dram_tensor("qs_s", [NH, 128, T], F32, **dk)
    qr_s = nc.dram_tensor("qr_s", [2, NH, 128, T], BF16, **dk)
    ks_s = nc.dram_tensor("ks_s", [NKV, 128, T], F32, **dk)
    kr_s = nc.dram_tensor("kr_s", [2, NKV, 128, T], BF16, **dk)
    v_s = nc.dram_tensor("v_s", [NKV, 128, T], F32, **dk)
    g_s = nc.dram_tensor("g_s", [NKV, 128, T], F32, **dk)
    o_s = nc.dram_tensor("o_s", [NH, 128, T], BF16, **dk)
    dbg_d = nc.dram_tensor("dbg_d", [13, 128, 128], F32,
                           kind="ExternalOutput") if dbg else None
    dbg2_d = nc.dram_tensor("dbg2_d", [3, 64, 128], F32,
                            kind="ExternalOutput") if dbg else None

    st = contextlib.ExitStack()
    en = st.enter_context
    with st:
        idn32 = en(nc.sbuf_tensor("c_idn32", [128, 128], F32))
        idn16 = en(nc.sbuf_tensor("c_idn16", [128, 128], BF16))
        onesk = en(nc.sbuf_tensor("c_onesk", [128, 1], F32))
        onesm = en(nc.sbuf_tensor("c_onesm", [1, 128], F32))
        rotT = en(nc.sbuf_tensor("c_rotT", [128, 128], F32))
        ltm = en(nc.sbuf_tensor("c_ltm", [128, 128], F32))
        swam = en(nc.sbuf_tensor("c_swam", [W, 2 * W], F32))
        swam0 = en(nc.sbuf_tensor("c_swam0", [W, 2 * W], F32))
        sh16 = en(nc.sbuf_tensor("c_sh16", [128, 16], I16))
        biasr = en(nc.sbuf_tensor("c_biasr", [128, 32], F32))
        hbias = en(nc.sbuf_tensor("c_hbias", [128, 32], F32))

        eT = Sem(nc, "s_eT")
        eV = Sem(nc, "s_eV")
        eS = Sem(nc, "s_eS")
        eG = Sem(nc, "s_eG")
        SL = [Sem(nc, f"sl{i}") for i in range(56)]
        _next_sl = [0]

        def slots(n):
            i = _next_sl[0]
            _next_sl[0] += n
            assert _next_sl[0] <= len(SL), "out of slot semaphores"
            return SL[i:i + n]

        def slot_reset():
            _next_sl[0] = 0

        dI = Sem(nc, "s_dI")

        with nc.Block() as blk:
            @blk.sync
            def _(sy):
                for dst, src in ((idn32, idn32_d), (idn16, idn16_d), (onesk, onesk_d),
                                 (onesm, onesm_d),
                                 (rotT, rotT_d), (ltm, ltm_d), (swam, swam_d),
                                 (swam0, swam0_d), (sh16, sh16_d), (biasr, bias_d)):
                    dI.inc(sy.dma_start(dst[:], src.ap()), 16)

            @blk.vector
            def _(ve):
                ve.wait_ge(dI.h, dI.n)
                eV.inc(ve.tensor_scalar(hbias[:], biasr[:], 0.5, None, ALU.mult))

        def barrier():
            marks = [(s.h, s.n) for s in (eT, eV, eS, eG, dI)] + \
                    [(s.h, s.n) for s in SL if s.n > 0]
            with nc.Block() as blk:
                for engname in ("sync", "tensor", "vector", "scalar", "gpsimd"):
                    def mk(_eng):
                        for h, v in marks:
                            _eng.wait_ge(h, v)
                    getattr(blk, engname)(mk)

        def pap(t, p0, pstride, pcount, flist):
            ps = t.ap().ap[0][0]
            return bass.AP(t, p0 * ps, [[pstride * ps, pcount]] + flist)

        with contextlib.ExitStack() as sst, nc.Block() as blk:
            sen = sst.enter_context
            ebuf = [sen(nc.sbuf_tensor(f"en_b{i}", [128, D], F32)) for i in range(2)]
            evac = [sen(nc.sbuf_tensor(f"en_e{i}", [128, 128], F32)) for i in range(4)]
            epp = [sen(nc.psum_tensor(f"en_p{i}", [128, 512], F32)) for i in range(4)]
            ld = slots(2)
            wr = slots(4)
            b_eT, b_eV = eT.n, eV.n
            ldv = [ld[tt % 2].n + 16 * (tt // 2 + 1) for tt in range(16)]
            wrv0 = [w.n for w in wr]

            @blk.sync
            def _(sy):
                sy.wait_ge(dI.h, dI.n)
                for tt in range(16):
                    if tt >= 2:
                        sy.wait_ge(eT.h, b_eT + 32 * (tt - 1))
                    ld[tt % 2].inc(sy.dma_start(ebuf[tt % 2][:],
                                                x_in[128 * tt:128 * (tt + 1), :]), 16)

            @blk.tensor
            def _(te):
                for tt in range(16):
                    te.wait_ge(ld[tt % 2].h, ldv[tt])
                    for k in range(KT):
                        idx = tt * KT + k
                        if idx >= 4:
                            te.wait_ge(eV.h, b_eV + idx - 3)
                        eT.inc(te.transpose(epp[idx % 4][:, 0:128],
                                            ebuf[tt % 2][:, 128 * k:128 * (k + 1)],
                                            idn32[:]))

            @blk.vector
            def _(ve):
                for tt in range(16):
                    for k in range(KT):
                        idx = tt * KT + k
                        ve.wait_ge(eT.h, b_eT + idx + 1)
                        if idx >= 4:
                            ve.wait_ge(wr[idx % 4].h, wrv0[idx % 4] + 16 * (idx // 4))
                        eV.inc(ve.tensor_copy(evac[idx % 4][:], epp[idx % 4][:, 0:128]))

            @blk.scalar
            def _(sc):
                for tt in range(16):
                    for k in range(KT):
                        idx = tt * KT + k
                        sc.wait_ge(eV.h, b_eV + idx + 1)
                        wr[idx % 4].inc(sc.dma_start(
                            xA[k, :, 128 * tt:128 * (tt + 1)], evac[idx % 4][:]), 16)

        barrier()
        slot_reset()

        if upto != "entry":
            with contextlib.ExitStack() as sst, nc.Block() as blk:
                sen = sst.enter_context
                lutb = [sen(nc.sbuf_tensor(f"dq_lut{i}", [128, 272], F32)) for i in range(2)]
                tmpb = sen(nc.sbuf_tensor("dq_tb", [128, 272], BF16))
                lob = sen(nc.sbuf_tensor("dq_lo", [128, 272], BF16))
                h32 = sen(nc.sbuf_tensor("dq_h32", [128, 272], I32))
                l32 = sen(nc.sbuf_tensor("dq_l32", [128, 272], I32))
                lutpk = [sen(nc.sbuf_tensor(f"dq_pk{i}", [128, 272], F32)) for i in range(2)]
                qwb = [sen(nc.sbuf_tensor(f"dq_qw{i}", [128, 256], I16)) for i in range(8)]
                shexp = sen(nc.sbuf_tensor("dq_shx", [128, 4096], I16))
                c8a = sen(nc.sbuf_tensor("dq_c8a", [128, 4096], I16))
                ccat = sen(nc.sbuf_tensor("dq_cc", [128, 8192], I16))
                cT = [sen(nc.sbuf_tensor(f"dq_cT{i}", [128, 8192], I16)) for i in range(2)]
                idxb = [sen(nc.sbuf_tensor(f"dq_ix{i}", [128, 512], I16)) for i in range(4)]
                gbuf = [sen(nc.sbuf_tensor(f"dq_gb{i}", [128, 8192], F32)) for i in range(2)]

                s_lut = slots(2)
                s_qw = slots(8)
                s_idx = slots(4)
                s_ext = slots(4)
                b_eV, b_eG = eV.n, eG.n
                VPT = 40

                def vm(tau, ph):
                    return b_eV + VPT * tau + (7 if ph == 1 else (VPT if ph == 2 else 0))

                lut_v = [[s_lut[j].n + 16 * (tau + 1) for j in range(2)] for tau in range(32)]
                qw_v = [[s_qw[p].n + 16 * (tau + 1) for p in range(8)] for tau in range(32)]
                idx_v = [s_idx[gi % 4].n + 16 * (gi // 4 + 1) for gi in range(512)]
                ext_v = [[s_ext[(gi % 2) * 2 + j].n + 16 * (gi // 2 + 1) for j in range(2)]
                         for gi in range(512)]

                @blk.sync
                def _(sy):
                    for tau in range(32):
                        if tau >= 2:
                            sy.wait_ge(eV.h, vm(tau - 2, 1))
                        sy.dma_start(lutb[tau % 2][:, 0:256],
                                     lut8_d[128 * tau:128 * (tau + 1), :]).then_inc(
                            s_lut[0].h, 16)
                        s_lut[0].n += 16
                        sy.dma_start(lutb[tau % 2][:, 256:272],
                                     lut4_d[128 * tau:128 * (tau + 1), :]).then_inc(
                            s_lut[1].h, 16)
                        s_lut[1].n += 16
                        if tau >= 1:
                            sy.wait_ge(eV.h, vm(tau - 1, 2))
                        for p in range(8):
                            sy.dma_start(qwb[p][:],
                                         qw_d[p, 128 * tau:128 * (tau + 1), :]
                                         .bitcast(I16)).then_inc(s_qw[p].h, 16)
                            s_qw[p].n += 16
                        sy.wait_ge(eV.h, vm(tau, 2))
                        for r in range(16):
                            gi = tau * 16 + r
                            if gi >= 4:
                                sy.wait_ge(eG.h, b_eG + gi - 3)
                            src = pap(cT[tau % 2], r, 16, 8, [[512, 16], [1, 512]])
                            sy.dma_start(idxb[gi % 4][:], src).then_inc(s_idx[gi % 4].h, 16)
                            s_idx[gi % 4].n += 16

                @blk.vector
                def _(ve):
                    for tau in range(32):
                        pk = lutpk[tau % 2]
                        ve.wait_ge(s_lut[0].h, lut_v[tau][0])
                        ve.wait_ge(s_lut[1].h, lut_v[tau][1])
                        if tau >= 2:
                            ve.wait_ge(eG.h, b_eG + 16 * (tau - 1))
                        eV.inc(ve.tensor_copy(tmpb[:], lutb[tau % 2][:]))
                        eV.inc(ve.tensor_tensor(lob[:], lutb[tau % 2][:], tmpb[:],
                                                ALU.subtract))
                        eV.inc(ve.tensor_copy(h32[:], tmpb[:].bitcast(I16)))
                        eV.inc(ve.tensor_copy(l32[:], lob[:].bitcast(I16)))
                        eV.inc(ve.tensor_scalar(l32[:], l32[:], 65535, None,
                                                ALU.bitwise_and))
                        eV.inc(ve.tensor_scalar(h32[:], h32[:], 16, None,
                                                ALU.logical_shift_left))
                        eV.inc(ve.tensor_tensor(pk.ap().bitcast(I32), h32[:], l32[:],
                                                ALU.bitwise_or))
                        for p in range(8):
                            ve.wait_ge(s_qw[p].h, qw_v[tau][p])
                            qbc = bass.AP(qwb[p], 0,
                                          [list(qwb[p].ap().ap[0]), [1, 256], [0, 16]])
                            shbc = bass.AP(sh16, 0,
                                           [list(sh16.ap().ap[0]), [0, 256], [1, 16]])
                            eV.inc(ve.tensor_tensor(shexp[:], qbc, shbc,
                                                    ALU.logical_shift_right))
                            if p == 0:
                                eV.inc(ve.tensor_scalar(c8a[:], shexp[:], 1,
                                                        None, ALU.bitwise_and))
                            else:
                                eV.inc(ve.tensor_scalar(shexp[:], shexp[:], 1, None,
                                                        ALU.bitwise_and))
                                eV.inc(ve.tensor_scalar(shexp[:], shexp[:], p, None,
                                                        ALU.logical_shift_left))
                                src_acc = c8a[:] if p % 2 == 1 else ccat[:, 0:4096]
                                dst_acc = ccat[:, 0:4096] if p % 2 == 1 else c8a[:]
                                eV.inc(ve.tensor_tensor(dst_acc, shexp[:], src_acc,
                                                        ALU.bitwise_or))
                        eV.inc(ve.tensor_scalar(c8a[:], ccat[:, 0:4096], 15, None,
                                                ALU.bitwise_and))
                        eV.inc(ve.tensor_scalar(ccat[:, 4096:8192], c8a[:], 256, None,
                                                ALU.add))
                        if tau >= 2:
                            for j in range(4):
                                ve.wait_ge(s_idx[j].h, idx_v[(tau - 2) * 16 + 12 + j])
                        eV.inc(ve.tensor_copy(
                            cT[tau % 2][:],
                            bass.AP(ccat, 0, [list(ccat.ap().ap[0]), [1, 16], [16, 512]])))

                @blk.gpsimd
                def _(gp):
                    for tau in range(32):
                        gp.wait_ge(eV.h, vm(tau, 1))
                        for r in range(16):
                            gi = tau * 16 + r
                            gp.wait_ge(s_idx[gi % 4].h, idx_v[gi])
                            if gi >= 2:
                                gp.wait_ge(s_ext[(gi % 2) * 2].h, ext_v[gi - 2][0])
                                gp.wait_ge(s_ext[(gi % 2) * 2 + 1].h, ext_v[gi - 2][1])
                            eG.inc(gp.ap_gather(gbuf[gi % 2][:], lutpk[tau % 2][:],
                                                idxb[gi % 4][:], channels=128,
                                                num_elems=272, d=1, num_idxs=8192))

                @blk.scalar
                def _(sc):
                    for tau in range(32):
                        for r in range(16):
                            gi = tau * 16 + r
                            sc.wait_ge(eG.h, b_eG + gi + 1)
                            for j, off in ((0, 0), (1, 4096)):
                                src = bass.AP(gbuf[gi % 2],
                                              r * gbuf[gi % 2].ap().ap[0][0] + off,
                                              [[16 * gbuf[gi % 2].ap().ap[0][0], 8],
                                               [1, 4096]])
                                dst = bass.AP(wpk, ((j * 32 + tau) * 128 + r) * D,
                                              [[16 * D, 8], [1, 4096]])
                                sc.dma_start(dst, src).then_inc(
                                    s_ext[(gi % 2) * 2 + j].h, 16)
                                s_ext[(gi % 2) * 2 + j].n += 16

            barrier()
            slot_reset()

            with contextlib.ExitStack() as sst, nc.Block() as blk:
                sen = sst.enter_context
                pbuf = [sen(nc.sbuf_tensor(f"wp_b{i}", [128, 4096], F32)) for i in range(2)]
                pl = [sen(nc.sbuf_tensor(f"wp_l{i}", [128, 2 * 4096], BF16)) for i in range(2)]
                abuf = [sen(nc.sbuf_tensor(f"wp_a{i}", [128, 2 * 4096], BF16)) for i in range(2)]
                wpp = [sen(nc.psum_tensor(f"wp_p{i}", [128, 1024], BF16)) for i in range(4)]
                s_ld = slots(2)
                s_wr = slots(4)
                b_eT, b_eV, b_eS = eT.n, eV.n, eS.n
                ld_v = [s_ld[i % 2].n + 16 * (i // 2 + 1) for i in range(64)]
                wr_v = [[s_wr[2 * (i % 2) + j].n + 16 * (i // 2 + 1) for j in range(2)]
                        for i in range(64)]

                def evac_mark(i, tj):
                    if tj % 2 == 0:
                        return b_eV + 34 * i + 2 + tj // 2 + 1
                    return b_eS + 32 * i + tj // 2 + 1

                @blk.sync
                def _(sy):
                    for i in range(64):
                        if i >= 2:
                            sy.wait_ge(eV.h, b_eV + 34 * (i - 1) + 2)
                        lut, tau = i // 32, i % 32
                        sy.dma_start(pbuf[i % 2][:],
                                     bass.AP(wpk, (lut * 32 + tau) * 128 * D,
                                             [[D, 128], [1, D]])).then_inc(
                            s_ld[i % 2].h, 16)
                        s_ld[i % 2].n += 16

                @blk.vector
                def _(ve):
                    for i in range(64):
                        ve.wait_ge(s_ld[i % 2].h, ld_v[i])
                        if i >= 2:
                            ve.wait_ge(eT.h, b_eT + 64 * (i - 1))
                        src16 = pbuf[i % 2].ap().bitcast(I16)
                        pl16 = pl[i % 2].ap().bitcast(I16)
                        eV.inc(ve.tensor_copy(pl16[:, 0:4096], src16[:, 1::2]))
                        eV.inc(ve.tensor_copy(pl16[:, 4096:8192], src16[:, 0::2]))
                        for tj in range(0, 64, 2):
                            j, k = tj // KT, tj % KT
                            ve.wait_ge(eT.h, b_eT + 64 * i + tj + 1)
                            eV.inc(ve.tensor_copy(
                                abuf[i % 2][:, 4096 * j + 128 * k:4096 * j + 128 * (k + 1)],
                                wpp[tj % 4][:, 0:128]))

                @blk.scalar
                def _(sc):
                    for i in range(64):
                        for tj in range(1, 64, 2):
                            j, k = tj // KT, tj % KT
                            sc.wait_ge(eT.h, b_eT + 64 * i + tj + 1)
                            eS.inc(sc.activation(
                                abuf[i % 2][:, 4096 * j + 128 * k:4096 * j + 128 * (k + 1)],
                                wpp[tj % 4][:, 0:128], AF.Copy))
                        sc.wait_ge(eV.h, b_eV + 34 * (i + 1))
                        lut, tau = i // 32, i % 32
                        for j in range(2):
                            plane = lut * 2 + j
                            sc.dma_start(wf[plane, tau, :, :],
                                         abuf[i % 2][:, 4096 * j:4096 * (j + 1)]).then_inc(
                                s_wr[2 * (i % 2) + j].h, 16)
                            s_wr[2 * (i % 2) + j].n += 16

                @blk.tensor
                def _(te):
                    for i in range(64):
                        ve_unpack = b_eV + 34 * i + 2
                        te.wait_ge(eV.h, ve_unpack)
                        if i >= 2:
                            te.wait_ge(s_wr[2 * (i % 2)].h, wr_v[i - 2][0])
                            te.wait_ge(s_wr[2 * (i % 2) + 1].h, wr_v[i - 2][1])
                        for tj in range(64):
                            j, k = tj // KT, tj % KT
                            if i * 64 + tj >= 4:
                                pi, ptj = divmod(i * 64 + tj - 4, 64)
                                te.wait_ge(eV.h if ptj % 2 == 0 else eS.h,
                                           evac_mark(pi, ptj))
                            eT.inc(te.transpose(wpp[tj % 4][:, 0:128],
                                                pl[i % 2][:, 4096 * j + 128 * k:
                                                            4096 * j + 128 * (k + 1)],
                                                idn16[:]))

            barrier()

        def emit_ffn(xsrc, xdst, tag):
            slot_reset()
            sst = contextlib.ExitStack()
            sen = sst.enter_context
            ps = [sen(nc.psum_tensor(f"f{tag}_p{i}", [128, 512], F32)) for i in range(8)]
            xb = [sen(nc.sbuf_tensor(f"f{tag}_x{i}", [128, 512], F32)) for i in range(4)]
            xsq = [sen(nc.sbuf_tensor(f"f{tag}_q{i}", [128, 512], F32)) for i in range(2)]
            rows = sen(nc.sbuf_tensor(f"f{tag}_rw", [1, 512 * 8], F32))
            lnf = [sen(nc.sbuf_tensor(f"f{tag}_lf{i}", [128, 512], F32)) for i in range(2)]
            lnhi = sen(nc.sbuf_tensor(f"f{tag}_lh", [128, KT * 1024], BF16))
            lnlo = sen(nc.sbuf_tensor(f"f{tag}_ll", [128, KT * 1024], BF16))
            mbuf = [sen(nc.sbuf_tensor(f"f{tag}_mb{i}", [128, 512], I32)) for i in range(2)]
            wbuf = [sen(nc.sbuf_tensor(f"f{tag}_w{i}", [128, 2 * 2048], BF16))
                    for i in range(2)]
            ysel = [sen(nc.sbuf_tensor(f"f{tag}_ys{i}", [128, 512], F32)) for i in range(2)]
            yh = [sen(nc.sbuf_tensor(f"f{tag}_yh{i}", [128, 512], F32)) for i in range(2)]
            xr = [sen(nc.sbuf_tensor(f"f{tag}_xr{i}", [128, 512], F32)) for i in range(4)]
            xo = [sen(nc.sbuf_tensor(f"f{tag}_xo{i}", [128, 512], F32)) for i in range(2)]
            s_x = slots(4)
            s_w = slots(8)
            s_xr = slots(4)
            s_wr = slots(2)

            def ln_pass(h, tc):
                col0 = 1024 * h + 512 * tc
                b_eT, b_eV, b_eS = eT.n, eV.n, eS.n
                xv = [s_x[k % 4].n + 16 * (k // 4 + 1) for k in range(32)]

                with nc.Block() as blk:
                    @blk.sync
                    def _(sy):
                        for k in range(KT):
                            if k >= 4:
                                sy.wait_ge(eT.h, b_eT + 2 * (k - 3))
                            sy.dma_start(xb[k % 4][:],
                                         xsrc[k, :, col0:col0 + 512]).then_inc(
                                s_x[k % 4].h, 16)
                            s_x[k % 4].n += 16

                    @blk.scalar
                    def _(sc):
                        for k in range(KT):
                            sc.wait_ge(s_x[k % 4].h, xv[k])
                            if k >= 2:
                                sc.wait_ge(eT.h, b_eT + 2 * (k - 1))
                            eS.inc(sc.activation(xsq[k % 2][:], xb[k % 4][:], AF.Square))
                        sc.wait_ge(eV.h, b_eV + 5)
                        eS.inc(sc.activation(rows[:, 1536:2048], rows[:, 2560:3072],
                                             AF.Ln))
                        eS.inc(sc.activation(rows[:, 1536:2048], rows[:, 1536:2048],
                                             AF.Exp, scale=-0.5))

                    @blk.tensor
                    def _(te):
                        for k in range(KT):
                            te.wait_ge(s_x[k % 4].h, xv[k])
                            te.wait_ge(eS.h, b_eS + k + 1)
                            eT.inc(te.matmul(ps[4][0:1, :], onesk[:], xb[k % 4][:],
                                             start=(k == 0), stop=(k == KT - 1)))
                            eT.inc(te.matmul(ps[5][0:1, :], onesk[:], xsq[k % 2][:],
                                             start=(k == 0), stop=(k == KT - 1)))
                        te.wait_ge(eS.h, b_eS + 34)
                        te.wait_ge(eV.h, b_eV + 8)
                        eT.inc(te.matmul(ps[4][:], onesm[:], rows[0:1, 0:512],
                                         start=True, stop=True))
                        eT.inc(te.matmul(ps[5][:], onesm[:], rows[0:1, 1536:2048],
                                         start=True, stop=True))
                        eT.inc(te.matmul(ps[6][:], onesm[:], rows[0:1, 3584:4096],
                                         start=True, stop=True))

                    @blk.vector
                    def _(ve):
                        ve.wait_ge(eT.h, b_eT + 64)
                        eV.inc(ve.tensor_scalar(rows[:, 0:512], ps[4][0:1, :],
                                                1.0 / D, None, ALU.mult))
                        eV.inc(ve.tensor_scalar(rows[:, 512:1024], ps[5][0:1, :],
                                                1.0 / D, None, ALU.mult))
                        eV.inc(ve.tensor_tensor(rows[:, 2048:2560], rows[:, 0:512],
                                                rows[:, 0:512], ALU.mult))
                        eV.inc(ve.tensor_tensor(rows[:, 1024:1536], rows[:, 512:1024],
                                                rows[:, 2048:2560], ALU.subtract))
                        eV.inc(ve.tensor_scalar(rows[:, 2560:3072], rows[:, 1024:1536],
                                                EPS, None, ALU.add))
                        eV.inc(ve.reciprocal(rows[:, 3072:3584], rows[:, 2560:3072]))
                        eV.inc(ve.tensor_tensor(rows[:, 2048:2560], rows[:, 1024:1536],
                                                rows[:, 3072:3584], ALU.mult))
                        eV.inc(ve.tensor_scalar(rows[:, 3584:4096], rows[:, 2048:2560],
                                                1.0, None, ALU.is_gt))
                        ve.wait_ge(eT.h, b_eT + 67)
                        eV.inc(ve.tensor_copy(mbuf[tc][:], ps[6][:]))

                b2_eT, b2_eV, b2_eS = eT.n, eV.n, eS.n
                x2v = [s_x[k % 4].n + 16 * (k // 4 + 1) for k in range(32)]

                with nc.Block() as blk:
                    @blk.sync
                    def _(sy):
                        for k in range(KT):
                            if k >= 4:
                                sy.wait_ge(eV.h, b2_eV + 3 * (k - 3))
                            sy.dma_start(xb[k % 4][:],
                                         xsrc[k, :, col0:col0 + 512]).then_inc(
                                s_x[k % 4].h, 16)
                            s_x[k % 4].n += 16

                    @blk.vector
                    def _(ve):
                        for k in range(KT):
                            ve.wait_ge(s_x[k % 4].h, x2v[k])
                            if k >= 2:
                                ve.wait_ge(eS.h, b2_eS + k - 1)
                            eV.inc(ve.tensor_tensor(lnf[k % 2][:], xb[k % 4][:],
                                                    ps[4][:], ALU.subtract))
                            eV.inc(ve.tensor_tensor(lnf[k % 2][:], lnf[k % 2][:],
                                                    ps[5][:], ALU.mult))
                            ve.wait_ge(eS.h, b2_eS + k + 1)
                            col = 1024 * k + 512 * tc
                            eV.inc(ve.tensor_tensor(lnlo[:, col:col + 512], lnf[k % 2][:],
                                                    lnhi[:, col:col + 512], ALU.subtract))

                    @blk.scalar
                    def _(sc):
                        for k in range(KT):
                            sc.wait_ge(eV.h, b2_eV + 3 * k + 2)
                            col = 1024 * k + 512 * tc
                            eS.inc(sc.activation(lnhi[:, col:col + 512], lnf[k % 2][:],
                                                 AF.Copy))

            def sweep(h):
                b_eT, b_eV, b_eS = eT.n, eV.n, eS.n
                w_v = {}
                cnt = [0] * 8
                for tau in range(32):
                    for path in range(2):
                        for kh in range(2):
                            gi = (tau * 2 + path) * 2 + kh
                            for pj in range(2):
                                sl = 4 * (gi % 2) + pj
                                cnt[sl] += 1
                                w_v[(tau, path, kh, pj)] = (sl, s_w[sl].n + 16 * cnt[sl])
                xr_v = [s_xr[i % 4].n + 16 * (i // 4 + 1) for i in range(64)]
                wr_vv = [s_wr[i % 2].n + 16 * (i // 2 + 1) for i in range(64)]

                with nc.Block() as blk:
                    @blk.sync
                    def _(sy):
                        for tau in range(32):
                            for path in range(2):
                                for kh in range(2):
                                    gi = (tau * 2 + path) * 2 + kh
                                    for pj in range(2):
                                        sl, _v = w_v[(tau, path, kh, pj)]
                                        if gi >= 2:
                                            sy.wait_ge(eT.h, b_eT + 96 * (gi - 1))
                                        plane = path * 2 + pj
                                        sy.dma_start(
                                            wbuf[gi % 2][:, 2048 * pj:2048 * (pj + 1)],
                                            bass.AP(wf,
                                                    (plane * 32 + tau) * 128 * D + 2048 * kh,
                                                    [[D, 128], [1, 2048]])).then_inc(
                                            s_w[sl].h, 16)
                                        s_w[sl].n += 16


                    @blk.tensor
                    def _(te):
                        for tau in range(32):
                            for path in range(2):
                                for kh in range(2):
                                    gi = (tau * 2 + path) * 2 + kh
                                    for pj in range(2):
                                        sl, v = w_v[(tau, path, kh, pj)]
                                        te.wait_ge(s_w[sl].h, v)
                                    if tau >= 2 and path == 0 and kh == 0:
                                        te.wait_ge(eV.h,
                                                   b_eV + 4 * (2 * (tau - 2) + 2))
                                    wb = wbuf[gi % 2]
                                    for k16 in range(16):
                                        k = 16 * kh + k16
                                        colh = 1024 * k
                                        whi = wb[:, 128 * k16:128 * (k16 + 1)]
                                        wlo = wb[:, 2048 + 128 * k16:
                                                 2048 + 128 * (k16 + 1)]
                                        for tc in range(2):
                                            col = colh + 512 * tc
                                            bank = ps[4 * (tau % 2) + path * 2 + tc]
                                            first = (k == 0)
                                            eT.inc(te.matmul(
                                                bank[:], whi,
                                                lnhi[:, col:col + 512],
                                                start=first, stop=False))
                                            eT.inc(te.matmul(
                                                bank[:], whi,
                                                lnlo[:, col:col + 512],
                                                start=False, stop=False))
                                            eT.inc(te.matmul(
                                                bank[:], wlo,
                                                lnhi[:, col:col + 512],
                                                start=False,
                                                stop=(k == KT - 1)))

                    @blk.vector
                    def _(ve):
                        for tau in range(32):
                            for tc in range(2):
                                ti = 2 * tau + tc
                                ve.wait_ge(eT.h, b_eT + 384 * (tau + 1))
                                ve.wait_ge(s_xr[ti % 4].h, xr_v[ti])
                                if ti >= 2:
                                    ve.wait_ge(s_wr[ti % 2].h, wr_vv[ti - 2])
                                pbank_l = ps[4 * (tau % 2) + 2 + tc]
                                pbank_h = ps[4 * (tau % 2) + tc]
                                eV.inc(ve.tensor_scalar(ysel[ti % 2][:], pbank_l[:],
                                                        0.5, hbias[:, tau:tau + 1],
                                                        ALU.mult, ALU.add))
                                eV.inc(ve.tensor_scalar(yh[ti % 2][:], pbank_h[:],
                                                        0.5, hbias[:, tau:tau + 1],
                                                        ALU.mult, ALU.add))
                                eV.inc(ve.copy_predicated(ysel[ti % 2][:], mbuf[tc][:],
                                                          yh[ti % 2][:]))
                                eV.inc(ve.tensor_tensor(xo[ti % 2][:], ysel[ti % 2][:],
                                                        xr[ti % 4][:], ALU.add))

                    @blk.gpsimd
                    def _(gp):
                        for tau in range(32):
                            for tc in range(2):
                                ti = 2 * tau + tc
                                col0 = 1024 * h + 512 * tc
                                if ti >= 4:
                                    gp.wait_ge(eV.h, b_eV + 4 * (ti - 3))
                                gp.dma_start(xr[ti % 4][:],
                                             xsrc[tau, :, col0:col0 + 512]).then_inc(
                                    s_xr[ti % 4].h, 16)
                                s_xr[ti % 4].n += 16
                                gp.wait_ge(eV.h, b_eV + 4 * (ti + 1))
                                gp.dma_start(xdst[tau, :, col0:col0 + 512],
                                             xo[ti % 2][:]).then_inc(s_wr[ti % 2].h, 16)
                                s_wr[ti % 2].n += 16

            for h in range(2):
                ln_pass(h, 0)
                barrier()
                ln_pass(h, 1)
                barrier()
                sweep(h)
                barrier()
            sst.close()

        def emit_wsplit():
            slot_reset()
            sst = contextlib.ExitStack()
            sen = sst.enter_context
            wt = [sen(nc.sbuf_tensor(f"ws_t{i}", [128, D], F32)) for i in range(2)]
            whi = [sen(nc.sbuf_tensor(f"ws_h{i}", [128, D], BF16)) for i in range(2)]
            wlo = [sen(nc.sbuf_tensor(f"ws_l{i}", [128, D], BF16)) for i in range(2)]
            s_ld = slots(2)
            s_wr = slots(4)
            jobs = []
            for tau in range(32):
                jobs.append((wq_d, wqs, tau))
            for tau in range(8):
                jobs.append((wk_d, wks, tau))
            for tau in range(8):
                jobs.append((wv_d, wvs, tau))
            for tau in range(32):
                jobs.append((wo_d, wos, tau))
            NJ = len(jobs)
            b_eV, b_eS = eV.n, eS.n
            ld_v = [s_ld[i % 2].n + 16 * (i // 2 + 1) for i in range(NJ)]
            wr_v = [[s_wr[2 * (i % 2) + j].n + 16 * (i // 2 + 1) for j in range(2)]
                    for i in range(NJ)]

            with nc.Block() as blk:
                @blk.sync
                def _(sy):
                    for i, (srcd, dstd, tau) in enumerate(jobs):
                        if i >= 2:
                            sy.wait_ge(eV.h, b_eV + (i - 1))
                        ncol = srcd.shape[1]
                        src = bass.AP(srcd, 128 * tau,
                                      [[ncol, 128], [128 * ncol, 32], [1, 128]])
                        sy.dma_start(wt[i % 2][:], src).then_inc(s_ld[i % 2].h, 16)
                        s_ld[i % 2].n += 16

                @blk.scalar
                def _(sc):
                    for i, (srcd, dstd, tau) in enumerate(jobs):
                        sc.wait_ge(s_ld[i % 2].h, ld_v[i])
                        if i >= 2:
                            sc.wait_ge(s_wr[2 * (i % 2)].h, wr_v[i - 2][0])
                        eS.inc(sc.activation(whi[i % 2][:], wt[i % 2][:], AF.Copy))
                        sc.wait_ge(eV.h, b_eV + i + 1)
                        sc.dma_start(dstd[0, tau, :, :], whi[i % 2][:]).then_inc(
                            s_wr[2 * (i % 2)].h, 16)
                        s_wr[2 * (i % 2)].n += 16
                        sc.dma_start(dstd[1, tau, :, :], wlo[i % 2][:]).then_inc(
                            s_wr[2 * (i % 2) + 1].h, 16)
                        s_wr[2 * (i % 2) + 1].n += 16

                @blk.vector
                def _(ve):
                    for i, (srcd, dstd, tau) in enumerate(jobs):
                        ve.wait_ge(eS.h, b_eS + i + 1)
                        if i >= 2:
                            ve.wait_ge(s_wr[2 * (i % 2) + 1].h, wr_v[i - 2][1])
                        eV.inc(ve.tensor_tensor(wlo[i % 2][:], wt[i % 2][:],
                                                whi[i % 2][:], ALU.subtract))
            sst.close()
            barrier()

        def emit_proj(xsrc):
            slot_reset()
            sst = contextlib.ExitStack()
            sen = sst.enter_context
            ps = [sen(nc.psum_tensor(f"pr_p{i}", [128, 512], F32)) for i in range(8)]
            xb = [sen(nc.sbuf_tensor(f"pr_x{i}", [128, 512], F32)) for i in range(4)]
            xsq = [sen(nc.sbuf_tensor(f"pr_q{i}", [128, 512], F32)) for i in range(2)]
            rows = sen(nc.sbuf_tensor("pr_rw", [1, 512 * 8], F32))
            lnf = [sen(nc.sbuf_tensor(f"pr_lf{i}", [128, 512], F32)) for i in range(2)]
            lnhi = sen(nc.sbuf_tensor("pr_lh", [128, KT * 512], BF16))
            lnlo = sen(nc.sbuf_tensor("pr_ll", [128, KT * 512], BF16))
            wbuf = [sen(nc.sbuf_tensor(f"pr_w{i}", [128, 2 * 2048], BF16))
                    for i in range(2)]
            cosb = sen(nc.sbuf_tensor("pr_cos", [128, 512], F32))
            sinb = sen(nc.sbuf_tensor("pr_sin", [128, 512], F32))
            qraw = [sen(nc.sbuf_tensor(f"pr_qr{i}", [128, 512], F32)) for i in range(2)]
            t1b = [sen(nc.sbuf_tensor(f"pr_t1{i}", [128, 512], F32)) for i in range(2)]
            t2b = [sen(nc.sbuf_tensor(f"pr_t2{i}", [128, 512], F32)) for i in range(2)]
            rpf = [sen(nc.sbuf_tensor(f"pr_rp{i}", [128, 512], F32)) for i in range(2)]
            rphi = [sen(nc.sbuf_tensor(f"pr_rh{i}", [128, 512], BF16)) for i in range(2)]
            rplo = [sen(nc.sbuf_tensor(f"pr_rl{i}", [128, 512], BF16)) for i in range(2)]
            eb = [sen(nc.sbuf_tensor(f"pr_e{i}", [128, 512], F32)) for i in range(2)]
            qsf = [sen(nc.sbuf_tensor(f"pr_qs{i}", [128, 512], F32)) for i in range(2)]
            ub = [sen(nc.sbuf_tensor(f"pr_u{i}", [128, 512], F32)) for i in range(2)]
            vb = [sen(nc.sbuf_tensor(f"pr_v{i}", [128, 512], F32)) for i in range(2)]
            gsf = [sen(nc.sbuf_tensor(f"pr_gs{i}", [128, 512], F32)) for i in range(2)]
            s_x = slots(4)
            s_w = slots(8)
            s_sp = slots(4)

            def ln_pass2(h, tc):
                col0 = 512 * h
                b_eT, b_eV, b_eS = eT.n, eV.n, eS.n
                xv = [s_x[k % 4].n + 16 * (k // 4 + 1) for k in range(32)]

                with nc.Block() as blk:
                    @blk.sync
                    def _(sy):
                        for k in range(KT):
                            if k >= 4:
                                sy.wait_ge(eT.h, b_eT + 2 * (k - 3))
                            sy.dma_start(xb[k % 4][:],
                                         xsrc[k, :, col0:col0 + 512]).then_inc(
                                s_x[k % 4].h, 16)
                            s_x[k % 4].n += 16

                    @blk.scalar
                    def _(sc):
                        for k in range(KT):
                            sc.wait_ge(s_x[k % 4].h, xv[k])
                            if k >= 2:
                                sc.wait_ge(eT.h, b_eT + 2 * (k - 1))
                            eS.inc(sc.activation(xsq[k % 2][:], xb[k % 4][:], AF.Square))
                        sc.wait_ge(eV.h, b_eV + 5)
                        eS.inc(sc.activation(rows[:, 1536:2048], rows[:, 2560:3072],
                                             AF.Ln))
                        eS.inc(sc.activation(rows[:, 1536:2048], rows[:, 1536:2048],
                                             AF.Exp, scale=-0.5))

                    @blk.tensor
                    def _(te):
                        for k in range(KT):
                            te.wait_ge(s_x[k % 4].h, xv[k])
                            te.wait_ge(eS.h, b_eS + k + 1)
                            eT.inc(te.matmul(ps[4][0:1, :], onesk[:], xb[k % 4][:],
                                             start=(k == 0), stop=(k == KT - 1)))
                            eT.inc(te.matmul(ps[5][0:1, :], onesk[:], xsq[k % 2][:],
                                             start=(k == 0), stop=(k == KT - 1)))
                        te.wait_ge(eS.h, b_eS + 34)
                        te.wait_ge(eV.h, b_eV + 5)
                        eT.inc(te.matmul(ps[4][:], onesm[:], rows[0:1, 0:512],
                                         start=True, stop=True))
                        eT.inc(te.matmul(ps[5][:], onesm[:], rows[0:1, 1536:2048],
                                         start=True, stop=True))

                    @blk.vector
                    def _(ve):
                        ve.wait_ge(eT.h, b_eT + 64)
                        eV.inc(ve.tensor_scalar(rows[:, 0:512], ps[4][0:1, :],
                                                1.0 / D, None, ALU.mult))
                        eV.inc(ve.tensor_scalar(rows[:, 512:1024], ps[5][0:1, :],
                                                1.0 / D, None, ALU.mult))
                        eV.inc(ve.tensor_tensor(rows[:, 2048:2560], rows[:, 0:512],
                                                rows[:, 0:512], ALU.mult))
                        eV.inc(ve.tensor_tensor(rows[:, 1024:1536], rows[:, 512:1024],
                                                rows[:, 2048:2560], ALU.subtract))
                        eV.inc(ve.tensor_scalar(rows[:, 2560:3072], rows[:, 1024:1536],
                                                EPS, None, ALU.add))

                b2_eT, b2_eV, b2_eS = eT.n, eV.n, eS.n
                x2v = [s_x[k % 4].n + 16 * (k // 4 + 1) for k in range(32)]

                with nc.Block() as blk:
                    @blk.sync
                    def _(sy):
                        for k in range(KT):
                            if k >= 4:
                                sy.wait_ge(eV.h, b2_eV + 3 * (k - 3))
                            sy.dma_start(xb[k % 4][:],
                                         xsrc[k, :, col0:col0 + 512]).then_inc(
                                s_x[k % 4].h, 16)
                            s_x[k % 4].n += 16

                    @blk.vector
                    def _(ve):
                        for k in range(KT):
                            ve.wait_ge(s_x[k % 4].h, x2v[k])
                            if k >= 2:
                                ve.wait_ge(eS.h, b2_eS + k - 1)
                            eV.inc(ve.tensor_tensor(lnf[k % 2][:], xb[k % 4][:],
                                                    ps[4][:], ALU.subtract))
                            eV.inc(ve.tensor_tensor(lnf[k % 2][:], lnf[k % 2][:],
                                                    ps[5][:], ALU.mult))
                            ve.wait_ge(eS.h, b2_eS + k + 1)
                            col = 512 * k
                            eV.inc(ve.tensor_tensor(lnlo[:, col:col + 512], lnf[k % 2][:],
                                                    lnhi[:, col:col + 512], ALU.subtract))

                    @blk.scalar
                    def _(sc):
                        for k in range(KT):
                            sc.wait_ge(eV.h, b2_eV + 3 * k + 2)
                            col = 512 * k
                            eS.inc(sc.activation(lnhi[:, col:col + 512], lnf[k % 2][:],
                                                 AF.Copy))

            def jtype(tau):
                return "q" if tau < 32 else ("k" if tau < 40 else "v")

            def jsrc(tau):
                if tau < 32:
                    return wqs, tau
                if tau < 40:
                    return wks, tau - 32
                return wvs, tau - 40

            def sweep2(h):
                for tau in range(48):
                    t = jtype(tau)
                    dstd, lt = jsrc(tau)
                    b_eT, b_eV, b_eS = eT.n, eV.n, eS.n
                    wv = []
                    with nc.Block() as blk:
                        @blk.sync
                        def _(sy):
                            for kh in range(2):
                                for pj in range(2):
                                    sl = s_w[2 * kh + pj]
                                    sy.dma_start(
                                        wbuf[kh][:, 2048 * pj:2048 * (pj + 1)],
                                        bass.AP(dstd,
                                                (pj * dstd.shape[1] + lt) * 128 * D
                                                + 2048 * kh,
                                                [[D, 128], [1, 2048]])).then_inc(
                                        sl.h, 16)
                                    sl.n += 16
                                    wv.append((sl, sl.n))

                        @blk.tensor
                        def _(te):
                            for kh in range(2):
                                for sl, v in wv[2 * kh:2 * kh + 2]:
                                    te.wait_ge(sl.h, v)
                                wb = wbuf[kh]
                                for k16 in range(16):
                                    k = 16 * kh + k16
                                    colh = 512 * k
                                    whi_ = wb[:, 128 * k16:128 * (k16 + 1)]
                                    wlo_ = wb[:, 2048 + 128 * k16:2048 + 128 * (k16 + 1)]
                                    for tc in range(1):
                                        col = colh + 512 * tc
                                        bank = ps[tc]
                                        eT.inc(te.matmul(bank[:], whi_,
                                                         lnhi[:, col:col + 512],
                                                         start=(k == 0), stop=False))
                                        eT.inc(te.matmul(bank[:], whi_,
                                                         lnlo[:, col:col + 512],
                                                         start=False, stop=False))
                                        eT.inc(te.matmul(bank[:], wlo_,
                                                         lnhi[:, col:col + 512],
                                                         start=False,
                                                         stop=(k == KT - 1)))
                            if t in ("q", "k"):
                                soff = {"q": 2, "k": 4}[t]
                                for tc in range(1):
                                    te.wait_ge(eS.h, b_eS + soff * tc + 1)
                                    eT.inc(te.matmul(ps[4 + tc][:], rotT[:],
                                                     qraw[tc][:], start=True, stop=True))
                                    te.wait_ge(eS.h, b_eS + soff * tc + 2)
                                    eT.inc(te.matmul(ps[6][0:1, 0:512], onesk[:],
                                                     eb[tc][:], start=True, stop=True))
                                    te.wait_ge(eV.h, b_eV + 7 * tc + 6)
                                    eT.inc(te.matmul(ps[7][:], onesm[:],
                                                     rows[0:1, 3584:4096],
                                                     start=True, stop=True))

                        @blk.scalar
                        def _(sc):
                            for tc in range(1):
                                bank = ps[tc]
                                sc.wait_ge(eT.h, b_eT + 96)
                                if t == "v":
                                    eS.inc(sc.activation(vb[tc][:], bank[:], AF.Copy))
                                    sc.dma_start(
                                        v_s[tau - 40, :, 512 * h:512 * h + 512],
                                        vb[tc][:]).then_inc(s_sp[tc].h, 16)
                                    s_sp[tc].n += 16
                                else:
                                    eS.inc(sc.activation(qraw[tc][:], bank[:], AF.Copy))
                                    eS.inc(sc.activation(eb[tc][:], bank[:], AF.Exp))
                                    if t == "k":
                                        eS.inc(sc.activation(ub[tc][:], bank[:], AF.Exp,
                                                             scale=-1.0))
                                        eS.inc(sc.activation(ub[tc][:], ub[tc][:],
                                                             AF.Ln, bias=1.0))

                        if t != "v":
                            @blk.vector
                            def _(ve):
                                soff = {"q": 2, "k": 4}[t]
                                for tc in range(1):
                                    col0 = 512 * h
                                    ve.wait_ge(eT.h, b_eT + 96 + 3 * tc + 1)
                                    eV.inc(ve.tensor_tensor(t1b[tc][:], qraw[tc][:],
                                                            cosb[:],
                                                            ALU.mult))
                                    eV.inc(ve.tensor_tensor(t2b[tc][:], ps[4 + tc][:],
                                                            sinb[:],
                                                            ALU.mult))
                                    eV.inc(ve.tensor_tensor(rpf[tc][:], t1b[tc][:],
                                                            t2b[tc][:], ALU.add))
                                    eV.inc(ve.tensor_copy(rphi[tc][:], rpf[tc][:]))
                                    eV.inc(ve.tensor_tensor(rplo[tc][:], rpf[tc][:],
                                                            rphi[tc][:], ALU.subtract))
                                    ve.wait_ge(eT.h, b_eT + 96 + 3 * tc + 2)
                                    eV.inc(ve.reciprocal(rows[0:1, 3584:4096],
                                                         ps[6][0:1, 0:512]))
                                    ve.wait_ge(eT.h, b_eT + 96 + 3 * tc + 3)
                                    eV.inc(ve.tensor_tensor(qsf[tc][:], eb[tc][:],
                                                            ps[7][:], ALU.mult))
                                    if t == "k":
                                        ve.wait_ge(eS.h, b_eS + soff * tc + 4)
                                        eV.inc(ve.tensor_scalar(gsf[tc][:], ub[tc][:],
                                                                -1.0 / 16.0, None,
                                                                ALU.mult))

                            @blk.gpsimd
                            def _(gp):
                                voff = {"q": 7, "k": 8}[t]
                                for tc in range(1):
                                    col0 = 512 * h
                                    gp.wait_ge(eV.h, b_eV + voff * tc + 5)
                                    if t == "q":
                                        gp.dma_start(qr_s[0, tau, :, col0:col0 + 512],
                                                     rphi[tc][:]).then_inc(s_sp[0].h, 16)
                                        s_sp[0].n += 16
                                        gp.dma_start(qr_s[1, tau, :, col0:col0 + 512],
                                                     rplo[tc][:]).then_inc(s_sp[1].h, 16)
                                        s_sp[1].n += 16
                                    else:
                                        gp.dma_start(kr_s[0, tau - 32, :, col0:col0 + 512],
                                                     rphi[tc][:]).then_inc(s_sp[0].h, 16)
                                        s_sp[0].n += 16
                                        gp.dma_start(kr_s[1, tau - 32, :, col0:col0 + 512],
                                                     rplo[tc][:]).then_inc(s_sp[1].h, 16)
                                        s_sp[1].n += 16
                                    gp.wait_ge(eV.h, b_eV + voff * tc + 7)
                                    dst = (qs_s[tau, :, col0:col0 + 512] if t == "q"
                                           else ks_s[tau - 32, :, col0:col0 + 512])
                                    gp.dma_start(dst, qsf[tc][:]).then_inc(s_sp[2].h, 16)
                                    s_sp[2].n += 16
                                    if t == "k":
                                        gp.wait_ge(eV.h, b_eV + voff * tc + 8)
                                        gp.dma_start(g_s[tau - 32, :, col0:col0 + 512],
                                                     gsf[tc][:]).then_inc(s_sp[3].h, 16)
                                        s_sp[3].n += 16
                    barrier()

            for h in range(4):
                with nc.Block() as blk:
                    @blk.sync
                    def _(sy):
                        dI.inc(sy.dma_start(
                            cosb[:], ropec_d[:, 512 * h:512 * (h + 1)]), 16)
                        dI.inc(sy.dma_start(
                            sinb[:], ropes_d[:, 512 * h:512 * (h + 1)]), 16)
                barrier()
                ln_pass2(h, 0)
                barrier()
                sweep2(h)
                barrier()
            sst.close()

        if upto != "entry":
            emit_ffn(xA, xB, "1")

        if upto in ("proj", "mixer", "attn", "full"):
            emit_wsplit()
            emit_proj(xB)

        def emit_mixer():
            slot_reset()
            sst = contextlib.ExitStack()
            sen = sst.enter_context
            qsb = [sen(nc.sbuf_tensor(f"mx_qs{i}", [128, 4096], F32)) for i in range(2)]
            qrh = [sen(nc.sbuf_tensor(f"mx_qh{i}", [128, 4096], BF16)) for i in range(2)]
            qrl = [sen(nc.sbuf_tensor(f"mx_ql{i}", [128, 4096], BF16)) for i in range(2)]
            ksb = [sen(nc.sbuf_tensor(f"mx_ks{i}", [128, 1024], F32)) for i in range(2)]
            krh = [sen(nc.sbuf_tensor(f"mx_kh{i}", [128, 2048], BF16)) for i in range(2)]
            krl = [sen(nc.sbuf_tensor(f"mx_kl{i}", [128, 2048], BF16)) for i in range(2)]
            vsb = [sen(nc.sbuf_tensor(f"mx_vs{i}", [128, 2048], F32)) for i in range(2)]
            gsb = [sen(nc.sbuf_tensor(f"mx_gs{i}", [128, 1024], F32)) for i in range(2)]
            vteh = sen(nc.sbuf_tensor("mx_vteh", [128, 128], BF16))
            vtel = sen(nc.sbuf_tensor("mx_vtel", [128, 128], BF16))
            vtoh = sen(nc.sbuf_tensor("mx_vtoh", [128, 128], BF16))
            vtol = sen(nc.sbuf_tensor("mx_vtol", [128, 128], BF16))
            Sb = [sen(nc.sbuf_tensor(f"mx_S{i}", [128, 1024], F32)) for i in range(2)]
            Ab = sen(nc.sbuf_tensor("mx_A", [128, 128], F32))
            Ar = sen(nc.sbuf_tensor("mx_Ar", [128, 128], F32))
            Pb = sen(nc.sbuf_tensor("mx_P", [128, 128], F32))
            Pi = sen(nc.sbuf_tensor("mx_Pi", [128, 128], F32))
            ktb = sen(nc.sbuf_tensor("mx_kt", [128, 128], F32))
            kth = sen(nc.sbuf_tensor("mx_kth", [128, 128], BF16))
            ktl = sen(nc.sbuf_tensor("mx_ktl", [128, 128], BF16))
            kTh = sen(nc.sbuf_tensor("mx_kTh", [128, 128], BF16))
            kTl = sen(nc.sbuf_tensor("mx_kTl", [128, 128], BF16))
            s0s = sen(nc.sbuf_tensor("mx_s0s", [128, 128], F32))
            s0h = sen(nc.sbuf_tensor("mx_s0h", [128, 128], BF16))
            s0l = sen(nc.sbuf_tensor("mx_s0l", [128, 128], BF16))
            stmp = sen(nc.sbuf_tensor("mx_stmp", [128, 128], F32))
            r2 = sen(nc.sbuf_tensor("mx_r2", [128, 8], F32))
            qtb = sen(nc.sbuf_tensor("mx_qt", [128, 128], F32))
            qth = sen(nc.sbuf_tensor("mx_qth", [128, 128], BF16))
            qtl = sen(nc.sbuf_tensor("mx_qtl", [128, 128], BF16))
            smb = sen(nc.sbuf_tensor("mx_sm", [128, 128], F32))
            smh = sen(nc.sbuf_tensor("mx_smh", [128, 128], BF16))
            sml = sen(nc.sbuf_tensor("mx_sml", [128, 128], BF16))
            swb = sen(nc.sbuf_tensor("mx_sw", [W, 128], F32))
            swp = sen(nc.sbuf_tensor("mx_swp", [W, 128], F32))
            swsum = sen(nc.sbuf_tensor("mx_sws", [W, 1], F32))
            swr = sen(nc.sbuf_tensor("mx_swr", [W, 1], F32))
            swn = sen(nc.sbuf_tensor("mx_swn", [W, 128], F32))
            pth = sen(nc.sbuf_tensor("mx_pth", [128, W], BF16))
            ptl = sen(nc.sbuf_tensor("mx_ptl", [128, W], BF16))
            olh = sen(nc.sbuf_tensor("mx_olh", [128, 128], F32))
            ob = [sen(nc.sbuf_tensor(f"mx_ob{i}", [128, 128], BF16)) for i in range(2)]
            pTrA = sen(nc.psum_tensor("mx_pA", [128, 512], F32))
            pTrB = sen(nc.psum_tensor("mx_pB", [128, 512], F32))
            pU = sen(nc.psum_tensor("mx_pU", [128, 512], F32))
            pSCR = sen(nc.psum_tensor("mx_pS", [128, 512], F32))
            pO = sen(nc.psum_tensor("mx_pO", [128, 512], F32))
            pW = sen(nc.psum_tensor("mx_pW", [128, 512], F32))
            pY0 = sen(nc.psum_tensor("mx_pY0", [128, 512], F32))
            pY1 = sen(nc.psum_tensor("mx_pY1", [128, 512], F32))
            s_ld = slots(8)
            s_ob = slots(2)

            with nc.Block() as blk:
                @blk.vector
                def _(ve):
                    eV.inc(ve.memset(Sb[0][:], 0.0))
                    eV.inc(ve.memset(vsb[0][:], 0.0))
                    eV.inc(ve.memset(krh[0][:], 0.0))
                    eV.inc(ve.memset(krl[0][:], 0.0))
            barrier()

            for c in range(16):
                pc = c % 2
                with nc.Block() as blk:
                    @blk.sync
                    def _(sy):
                        def ld(i, dst, srcd, nh, plane, c0, cw):
                            if plane is None:
                                srcap = bass.AP(srcd, c0, [[T, 128], [128 * T, nh],
                                                           [1, cw]])
                            else:
                                srcap = bass.AP(srcd, plane * nh * 128 * T + c0,
                                                [[T, 128], [128 * T, nh], [1, cw]])
                            sy.dma_start(dst, srcap).then_inc(s_ld[i].h, 16)
                            s_ld[i].n += 16
                        ld(0, qsb[pc][:], qs_s, NH, None, 128 * c, 128)
                        ld(1, qrh[pc][:], qr_s, NH, 0, 128 * c, 128)
                        ld(2, qrl[pc][:], qr_s, NH, 1, 128 * c, 128)
                        ld(3, ksb[pc][:], ks_s, NKV, None, 128 * c, 128)
                        if c == 0:
                            kdst = bass.AP(krh[pc], 128,
                                           [[2048, 128], [256, 8], [1, 128]])
                            ld(4, kdst, kr_s, NKV, 0, 0, 128)
                            kdst = bass.AP(krl[pc], 128,
                                           [[2048, 128], [256, 8], [1, 128]])
                            ld(5, kdst, kr_s, NKV, 1, 0, 128)
                        else:
                            kdst = bass.AP(krh[pc], 0,
                                           [[2048, 128], [256, 8], [1, 256]])
                            ld(4, kdst, kr_s, NKV, 0, 128 * (c - 1), 256)
                            kdst = bass.AP(krl[pc], 0,
                                           [[2048, 128], [256, 8], [1, 256]])
                            ld(5, kdst, kr_s, NKV, 1, 128 * (c - 1), 256)
                        if c == 0:
                            vdst = bass.AP(vsb[pc], 128,
                                           [[2048, 128], [256, 8], [1, 128]])
                            ld(6, vdst, v_s, NKV, None, 0, 128)
                        else:
                            vdst = bass.AP(vsb[pc], 0,
                                           [[2048, 128], [256, 8], [1, 256]])
                            ld(6, vdst, v_s, NKV, None, 128 * (c - 1), 256)
                        ld(7, gsb[pc][:], g_s, NKV, None, 128 * c, 128)
                barrier()

                for j in range(NKV):
                    Ssrc, Sdst = Sb[c % 2], Sb[(c + 1) % 2]
                    bT, bV, bS = eT.n, eV.n, eS.n
                    obv = [s_ob[i].n for i in range(2)]
                    with nc.Block() as blk:
                        @blk.tensor
                        def _(te):
                            eT.inc(te.transpose(
                                pTrA[:, 0:128],
                                vsb[pc][:, 256 * j + 64:256 * j + 192], idn32[:]))
                            eT.inc(te.transpose(
                                pTrB[:, 128:256],
                                vsb[pc][:, 256 * j + 128:256 * j + 256], idn32[:]))
                            te.wait_ge(eV.h, bV + 4)
                            eT.inc(te.transpose(pSCR[:, 128:256], ktb[:], idn32[:]))
                            te.wait_ge(eV.h, bV + 12)
                            eT.inc(te.matmul(pU[:, 0:128], kTh[:], vtoh[:],
                                             start=True, stop=False))
                            eT.inc(te.matmul(pU[:, 0:128], kTh[:], vtol[:],
                                             start=False, stop=False))
                            eT.inc(te.matmul(pU[:, 0:128], kTl[:], vtoh[:],
                                             start=False, stop=True))
                            for il in range(4):
                                qb = bV + 17 + 18 * il
                                sb = bS + 5 + 7 * il
                                tb = bT + 6 + 23 * il
                                hh = 4 * j + il
                                te.wait_ge(eV.h, qb + 3)
                                eT.inc(te.matmul(pSCR[:, 0:128], kth[:],
                                                 qth[:], start=True, stop=False))
                                eT.inc(te.matmul(pSCR[:, 0:128], kth[:],
                                                 qtl[:], start=False, stop=False))
                                eT.inc(te.matmul(pSCR[:, 0:128], ktl[:],
                                                 qth[:], start=False, stop=True))
                                te.wait_ge(eV.h, qb + 6)
                                te.wait_ge(eS.h, sb)
                                eT.inc(te.matmul(pO[:, 0:128], vtoh[:], smh[:],
                                                 start=True, stop=False))
                                eT.inc(te.matmul(pO[:, 0:128], vtoh[:], sml[:],
                                                 start=False, stop=False))
                                eT.inc(te.matmul(pO[:, 0:128], vtol[:], smh[:],
                                                 start=False, stop=False))
                                eT.inc(te.matmul(pO[:, 0:128], s0h[:], qth[:],
                                                 start=False, stop=False))
                                eT.inc(te.matmul(pO[:, 0:128], s0h[:], qtl[:],
                                                 start=False, stop=False))
                                eT.inc(te.matmul(pO[:, 0:128], s0l[:], qth[:],
                                                 start=False, stop=True))
                                for b in range(2):
                                    qcol = 128 * hh + 64 * b
                                    kwin = (64 + 64 * b, 64 + 64 * b + 128)
                                    kwh = krh[pc][:, 256 * j + kwin[0]:
                                                  256 * j + kwin[1]]
                                    kwl = krl[pc][:, 256 * j + kwin[0]:
                                                  256 * j + kwin[1]]
                                    qbh = qrh[pc][:, qcol:qcol + 64]
                                    qbl = qrl[pc][:, qcol:qcol + 64]
                                    if b == 1:
                                        te.wait_ge(eV.h, qb + 7)
                                    eT.inc(te.matmul(pW[0:64, 0:128], qbh, kwh,
                                                     start=True, stop=False))
                                    eT.inc(te.matmul(pW[0:64, 0:128], qbh, kwl,
                                                     start=False, stop=False))
                                    eT.inc(te.matmul(pW[0:64, 0:128], qbl, kwh,
                                                     start=False, stop=True))
                                    te.wait_ge(eV.h, qb + 9 + 5 * b)
                                    pT = pTrA if b == 0 else pTrB
                                    eT.inc(te.transpose(pT[:, 128:192], swn[:],
                                                        idn32.ap()[0:64, 0:64]))
                                    te.wait_ge(eV.h, qb + 11 + 5 * b)
                                    pY = pY0 if b == 0 else pY1
                                    vh_ = vteh if b == 0 else vtoh
                                    vl_ = vtel if b == 0 else vtol
                                    eT.inc(te.matmul(pY[:, 0:64], vh_[:], pth[:],
                                                     start=True, stop=False))
                                    eT.inc(te.matmul(pY[:, 0:64], vh_[:], ptl[:],
                                                     start=False, stop=False))
                                    eT.inc(te.matmul(pY[:, 0:64], vl_[:], pth[:],
                                                     start=False, stop=True))

                        @blk.vector
                        def _(ve):
                            gsj = gsb[pc][:, 128 * j:128 * (j + 1)]
                            eV.inc(ve.tensor_tensor_scan(Ab[:], gsj, gsj, 0.0,
                                                         ALU.add, ALU.bypass))
                            eV.inc(ve.tensor_scalar(Ar[:], Ab[:], Ab[:, 63:64],
                                                    None, ALU.subtract))
                            eV.inc(ve.tensor_tensor(r2[:, 0:1], Ab[:, 127:128],
                                                    Ab[:, 63:64], ALU.subtract))
                            ve.wait_ge(eS.h, bS + 2)
                            eV.inc(ve.tensor_tensor(ktb[:],
                                                    ksb[pc][:, 128 * j:128 * (j + 1)],
                                                    Pi[:], ALU.mult))
                            eV.inc(ve.tensor_copy(kth[:], ktb[:]))
                            eV.inc(ve.tensor_tensor(ktl[:], ktb[:], kth[:],
                                                    ALU.subtract))
                            ve.wait_ge(eT.h, bT + 3)
                            eV.inc(ve.tensor_copy(kTh[:], pSCR[:, 128:256]))
                            eV.inc(ve.tensor_tensor(kTl[:], pSCR[:, 128:256],
                                                    kTh[:], ALU.subtract))
                            eV.inc(ve.tensor_copy(vteh[:], pTrA[:, 0:128]))
                            eV.inc(ve.tensor_tensor(vtel[:], pTrA[:, 0:128],
                                                    vteh[:], ALU.subtract))
                            eV.inc(ve.tensor_copy(vtoh[:], pTrB[:, 128:256]))
                            eV.inc(ve.tensor_tensor(vtol[:], pTrB[:, 128:256],
                                                    vtoh[:], ALU.subtract))
                            ve.wait_ge(eS.h, bS + 3)
                            Ssrcj = Ssrc[:, 128 * j:128 * (j + 1)]
                            eV.inc(ve.tensor_scalar(s0s[:], Ssrcj, r2[:, 2:3],
                                                    None, ALU.mult))
                            eV.inc(ve.tensor_copy(s0h[:], s0s[:]))
                            eV.inc(ve.tensor_tensor(s0l[:], s0s[:], s0h[:],
                                                    ALU.subtract))
                            ve.wait_ge(eS.h, bS + 4)
                            eV.inc(ve.tensor_scalar(stmp[:], Ssrcj, r2[:, 3:4],
                                                    None, ALU.mult))
                            ve.wait_ge(eT.h, bT + 6)
                            ve.wait_ge(eS.h, bS + 5)
                            eV.inc(ve.scalar_tensor_tensor(
                                Sdst[:, 128 * j:128 * (j + 1)], pU[:, 0:128],
                                r2[:, 4:5], stmp[:], ALU.mult, ALU.add))
                            for il in range(4):
                                hh = 4 * j + il
                                qb = bV + 17 + 18 * il
                                sb = bS + 5 + 7 * il
                                tb = bT + 6 + 23 * il
                                eV.inc(ve.tensor_tensor(
                                    qtb[:], qsb[pc][:, 128 * hh:128 * (hh + 1)],
                                    Pb[:], ALU.mult))
                                eV.inc(ve.tensor_copy(qth[:], qtb[:]))
                                eV.inc(ve.tensor_tensor(qtl[:], qtb[:], qth[:],
                                                        ALU.subtract))
                                ve.wait_ge(eT.h, tb + 3)
                                eV.inc(ve.tensor_tensor(smb[:], pSCR[:, 0:128],
                                                        ltm[:], ALU.mult))
                                eV.inc(ve.tensor_copy(smh[:], smb[:]))
                                eV.inc(ve.tensor_tensor(sml[:], smb[:], smh[:],
                                                        ALU.subtract))
                                for b in range(2):
                                    mask = swam0 if (c == 0 and b == 0) else swam
                                    ve.wait_ge(eT.h, tb + 12 + 7 * b)
                                    eV.inc(ve.tensor_tensor(swb[:], pW[0:64, 0:128],
                                                            mask[:], ALU.add))
                                    ve.wait_ge(eS.h, sb + 1 + 3 * b)
                                    eV.inc(ve.reduce_sum(swsum[:], swp[:],
                                                         axis=mybir.AxisListType.X))
                                    ve.wait_ge(eS.h, sb + 3 + 3 * b)
                                    eV.inc(ve.tensor_scalar(swn[:], swp[:],
                                                            swr[:, 0:1], None,
                                                            ALU.mult))
                                    ve.wait_ge(eT.h, tb + 13 + 7 * b)
                                    pT = pTrA if b == 0 else pTrB
                                    eV.inc(ve.tensor_copy(pth[:], pT[:, 128:192]))
                                    eV.inc(ve.tensor_tensor(ptl[:], pT[:, 128:192],
                                                            pth[:], ALU.subtract))
                                if il >= 2:
                                    ve.wait_ge(s_ob[il % 2].h, obv[il % 2] + 16)
                                ve.wait_ge(eT.h, tb + 16)
                                ve.wait_ge(eS.h, sb + 3)
                                eV.inc(ve.scalar_tensor_tensor(
                                    ob[il % 2][:, 0:64], pY0[:, 0:64], 0.5,
                                    olh[:, 0:64], ALU.mult, ALU.add))
                                ve.wait_ge(eT.h, tb + 23)
                                eV.inc(ve.scalar_tensor_tensor(
                                    ob[il % 2][:, 64:128], pY1[:, 0:64], 0.5,
                                    olh[:, 64:128], ALU.mult, ALU.add))

                        @blk.scalar
                        def _(sc):
                            sc.wait_ge(eV.h, bV + 2)
                            eS.inc(sc.activation(Pb[:], Ar[:], AF.Exp))
                            eS.inc(sc.activation(Pi[:], Ar[:], AF.Exp, scale=-1.0))
                            eS.inc(sc.activation(r2[:, 2:3], Ab[:, 63:64], AF.Exp))
                            eS.inc(sc.activation(r2[:, 3:4], Ab[:, 127:128], AF.Exp))
                            sc.wait_ge(eV.h, bV + 3)
                            eS.inc(sc.activation(r2[:, 4:5], r2[:, 0:1], AF.Exp))
                            for il in range(4):
                                qb = bV + 17 + 18 * il
                                tb = bT + 6 + 23 * il
                                for b in range(2):
                                    sc.wait_ge(eV.h, qb + 7 + 5 * b)
                                    eS.inc(sc.activation(swp[:], swb[:], AF.Exp,
                                                         scale=SQDI))
                                    sc.wait_ge(eV.h, qb + 8 + 5 * b)
                                    eS.inc(sc.activation(swr[:], swsum[:], AF.Ln))
                                    eS.inc(sc.activation(swr[:], swr[:], AF.Exp,
                                                         scale=-1.0))
                                sc.wait_ge(eT.h, tb + 9)
                                eS.inc(sc.activation(olh[:], pO[:, 0:128], AF.Copy,
                                                     scale=0.5))

                        @blk.gpsimd
                        def _(gp):
                            for il in range(4):
                                hh = 4 * j + il
                                qb = bV + 17 + 18 * il
                                if dbg and c == 1 and j == 0 and il == 3:
                                    gp.wait_ge(eV.h, qb + 15)
                                    gp.dma_start(dbg2_d[0, 0:64, 0:1], swr[:],
                                                 allow_slow_non_contiguous=True
                                                 ).then_inc(dI.h, 16)
                                    dI.n += 16
                                    gp.dma_start(dbg2_d[1, 0:64, 0:1], swsum[:],
                                                 allow_slow_non_contiguous=True
                                                 ).then_inc(dI.h, 16)
                                    dI.n += 16
                                    gp.wait_ge(eV.h, qb + 16)
                                    gp.dma_start(dbg2_d[2, 0:64, :],
                                                 swn[:]).then_inc(dI.h, 16)
                                    dI.n += 16
                                gp.wait_ge(eV.h, qb + 18)
                                gp.dma_start(o_s[hh, :, 128 * c:128 * (c + 1)],
                                             ob[il % 2][:]).then_inc(
                                    s_ob[il % 2].h, 16)
                                s_ob[il % 2].n += 16
                    barrier()
                    if dbg and c == 1 and j == 0:
                        with nc.allow_non_contiguous_dma(reason="dbg"), \
                                nc.Block() as blk:
                            @blk.sync
                            def _(sy):
                                for i_, t_ in enumerate((Ab, Pb, ktb, smb, s0s,
                                                         swb, swn, olh, qtb,
                                                         swp, swsum, swr)):
                                    dI.inc(sy.dma_start(
                                        dbg_d[i_, 0:t_.shape[0], 0:t_.shape[1]],
                                        t_[:]), 16)
                                dI.inc(sy.dma_start(
                                    dbg_d[12, :, :],
                                    Sb[(c + 1) % 2][:, 0:128]), 16)
                        barrier()
            sst.close()

        def emit_wo():
            slot_reset()
            sst = contextlib.ExitStack()
            sen = sst.enter_context
            ps2 = [sen(nc.psum_tensor(f"wo_p{i}", [128, 512], F32)) for i in range(2)]
            obuf = sen(nc.sbuf_tensor("wo_ob", [128, KT * 512], BF16))
            wbuf = [sen(nc.sbuf_tensor(f"wo_w{i}", [128, 2 * 2048], BF16))
                    for i in range(2)]
            x1b = [sen(nc.sbuf_tensor(f"wo_x{i}", [128, 512], F32)) for i in range(2)]
            xob = [sen(nc.sbuf_tensor(f"wo_o{i}", [128, 512], F32)) for i in range(2)]
            s_o = slots(1)
            s_w = slots(4)
            s_x1 = slots(2)
            s_wr = slots(2)
            for tc in range(4):
                bload = s_o.n if False else None
                with nc.Block() as blk:
                    @blk.sync
                    def _(sy):
                        src = bass.AP(o_s, 512 * tc,
                                      [[T, 128], [128 * T, KT], [1, 512]])
                        sy.dma_start(obuf[:], src).then_inc(s_o[0].h, 16)
                        s_o[0].n += 16
                barrier()
                for tau in range(32):
                    bT, bV = eT.n, eV.n
                    wv = []
                    x1v = []
                    with nc.Block() as blk:
                        @blk.sync
                        def _(sy):
                            for kh in range(2):
                                for pj in range(2):
                                    sl = s_w[2 * kh + pj]
                                    sy.dma_start(
                                        wbuf[kh][:, 2048 * pj:2048 * (pj + 1)],
                                        bass.AP(wos,
                                                (pj * 32 + tau) * 128 * D + 2048 * kh,
                                                [[D, 128], [1, 2048]])).then_inc(
                                        sl.h, 16)
                                    sl.n += 16
                                    wv.append((sl, sl.n))
                            sy.dma_start(x1b[0][:],
                                         xB[tau, :, 512 * tc:512 * (tc + 1)]
                                         ).then_inc(s_x1[0].h, 16)
                            s_x1[0].n += 16
                            x1v.append(s_x1[0].n)

                        @blk.tensor
                        def _(te):
                            for kh in range(2):
                                for sl, v in wv[2 * kh:2 * kh + 2]:
                                    te.wait_ge(sl.h, v)
                                wb = wbuf[kh]
                                for k16 in range(16):
                                    k = 16 * kh + k16
                                    whi_ = wb[:, 128 * k16:128 * (k16 + 1)]
                                    wlo_ = wb[:, 2048 + 128 * k16:
                                              2048 + 128 * (k16 + 1)]
                                    rhs = obuf[:, 512 * k:512 * (k + 1)]
                                    eT.inc(te.matmul(ps2[0][:], whi_, rhs,
                                                     start=(k == 0), stop=False))
                                    eT.inc(te.matmul(ps2[0][:], wlo_, rhs,
                                                     start=False,
                                                     stop=(k == KT - 1)))

                        @blk.vector
                        def _(ve):
                            ve.wait_ge(eT.h, bT + 64)
                            ve.wait_ge(s_x1[0].h, x1v[0])
                            eV.inc(ve.tensor_tensor(xob[0][:], ps2[0][:],
                                                    x1b[0][:], ALU.add))

                        @blk.scalar
                        def _(sc):
                            sc.wait_ge(eV.h, bV + 1)
                            sc.dma_start(xC[tau, :, 512 * tc:512 * (tc + 1)],
                                         xob[0][:]).then_inc(s_wr[0].h, 16)
                            s_wr[0].n += 16
                    barrier()
            sst.close()

        if upto in ("mixer", "attn", "full"):
            emit_mixer()
        if upto in ("attn", "full"):
            emit_wo()

        x_final = {"entry": xA, "ffn1": xB, "proj": xB, "mixer": xB,
                   "attn": xC, "full": xA}[upto]
        if upto == "full":
            emit_ffn(xC, xA, "2")

        dO = Sem(nc, "s_dO")

        def emit_exit(src_dram, dst_dram, tag):
            slot_reset()
            sst = contextlib.ExitStack()
            sen = sst.enter_context
            tb = [sen(nc.sbuf_tensor(f"ex_{tag}_t{i}", [128, 128], F32)) for i in range(4)]
            ob = [sen(nc.sbuf_tensor(f"ex_{tag}_o{i}", [128, D], F32)) for i in range(2)]
            xpp = [sen(nc.psum_tensor(f"ex_{tag}_p{i}", [128, 512], F32)) for i in range(4)]
            s_t = slots(4)
            b_eT, b_eV, b_dO = eT.n, eV.n, dO.n
            tv = [s_t[i % 4].n + 16 * (i // 4 + 1) for i in range(512)]

            with nc.Block() as blk:
                @blk.sync
                def _(sy):
                    for tt in range(16):
                        for k in range(KT):
                            idx = tt * KT + k
                            if idx >= 4:
                                sy.wait_ge(eT.h, b_eT + idx - 3)
                            sy.dma_start(tb[idx % 4][:],
                                         src_dram[k, :, 128 * tt:128 * (tt + 1)]
                                         ).then_inc(s_t[idx % 4].h, 16)
                            s_t[idx % 4].n += 16

                @blk.tensor
                def _(te):
                    for tt in range(16):
                        for k in range(KT):
                            idx = tt * KT + k
                            te.wait_ge(s_t[idx % 4].h, tv[idx])
                            if idx >= 4:
                                te.wait_ge(eV.h, b_eV + idx - 3)
                            eT.inc(te.transpose(xpp[idx % 4][:, 0:128], tb[idx % 4][:],
                                                idn32[:]))

                @blk.vector
                def _(ve):
                    for tt in range(16):
                        if tt >= 2:
                            ve.wait_ge(dO.h, b_dO + 16 * (tt - 1))
                        for k in range(KT):
                            idx = tt * KT + k
                            ve.wait_ge(eT.h, b_eT + idx + 1)
                            eV.inc(ve.tensor_copy(ob[tt % 2][:, 128 * k:128 * (k + 1)],
                                                  xpp[idx % 4][:, 0:128]))

                @blk.scalar
                def _(sc):
                    for tt in range(16):
                        sc.wait_ge(eV.h, b_eV + 32 * (tt + 1))
                        dO.inc(sc.dma_start(dst_dram[128 * tt:128 * (tt + 1), :],
                                            ob[tt % 2][:]), 16)
            sst.close()

        for name in taps:
            src = {"x0": xA, "x1": xB, "x2": xC}[name]
            barrier()
            emit_exit(src, tap_d[name], name)

        barrier()
        emit_exit(x_final, out_d, "out")

        with nc.Block() as blk:
            @blk.sync
            def _(sy):
                sy.wait_ge(dO.h, dO.n)

        nc.compile()
    return nc


_consts = make_consts()


def _in_map(core_inputs):
    m = dict(core_inputs)
    m.update({"idn32c": _consts["idn32"], "idn16c": _consts["idn16"],
              "ones_kc": _consts["ones_k"], "ones_mc": _consts["ones_m"],
              "ropecc": _consts["ropec"], "ropesc": _consts["ropes"],
              "rotTc": _consts["rotT"], "ltmaskc": _consts["ltmask"],
              "swamc": _consts["swam"], "swam0c": _consts["swam0"],
              "sh16c": _consts["sh16"]})
    return m


def kernel(L0_attn_hidden_states, L0_attn_Wq, L0_attn_Wk, L0_attn_Wv, L0_attn_Wo,
           L0_ffn_qweight, L0_ffn_lut4, L0_ffn_lut8, L0_ffn_bias):
    x = np.ascontiguousarray(np.asarray(L0_attn_hidden_states, dtype=np.float32))
    biasr = np.ascontiguousarray(np.asarray(L0_ffn_bias, dtype=np.float32)
                                 .reshape(32, 128).T)
    shared = {"wq": np.ascontiguousarray(L0_attn_Wq),
              "wk": np.ascontiguousarray(L0_attn_Wk),
              "wv": np.ascontiguousarray(L0_attn_Wv),
              "wo": np.ascontiguousarray(L0_attn_Wo),
              "qw": np.ascontiguousarray(L0_ffn_qweight),
              "lut4": np.ascontiguousarray(L0_ffn_lut4),
              "lut8": np.ascontiguousarray(L0_ffn_lut8), "biasr": biasr}
    nc = build()
    in_maps = [_in_map({"x_in": np.ascontiguousarray(x[c]), **shared})
               for c in range(8)]
    res = run_bass_kernel_spmd(nc, in_maps, core_ids=list(range(8)))
    return np.stack([r["out"] for r in res.results], axis=0)


# version 04
# speedup vs baseline: 2.5209x; 2.5209x over previous
import contextlib
import numpy as np
import ml_dtypes
import concourse.bass as bass
import concourse.bacc as bacc
import concourse.mybir as mybir
from concourse.bass_utils import run_bass_kernel_spmd

import concourse.dve_ops as _dvo
from concourse.dve_op import DveOp as _DveOp
from concourse.dve_spec import (Spec as _Spec, select as _dsel, eq as _deq,
                                Src0 as _S0, Src1 as _S1, C0 as _C0, C1 as _C1,
                                C2 as _C2, One as _DOne, Zero as _DZero,
                                lower as _dlower)
from concourse.dve_table_gen import dve_ver_for as _dve_ver_for

F32 = mybir.dt.float32
BF16 = mybir.dt.bfloat16
I16 = mybir.dt.int16
I32 = mybir.dt.int32
AF = mybir.ActivationFunctionType
ALU = mybir.AluOpType


def _reg_dve_op(name, body, ref, row):
    if name in _dvo._SUB_OPCODE_FOR_NAME:
        return next(o for o in _dvo.OPS if o.name == name)
    spec = _Spec(body=body)
    ver = _dve_ver_for("TRN2")
    sha = _dlower(spec, ver=ver, perf_en=False, name=name).sha(ver)
    op = _DveOp(name, spec, subdim=False, uops_sha={ver: sha}, reference=ref)
    _dvo.OPS.append(op)
    _dvo.CUSTOM_DVE_SPECS[name] = spec
    _dvo.CUSTOM_DVE_REFS[name] = ref
    _dvo._SUB_OPCODE_FOR_NAME[name] = row
    assert row < 0x20
    return op


def _lut_sel2_ref(in0, in1, s0, s1, imm2):
    f = np.float64 if np.asarray(in0).dtype == np.float64 else np.float32
    return np.where(in0 == imm2, s0, np.where(in0 == imm2 + 1.0, s1, in1)).astype(f)


def _lut_seed2_ref(in0, in1, s0, s1, imm2):
    f = np.float64 if np.asarray(in0).dtype == np.float64 else np.float32
    return np.where(in0 == imm2, s0, np.where(in0 == imm2 + 1.0, s1, 0.0)).astype(f)


LUT_SEL2 = _reg_dve_op(
    "LUT_SEL2",
    _dsel(_deq(_S0, _C2), _C0, _dsel(_deq(_S0 - _C2, _DOne), _C1, _S1)),
    _lut_sel2_ref, 30)
LUT_SEED2 = _reg_dve_op(
    "LUT_SEED2",
    _dsel(_deq(_S0, _C2), _C0, _dsel(_deq(_S0 - _C2, _DOne), _C1, _DZero)),
    _lut_seed2_ref, 31)

T = 2048
D = 4096
KT = 32
NH = 32
NKV = 8
W = 64
EPS = 1e-5
NEG = -1e30
SQDI = float(1.0 / np.sqrt(128.0))


class Sem:
    def __init__(self, nc, name):
        self.h = nc.alloc_semaphore(name)
        self.n = 0

    def inc(self, inst, k=1):
        inst.then_inc(self.h, k)
        self.n += k
        return self.n


def make_consts():
    c = {}
    c["idn32"] = np.eye(128, dtype=np.float32)
    c["idn16"] = np.eye(128, dtype=ml_dtypes.bfloat16)
    c["ones_k"] = np.ones((128, 1), dtype=np.float32)
    c["ones_m"] = np.ones((1, 128), dtype=np.float32)
    half = np.arange(64, dtype=np.float64)
    inv = 1.0 / (10000.0 ** (2.0 * half / 128.0))
    pos = np.arange(T, dtype=np.float64)
    fr = pos[None, :] * inv[:, None]
    emb = np.concatenate([fr, fr], axis=0)
    c["ropec"] = np.cos(emb).astype(np.float32)
    c["ropes"] = np.sin(emb).astype(np.float32)
    R = np.zeros((128, 128), dtype=np.float32)
    for m in range(64):
        R[m + 64, m] = -1.0
        R[m, m + 64] = 1.0
    c["rotT"] = R
    c["ltmask"] = (np.arange(128)[:, None] <= np.arange(128)[None, :]).astype(np.float32)
    a = np.arange(W)[:, None]
    cc = np.arange(2 * W)[None, :]
    rel = a + W - cc
    base = (rel >= 0) & (rel <= W)
    mb = np.where(base, 0.0, NEG).astype(np.float32)
    c["swam"] = mb
    mb0 = mb.copy()
    mb0[:, :W] = NEG
    c["swam0"] = mb0
    c["sh16"] = np.tile(np.arange(16, dtype=np.int16), (128, 1))
    return c


def build(taps=(), upto="full", dbg=False):
    nc = bacc.Bacc("TRN2", target_bir_lowering=False)
    dk = {"kind": "ExternalOutput"} if dbg else {}

    x_in = nc.dram_tensor("x_in", [T, D], F32, kind="ExternalInput")
    wq_d = nc.dram_tensor("wq", [D, D], F32, kind="ExternalInput")
    wk_d = nc.dram_tensor("wk", [D, 1024], F32, kind="ExternalInput")
    wv_d = nc.dram_tensor("wv", [D, 1024], F32, kind="ExternalInput")
    wo_d = nc.dram_tensor("wo", [D, D], F32, kind="ExternalInput")
    qw_d = nc.dram_tensor("qw", [8, D, 128], I32, kind="ExternalInput")
    lut4_d = nc.dram_tensor("lut4", [D, 16], F32, kind="ExternalInput")
    lut8_d = nc.dram_tensor("lut8", [D, 256], F32, kind="ExternalInput")
    bias_d = nc.dram_tensor("biasr", [128, 32], F32, kind="ExternalInput")
    idn32_d = nc.dram_tensor("idn32c", [128, 128], F32, kind="ExternalInput")
    idn16_d = nc.dram_tensor("idn16c", [128, 128], BF16, kind="ExternalInput")
    onesk_d = nc.dram_tensor("ones_kc", [128, 1], F32, kind="ExternalInput")
    onesm_d = nc.dram_tensor("ones_mc", [1, 128], F32, kind="ExternalInput")
    ropec_d = nc.dram_tensor("ropecc", [128, T], F32, kind="ExternalInput")
    ropes_d = nc.dram_tensor("ropesc", [128, T], F32, kind="ExternalInput")
    rotT_d = nc.dram_tensor("rotTc", [128, 128], F32, kind="ExternalInput")
    ltm_d = nc.dram_tensor("ltmaskc", [128, 128], F32, kind="ExternalInput")
    swam_d = nc.dram_tensor("swamc", [W, 2 * W], F32, kind="ExternalInput")
    swam0_d = nc.dram_tensor("swam0c", [W, 2 * W], F32, kind="ExternalInput")
    sh16_d = nc.dram_tensor("sh16c", [128, 16], I16, kind="ExternalInput")

    out_d = nc.dram_tensor("out", [T, D], F32, kind="ExternalOutput")
    tap_d = {name: nc.dram_tensor(f"tap_{name}", [T, D], F32, kind="ExternalOutput")
             for name in taps}

    xA = nc.dram_tensor("xA", [KT, 128, T], F32)
    xB = nc.dram_tensor("xB", [KT, 128, T], F32)
    xC = nc.dram_tensor("xC", [KT, 128, T], F32)
    wf = nc.dram_tensor("wf", [4, 32, 128, D], BF16, **dk)
    wqs = nc.dram_tensor("wqs", [2, 32, 128, D], BF16)
    wks = nc.dram_tensor("wks", [2, 8, 128, D], BF16)
    wvs = nc.dram_tensor("wvs", [2, 8, 128, D], BF16)
    wos = nc.dram_tensor("wos", [2, 32, 128, D], BF16)
    qs_s = nc.dram_tensor("qs_s", [NH, 128, T], F32, **dk)
    qr_s = nc.dram_tensor("qr_s", [2, NH, 128, T], BF16, **dk)
    ks_s = nc.dram_tensor("ks_s", [NKV, 128, T], F32, **dk)
    kr_s = nc.dram_tensor("kr_s", [2, NKV, 128, T], BF16, **dk)
    v_s = nc.dram_tensor("v_s", [NKV, 128, T], F32, **dk)
    g_s = nc.dram_tensor("g_s", [NKV, 128, T], F32, **dk)
    o_s = nc.dram_tensor("o_s", [NH, 128, T], BF16, **dk)
    dbg_d = nc.dram_tensor("dbg_d", [13, 128, 128], F32,
                           kind="ExternalOutput") if dbg else None
    dbg2_d = nc.dram_tensor("dbg2_d", [3, 64, 128], F32,
                            kind="ExternalOutput") if dbg else None

    st = contextlib.ExitStack()
    en = st.enter_context
    with st:
        idn32 = en(nc.sbuf_tensor("c_idn32", [128, 128], F32))
        idn16 = en(nc.sbuf_tensor("c_idn16", [128, 128], BF16))
        onesk = en(nc.sbuf_tensor("c_onesk", [128, 1], F32))
        onesm = en(nc.sbuf_tensor("c_onesm", [1, 128], F32))
        rotT = en(nc.sbuf_tensor("c_rotT", [128, 128], F32))
        ltm = en(nc.sbuf_tensor("c_ltm", [128, 128], F32))
        swam = en(nc.sbuf_tensor("c_swam", [W, 2 * W], F32))
        swam0 = en(nc.sbuf_tensor("c_swam0", [W, 2 * W], F32))
        sh16 = en(nc.sbuf_tensor("c_sh16", [128, 16], I16))
        biasr = en(nc.sbuf_tensor("c_biasr", [128, 32], F32))
        hbias = en(nc.sbuf_tensor("c_hbias", [128, 32], F32))

        eT = Sem(nc, "s_eT")
        eV = Sem(nc, "s_eV")
        eS = Sem(nc, "s_eS")
        eG = Sem(nc, "s_eG")
        SL = [Sem(nc, f"sl{i}") for i in range(56)]
        _next_sl = [0]

        def slots(n):
            i = _next_sl[0]
            _next_sl[0] += n
            assert _next_sl[0] <= len(SL), "out of slot semaphores"
            return SL[i:i + n]

        def slot_reset():
            _next_sl[0] = 0

        dI = Sem(nc, "s_dI")

        with nc.Block() as blk:
            @blk.sync
            def _(sy):
                for dst, src in ((idn32, idn32_d), (idn16, idn16_d), (onesk, onesk_d),
                                 (onesm, onesm_d),
                                 (rotT, rotT_d), (ltm, ltm_d), (swam, swam_d),
                                 (swam0, swam0_d), (sh16, sh16_d), (biasr, bias_d)):
                    dI.inc(sy.dma_start(dst[:], src.ap()), 16)

            @blk.vector
            def _(ve):
                ve.wait_ge(dI.h, dI.n)
                eV.inc(ve.tensor_scalar(hbias[:], biasr[:], 0.5, None, ALU.mult))

        def barrier():
            marks = [(s.h, s.n) for s in (eT, eV, eS, eG, dI)] + \
                    [(s.h, s.n) for s in SL if s.n > 0]
            with nc.Block() as blk:
                for engname in ("sync", "tensor", "vector", "scalar", "gpsimd"):
                    def mk(_eng):
                        for h, v in marks:
                            _eng.wait_ge(h, v)
                    getattr(blk, engname)(mk)

        def pap(t, p0, pstride, pcount, flist):
            ps = t.ap().ap[0][0]
            return bass.AP(t, p0 * ps, [[pstride * ps, pcount]] + flist)

        with contextlib.ExitStack() as sst, nc.Block() as blk:
            sen = sst.enter_context
            ebuf = [sen(nc.sbuf_tensor(f"en_b{i}", [128, D], F32)) for i in range(2)]
            evac = [sen(nc.sbuf_tensor(f"en_e{i}", [128, 128], F32)) for i in range(4)]
            epp = [sen(nc.psum_tensor(f"en_p{i}", [128, 512], F32)) for i in range(4)]
            ld = slots(2)
            wr = slots(4)
            b_eT, b_eV = eT.n, eV.n
            ldv = [ld[tt % 2].n + 16 * (tt // 2 + 1) for tt in range(16)]
            wrv0 = [w.n for w in wr]

            @blk.sync
            def _(sy):
                sy.wait_ge(dI.h, dI.n)
                for tt in range(16):
                    if tt >= 2:
                        sy.wait_ge(eT.h, b_eT + 32 * (tt - 1))
                    ld[tt % 2].inc(sy.dma_start(ebuf[tt % 2][:],
                                                x_in[128 * tt:128 * (tt + 1), :]), 16)

            @blk.tensor
            def _(te):
                for tt in range(16):
                    te.wait_ge(ld[tt % 2].h, ldv[tt])
                    for k in range(KT):
                        idx = tt * KT + k
                        if idx >= 4:
                            te.wait_ge(eV.h, b_eV + idx - 3)
                        eT.inc(te.transpose(epp[idx % 4][:, 0:128],
                                            ebuf[tt % 2][:, 128 * k:128 * (k + 1)],
                                            idn32[:]))

            @blk.vector
            def _(ve):
                for tt in range(16):
                    for k in range(KT):
                        idx = tt * KT + k
                        ve.wait_ge(eT.h, b_eT + idx + 1)
                        if idx >= 4:
                            ve.wait_ge(wr[idx % 4].h, wrv0[idx % 4] + 16 * (idx // 4))
                        eV.inc(ve.tensor_copy(evac[idx % 4][:], epp[idx % 4][:, 0:128]))

            @blk.scalar
            def _(sc):
                for tt in range(16):
                    for k in range(KT):
                        idx = tt * KT + k
                        sc.wait_ge(eV.h, b_eV + idx + 1)
                        wr[idx % 4].inc(sc.dma_start(
                            xA[k, :, 128 * tt:128 * (tt + 1)], evac[idx % 4][:]), 16)

        barrier()
        slot_reset()

        if upto != "entry":
            with contextlib.ExitStack() as sst, nc.Block() as blk:
                sen = sst.enter_context
                lutb = [sen(nc.sbuf_tensor(f"dq_lut{i}", [128, 272], F32)) for i in range(2)]
                qwb = [sen(nc.sbuf_tensor(f"dq_qw{i}", [128, 256], I16)) for i in range(8)]
                shexp = sen(nc.sbuf_tensor("dq_shx", [128, 4096], I16))
                cA = sen(nc.sbuf_tensor("dq_cA", [128, 4096], I16))
                cB = sen(nc.sbuf_tensor("dq_cB", [128, 4096], I16))
                ci32 = sen(nc.sbuf_tensor("dq_c32", [128, 4096], I32))
                c8f = sen(nc.sbuf_tensor("dq_c8f", [128, 4096], F32))
                c4f = sen(nc.sbuf_tensor("dq_c4f", [128, 4096], F32))
                acc = [sen(nc.sbuf_tensor(f"dq_ac{i}", [128, 4096], F32)) for i in range(2)]
                wsp = [sen(nc.sbuf_tensor(f"dq_ws{i}", [128, 4096], BF16)) for i in range(4)]
                ebuf = [sen(nc.sbuf_tensor(f"dq_eb{i}", [128, 1024], BF16)) for i in range(4)]
                wps = [sen(nc.psum_tensor(f"dq_p{i}", [128, 1024], BF16)) for i in range(8)]

                s_lut = slots(2)
                s_qw = slots(8)
                s_wr = slots(4)
                b_eV, b_eT, b_eS = eV.n, eT.n, eS.n
                VPT = 175

                def vm(tau, ph):
                    return b_eV + VPT * tau + {1: 35, 2: 171, 3: VPT}[ph]

                lut_v = [[s_lut[j].n + 16 * (tau + 1) for j in range(2)] for tau in range(32)]
                qw_v = [[s_qw[p].n + 16 * (tau + 1) for p in range(8)] for tau in range(32)]
                wr_b = [s_wr[p].n for p in range(4)]

                @blk.sync
                def _(sy):
                    for tau in range(32):
                        if tau >= 2:
                            sy.wait_ge(eV.h, vm(tau - 2, 2))
                        sy.dma_start(lutb[tau % 2][:, 0:256],
                                     lut8_d[128 * tau:128 * (tau + 1), :]).then_inc(
                            s_lut[0].h, 16)
                        s_lut[0].n += 16
                        sy.dma_start(lutb[tau % 2][:, 256:272],
                                     lut4_d[128 * tau:128 * (tau + 1), :]).then_inc(
                            s_lut[1].h, 16)
                        s_lut[1].n += 16
                        if tau >= 1:
                            sy.wait_ge(eV.h, vm(tau - 1, 1))
                        for p in range(8):
                            sy.dma_start(qwb[p][:],
                                         qw_d[p, 128 * tau:128 * (tau + 1), :]
                                         .bitcast(I16)).then_inc(s_qw[p].h, 16)
                            s_qw[p].n += 16

                @blk.vector
                def _(ve):
                    for tau in range(32):
                        for p in range(8):
                            ve.wait_ge(s_qw[p].h, qw_v[tau][p])
                            qbc = bass.AP(qwb[p], 0,
                                          [list(qwb[p].ap().ap[0]), [1, 256], [0, 16]])
                            shbc = bass.AP(sh16, 0,
                                           [list(sh16.ap().ap[0]), [0, 256], [1, 16]])
                            eV.inc(ve.tensor_tensor(shexp[:], qbc, shbc,
                                                    ALU.logical_shift_right))
                            if p == 0:
                                eV.inc(ve.tensor_scalar(cB[:], shexp[:], 1,
                                                        None, ALU.bitwise_and))
                            else:
                                eV.inc(ve.tensor_scalar(shexp[:], shexp[:], 1, None,
                                                        ALU.bitwise_and))
                                eV.inc(ve.tensor_scalar(shexp[:], shexp[:], p, None,
                                                        ALU.logical_shift_left))
                                src_acc = cB[:] if p % 2 == 1 else cA[:]
                                dst_acc = cA[:] if p % 2 == 1 else cB[:]
                                eV.inc(ve.tensor_tensor(dst_acc, shexp[:], src_acc,
                                                        ALU.bitwise_or))
                        eV.inc(ve.tensor_scalar(cB[:], cA[:], 15, None,
                                                ALU.bitwise_and))
                        eV.inc(ve.tensor_copy(ci32[:], cA[:]))
                        eV.inc(ve.tensor_copy(c8f[:], ci32[:]))
                        eV.inc(ve.tensor_copy(ci32[:], cB[:]))
                        eV.inc(ve.tensor_copy(c4f[:], ci32[:]))
                        ve.wait_ge(s_lut[0].h, lut_v[tau][0])
                        ve.wait_ge(s_lut[1].h, lut_v[tau][1])
                        pk = lutb[tau % 2]
                        eV.inc(ve._custom_dve(LUT_SEED2, out=acc[0][:], in0=c8f[:],
                                              s0=pk[:, 0:1], s1=pk[:, 1:2], imm2=0.0))
                        for v in range(2, 256, 2):
                            eV.inc(ve._custom_dve(
                                LUT_SEL2, out=acc[0][:], in0=c8f[:], in1=acc[0][:],
                                s0=pk[:, v:v + 1], s1=pk[:, v + 1:v + 2],
                                imm2=float(v)))
                        eV.inc(ve._custom_dve(LUT_SEED2, out=acc[1][:], in0=c4f[:],
                                              s0=pk[:, 256:257], s1=pk[:, 257:258],
                                              imm2=0.0))
                        for v in range(2, 16, 2):
                            eV.inc(ve._custom_dve(
                                LUT_SEL2, out=acc[1][:], in0=c4f[:], in1=acc[1][:],
                                s0=pk[:, 256 + v:257 + v], s1=pk[:, 257 + v:258 + v],
                                imm2=float(v)))
                        if tau >= 1:
                            ve.wait_ge(eT.h, b_eT + 128 * tau)
                        eV.inc(ve.tensor_copy(wsp[0][:], acc[0][:]))
                        eV.inc(ve.tensor_tensor(wsp[1][:], acc[0][:], wsp[0][:],
                                                ALU.subtract))
                        eV.inc(ve.tensor_copy(wsp[2][:], acc[1][:]))
                        eV.inc(ve.tensor_tensor(wsp[3][:], acc[1][:], wsp[2][:],
                                                ALU.subtract))

                @blk.tensor
                def _(te):
                    for tau in range(32):
                        te.wait_ge(eV.h, vm(tau, 3))
                        for r in range(4):
                            for p in range(4):
                                for k8 in range(8):
                                    g = 128 * tau + 32 * r + 8 * p + k8
                                    if g >= 8:
                                        te.wait_ge(eS.h, b_eS + g - 7)
                                    k = 8 * r + k8
                                    eT.inc(te.transpose(
                                        wps[g % 8][:, 0:128],
                                        wsp[p][:, 128 * k:128 * (k + 1)],
                                        idn16[:]))

                @blk.scalar
                def _(sc):
                    for tau in range(32):
                        for r in range(4):
                            for p in range(4):
                                i = tau * 4 + r
                                if i >= 1:
                                    sc.wait_ge(s_wr[p].h, wr_b[p] + 16 * i)
                                for k8 in range(8):
                                    g = 128 * tau + 32 * r + 8 * p + k8
                                    sc.wait_ge(eT.h, b_eT + g + 1)
                                    eS.inc(sc.activation(
                                        ebuf[p][:, 128 * k8:128 * (k8 + 1)],
                                        wps[g % 8][:, 0:128], AF.Copy))

                @blk.gpsimd
                def _(gp):
                    for tau in range(32):
                        for r in range(4):
                            for p in range(4):
                                gp.wait_ge(eS.h,
                                           b_eS + 128 * tau + 32 * r + 8 * (p + 1))
                                gp.dma_start(wf[p, tau, :, 1024 * r:1024 * (r + 1)],
                                             ebuf[p][:]).then_inc(s_wr[p].h, 16)
                                s_wr[p].n += 16

            barrier()

        def emit_ffn(xsrc, xdst, tag):
            slot_reset()
            sst = contextlib.ExitStack()
            sen = sst.enter_context
            ps = [sen(nc.psum_tensor(f"f{tag}_p{i}", [128, 512], F32)) for i in range(8)]
            xb = [sen(nc.sbuf_tensor(f"f{tag}_x{i}", [128, 512], F32)) for i in range(4)]
            xsq = [sen(nc.sbuf_tensor(f"f{tag}_q{i}", [128, 512], F32)) for i in range(2)]
            rows = sen(nc.sbuf_tensor(f"f{tag}_rw", [1, 512 * 8], F32))
            lnf = [sen(nc.sbuf_tensor(f"f{tag}_lf{i}", [128, 512], F32)) for i in range(2)]
            lnhi = sen(nc.sbuf_tensor(f"f{tag}_lh", [128, KT * 1024], BF16))
            lnlo = sen(nc.sbuf_tensor(f"f{tag}_ll", [128, KT * 1024], BF16))
            mbuf = [sen(nc.sbuf_tensor(f"f{tag}_mb{i}", [128, 512], I32)) for i in range(2)]
            wbuf = [sen(nc.sbuf_tensor(f"f{tag}_w{i}", [128, 2 * 2048], BF16))
                    for i in range(2)]
            ysel = [sen(nc.sbuf_tensor(f"f{tag}_ys{i}", [128, 512], F32)) for i in range(2)]
            yh = [sen(nc.sbuf_tensor(f"f{tag}_yh{i}", [128, 512], F32)) for i in range(2)]
            xr = [sen(nc.sbuf_tensor(f"f{tag}_xr{i}", [128, 512], F32)) for i in range(4)]
            xo = [sen(nc.sbuf_tensor(f"f{tag}_xo{i}", [128, 512], F32)) for i in range(2)]
            s_x = slots(4)
            s_w = slots(8)
            s_xr = slots(4)
            s_wr = slots(2)

            def ln_pass(h, tc):
                col0 = 1024 * h + 512 * tc
                b_eT, b_eV, b_eS = eT.n, eV.n, eS.n
                xv = [s_x[k % 4].n + 16 * (k // 4 + 1) for k in range(32)]

                with nc.Block() as blk:
                    @blk.sync
                    def _(sy):
                        for k in range(KT):
                            if k >= 4:
                                sy.wait_ge(eT.h, b_eT + 2 * (k - 3))
                            sy.dma_start(xb[k % 4][:],
                                         xsrc[k, :, col0:col0 + 512]).then_inc(
                                s_x[k % 4].h, 16)
                            s_x[k % 4].n += 16

                    @blk.scalar
                    def _(sc):
                        for k in range(KT):
                            sc.wait_ge(s_x[k % 4].h, xv[k])
                            if k >= 2:
                                sc.wait_ge(eT.h, b_eT + 2 * (k - 1))
                            eS.inc(sc.activation(xsq[k % 2][:], xb[k % 4][:], AF.Square))
                        sc.wait_ge(eV.h, b_eV + 5)
                        eS.inc(sc.activation(rows[:, 1536:2048], rows[:, 2560:3072],
                                             AF.Ln))
                        eS.inc(sc.activation(rows[:, 1536:2048], rows[:, 1536:2048],
                                             AF.Exp, scale=-0.5))

                    @blk.tensor
                    def _(te):
                        for k in range(KT):
                            te.wait_ge(s_x[k % 4].h, xv[k])
                            te.wait_ge(eS.h, b_eS + k + 1)
                            eT.inc(te.matmul(ps[4][0:1, :], onesk[:], xb[k % 4][:],
                                             start=(k == 0), stop=(k == KT - 1)))
                            eT.inc(te.matmul(ps[5][0:1, :], onesk[:], xsq[k % 2][:],
                                             start=(k == 0), stop=(k == KT - 1)))
                        te.wait_ge(eS.h, b_eS + 34)
                        te.wait_ge(eV.h, b_eV + 8)
                        eT.inc(te.matmul(ps[4][:], onesm[:], rows[0:1, 0:512],
                                         start=True, stop=True))
                        eT.inc(te.matmul(ps[5][:], onesm[:], rows[0:1, 1536:2048],
                                         start=True, stop=True))
                        eT.inc(te.matmul(ps[6][:], onesm[:], rows[0:1, 3584:4096],
                                         start=True, stop=True))

                    @blk.vector
                    def _(ve):
                        ve.wait_ge(eT.h, b_eT + 64)
                        eV.inc(ve.tensor_scalar(rows[:, 0:512], ps[4][0:1, :],
                                                1.0 / D, None, ALU.mult))
                        eV.inc(ve.tensor_scalar(rows[:, 512:1024], ps[5][0:1, :],
                                                1.0 / D, None, ALU.mult))
                        eV.inc(ve.tensor_tensor(rows[:, 2048:2560], rows[:, 0:512],
                                                rows[:, 0:512], ALU.mult))
                        eV.inc(ve.tensor_tensor(rows[:, 1024:1536], rows[:, 512:1024],
                                                rows[:, 2048:2560], ALU.subtract))
                        eV.inc(ve.tensor_scalar(rows[:, 2560:3072], rows[:, 1024:1536],
                                                EPS, None, ALU.add))
                        eV.inc(ve.reciprocal(rows[:, 3072:3584], rows[:, 2560:3072]))
                        eV.inc(ve.tensor_tensor(rows[:, 2048:2560], rows[:, 1024:1536],
                                                rows[:, 3072:3584], ALU.mult))
                        eV.inc(ve.tensor_scalar(rows[:, 3584:4096], rows[:, 2048:2560],
                                                1.0, None, ALU.is_gt))
                        ve.wait_ge(eT.h, b_eT + 67)
                        eV.inc(ve.tensor_copy(mbuf[tc][:], ps[6][:]))

                b2_eT, b2_eV, b2_eS = eT.n, eV.n, eS.n
                x2v = [s_x[k % 4].n + 16 * (k // 4 + 1) for k in range(32)]

                with nc.Block() as blk:
                    @blk.sync
                    def _(sy):
                        for k in range(KT):
                            if k >= 4:
                                sy.wait_ge(eV.h, b2_eV + 3 * (k - 3))
                            sy.dma_start(xb[k % 4][:],
                                         xsrc[k, :, col0:col0 + 512]).then_inc(
                                s_x[k % 4].h, 16)
                            s_x[k % 4].n += 16

                    @blk.vector
                    def _(ve):
                        for k in range(KT):
                            ve.wait_ge(s_x[k % 4].h, x2v[k])
                            if k >= 2:
                                ve.wait_ge(eS.h, b2_eS + k - 1)
                            eV.inc(ve.tensor_tensor(lnf[k % 2][:], xb[k % 4][:],
                                                    ps[4][:], ALU.subtract))
                            eV.inc(ve.tensor_tensor(lnf[k % 2][:], lnf[k % 2][:],
                                                    ps[5][:], ALU.mult))
                            ve.wait_ge(eS.h, b2_eS + k + 1)
                            col = 1024 * k + 512 * tc
                            eV.inc(ve.tensor_tensor(lnlo[:, col:col + 512], lnf[k % 2][:],
                                                    lnhi[:, col:col + 512], ALU.subtract))

                    @blk.scalar
                    def _(sc):
                        for k in range(KT):
                            sc.wait_ge(eV.h, b2_eV + 3 * k + 2)
                            col = 1024 * k + 512 * tc
                            eS.inc(sc.activation(lnhi[:, col:col + 512], lnf[k % 2][:],
                                                 AF.Copy))

            def sweep(h):
                b_eT, b_eV, b_eS = eT.n, eV.n, eS.n
                w_v = {}
                cnt = [0] * 8
                for tau in range(32):
                    for path in range(2):
                        for kh in range(2):
                            gi = (tau * 2 + path) * 2 + kh
                            for pj in range(2):
                                sl = 4 * (gi % 2) + pj
                                cnt[sl] += 1
                                w_v[(tau, path, kh, pj)] = (sl, s_w[sl].n + 16 * cnt[sl])
                xr_v = [s_xr[i % 4].n + 16 * (i // 4 + 1) for i in range(64)]
                wr_vv = [s_wr[i % 2].n + 16 * (i // 2 + 1) for i in range(64)]

                with nc.Block() as blk:
                    @blk.sync
                    def _(sy):
                        for tau in range(32):
                            for path in range(2):
                                for kh in range(2):
                                    gi = (tau * 2 + path) * 2 + kh
                                    for pj in range(2):
                                        sl, _v = w_v[(tau, path, kh, pj)]
                                        if gi >= 2:
                                            sy.wait_ge(eT.h, b_eT + 96 * (gi - 1))
                                        plane = path * 2 + pj
                                        sy.dma_start(
                                            wbuf[gi % 2][:, 2048 * pj:2048 * (pj + 1)],
                                            bass.AP(wf,
                                                    (plane * 32 + tau) * 128 * D + 2048 * kh,
                                                    [[D, 128], [1, 2048]])).then_inc(
                                            s_w[sl].h, 16)
                                        s_w[sl].n += 16


                    @blk.tensor
                    def _(te):
                        for tau in range(32):
                            for path in range(2):
                                for kh in range(2):
                                    gi = (tau * 2 + path) * 2 + kh
                                    for pj in range(2):
                                        sl, v = w_v[(tau, path, kh, pj)]
                                        te.wait_ge(s_w[sl].h, v)
                                    if tau >= 2 and path == 0 and kh == 0:
                                        te.wait_ge(eV.h,
                                                   b_eV + 4 * (2 * (tau - 2) + 2))
                                    wb = wbuf[gi % 2]
                                    for k16 in range(16):
                                        k = 16 * kh + k16
                                        colh = 1024 * k
                                        whi = wb[:, 128 * k16:128 * (k16 + 1)]
                                        wlo = wb[:, 2048 + 128 * k16:
                                                 2048 + 128 * (k16 + 1)]
                                        for tc in range(2):
                                            col = colh + 512 * tc
                                            bank = ps[4 * (tau % 2) + path * 2 + tc]
                                            first = (k == 0)
                                            eT.inc(te.matmul(
                                                bank[:], whi,
                                                lnhi[:, col:col + 512],
                                                start=first, stop=False))
                                            eT.inc(te.matmul(
                                                bank[:], whi,
                                                lnlo[:, col:col + 512],
                                                start=False, stop=False))
                                            eT.inc(te.matmul(
                                                bank[:], wlo,
                                                lnhi[:, col:col + 512],
                                                start=False,
                                                stop=(k == KT - 1)))

                    @blk.vector
                    def _(ve):
                        for tau in range(32):
                            for tc in range(2):
                                ti = 2 * tau + tc
                                ve.wait_ge(eT.h, b_eT + 384 * (tau + 1))
                                ve.wait_ge(s_xr[ti % 4].h, xr_v[ti])
                                if ti >= 2:
                                    ve.wait_ge(s_wr[ti % 2].h, wr_vv[ti - 2])
                                pbank_l = ps[4 * (tau % 2) + 2 + tc]
                                pbank_h = ps[4 * (tau % 2) + tc]
                                eV.inc(ve.tensor_scalar(ysel[ti % 2][:], pbank_l[:],
                                                        0.5, hbias[:, tau:tau + 1],
                                                        ALU.mult, ALU.add))
                                eV.inc(ve.tensor_scalar(yh[ti % 2][:], pbank_h[:],
                                                        0.5, hbias[:, tau:tau + 1],
                                                        ALU.mult, ALU.add))
                                eV.inc(ve.copy_predicated(ysel[ti % 2][:], mbuf[tc][:],
                                                          yh[ti % 2][:]))
                                eV.inc(ve.tensor_tensor(xo[ti % 2][:], ysel[ti % 2][:],
                                                        xr[ti % 4][:], ALU.add))

                    @blk.gpsimd
                    def _(gp):
                        for tau in range(32):
                            for tc in range(2):
                                ti = 2 * tau + tc
                                col0 = 1024 * h + 512 * tc
                                if ti >= 4:
                                    gp.wait_ge(eV.h, b_eV + 4 * (ti - 3))
                                gp.dma_start(xr[ti % 4][:],
                                             xsrc[tau, :, col0:col0 + 512]).then_inc(
                                    s_xr[ti % 4].h, 16)
                                s_xr[ti % 4].n += 16
                                gp.wait_ge(eV.h, b_eV + 4 * (ti + 1))
                                gp.dma_start(xdst[tau, :, col0:col0 + 512],
                                             xo[ti % 2][:]).then_inc(s_wr[ti % 2].h, 16)
                                s_wr[ti % 2].n += 16

            for h in range(2):
                ln_pass(h, 0)
                barrier()
                ln_pass(h, 1)
                barrier()
                sweep(h)
                barrier()
            sst.close()

        def emit_wsplit():
            slot_reset()
            sst = contextlib.ExitStack()
            sen = sst.enter_context
            wt = [sen(nc.sbuf_tensor(f"ws_t{i}", [128, D], F32)) for i in range(2)]
            whi = [sen(nc.sbuf_tensor(f"ws_h{i}", [128, D], BF16)) for i in range(2)]
            wlo = [sen(nc.sbuf_tensor(f"ws_l{i}", [128, D], BF16)) for i in range(2)]
            s_ld = slots(2)
            s_wr = slots(4)
            jobs = []
            for tau in range(32):
                jobs.append((wq_d, wqs, tau))
            for tau in range(8):
                jobs.append((wk_d, wks, tau))
            for tau in range(8):
                jobs.append((wv_d, wvs, tau))
            for tau in range(32):
                jobs.append((wo_d, wos, tau))
            NJ = len(jobs)
            b_eV, b_eS = eV.n, eS.n
            ld_v = [s_ld[i % 2].n + 16 * (i // 2 + 1) for i in range(NJ)]
            wr_v = [[s_wr[2 * (i % 2) + j].n + 16 * (i // 2 + 1) for j in range(2)]
                    for i in range(NJ)]

            with nc.Block() as blk:
                @blk.sync
                def _(sy):
                    for i, (srcd, dstd, tau) in enumerate(jobs):
                        if i >= 2:
                            sy.wait_ge(eV.h, b_eV + (i - 1))
                        ncol = srcd.shape[1]
                        src = bass.AP(srcd, 128 * tau,
                                      [[ncol, 128], [128 * ncol, 32], [1, 128]])
                        sy.dma_start(wt[i % 2][:], src).then_inc(s_ld[i % 2].h, 16)
                        s_ld[i % 2].n += 16

                @blk.scalar
                def _(sc):
                    for i, (srcd, dstd, tau) in enumerate(jobs):
                        sc.wait_ge(s_ld[i % 2].h, ld_v[i])
                        if i >= 2:
                            sc.wait_ge(s_wr[2 * (i % 2)].h, wr_v[i - 2][0])
                        eS.inc(sc.activation(whi[i % 2][:], wt[i % 2][:], AF.Copy))
                        sc.wait_ge(eV.h, b_eV + i + 1)
                        sc.dma_start(dstd[0, tau, :, :], whi[i % 2][:]).then_inc(
                            s_wr[2 * (i % 2)].h, 16)
                        s_wr[2 * (i % 2)].n += 16
                        sc.dma_start(dstd[1, tau, :, :], wlo[i % 2][:]).then_inc(
                            s_wr[2 * (i % 2) + 1].h, 16)
                        s_wr[2 * (i % 2) + 1].n += 16

                @blk.vector
                def _(ve):
                    for i, (srcd, dstd, tau) in enumerate(jobs):
                        ve.wait_ge(eS.h, b_eS + i + 1)
                        if i >= 2:
                            ve.wait_ge(s_wr[2 * (i % 2) + 1].h, wr_v[i - 2][1])
                        eV.inc(ve.tensor_tensor(wlo[i % 2][:], wt[i % 2][:],
                                                whi[i % 2][:], ALU.subtract))
            sst.close()
            barrier()

        def emit_proj(xsrc):
            slot_reset()
            sst = contextlib.ExitStack()
            sen = sst.enter_context
            ps = [sen(nc.psum_tensor(f"pr_p{i}", [128, 512], F32)) for i in range(8)]
            xb = [sen(nc.sbuf_tensor(f"pr_x{i}", [128, 512], F32)) for i in range(4)]
            xsq = [sen(nc.sbuf_tensor(f"pr_q{i}", [128, 512], F32)) for i in range(2)]
            rows = sen(nc.sbuf_tensor("pr_rw", [1, 512 * 8], F32))
            lnf = [sen(nc.sbuf_tensor(f"pr_lf{i}", [128, 512], F32)) for i in range(2)]
            lnhi = sen(nc.sbuf_tensor("pr_lh", [128, KT * 512], BF16))
            lnlo = sen(nc.sbuf_tensor("pr_ll", [128, KT * 512], BF16))
            wbuf = [sen(nc.sbuf_tensor(f"pr_w{i}", [128, 2 * 2048], BF16))
                    for i in range(2)]
            cosb = sen(nc.sbuf_tensor("pr_cos", [128, 512], F32))
            sinb = sen(nc.sbuf_tensor("pr_sin", [128, 512], F32))
            qraw = [sen(nc.sbuf_tensor(f"pr_qr{i}", [128, 512], F32)) for i in range(2)]
            t1b = [sen(nc.sbuf_tensor(f"pr_t1{i}", [128, 512], F32)) for i in range(2)]
            t2b = [sen(nc.sbuf_tensor(f"pr_t2{i}", [128, 512], F32)) for i in range(2)]
            rpf = [sen(nc.sbuf_tensor(f"pr_rp{i}", [128, 512], F32)) for i in range(2)]
            rphi = [sen(nc.sbuf_tensor(f"pr_rh{i}", [128, 512], BF16)) for i in range(2)]
            rplo = [sen(nc.sbuf_tensor(f"pr_rl{i}", [128, 512], BF16)) for i in range(2)]
            eb = [sen(nc.sbuf_tensor(f"pr_e{i}", [128, 512], F32)) for i in range(2)]
            qsf = [sen(nc.sbuf_tensor(f"pr_qs{i}", [128, 512], F32)) for i in range(2)]
            ub = [sen(nc.sbuf_tensor(f"pr_u{i}", [128, 512], F32)) for i in range(2)]
            vb = [sen(nc.sbuf_tensor(f"pr_v{i}", [128, 512], F32)) for i in range(2)]
            gsf = [sen(nc.sbuf_tensor(f"pr_gs{i}", [128, 512], F32)) for i in range(2)]
            s_x = slots(4)
            s_w = slots(8)
            s_sp = slots(4)

            def ln_pass2(h, tc):
                col0 = 512 * h
                b_eT, b_eV, b_eS = eT.n, eV.n, eS.n
                xv = [s_x[k % 4].n + 16 * (k // 4 + 1) for k in range(32)]

                with nc.Block() as blk:
                    @blk.sync
                    def _(sy):
                        for k in range(KT):
                            if k >= 4:
                                sy.wait_ge(eT.h, b_eT + 2 * (k - 3))
                            sy.dma_start(xb[k % 4][:],
                                         xsrc[k, :, col0:col0 + 512]).then_inc(
                                s_x[k % 4].h, 16)
                            s_x[k % 4].n += 16

                    @blk.scalar
                    def _(sc):
                        for k in range(KT):
                            sc.wait_ge(s_x[k % 4].h, xv[k])
                            if k >= 2:
                                sc.wait_ge(eT.h, b_eT + 2 * (k - 1))
                            eS.inc(sc.activation(xsq[k % 2][:], xb[k % 4][:], AF.Square))
                        sc.wait_ge(eV.h, b_eV + 5)
                        eS.inc(sc.activation(rows[:, 1536:2048], rows[:, 2560:3072],
                                             AF.Ln))
                        eS.inc(sc.activation(rows[:, 1536:2048], rows[:, 1536:2048],
                                             AF.Exp, scale=-0.5))

                    @blk.tensor
                    def _(te):
                        for k in range(KT):
                            te.wait_ge(s_x[k % 4].h, xv[k])
                            te.wait_ge(eS.h, b_eS + k + 1)
                            eT.inc(te.matmul(ps[4][0:1, :], onesk[:], xb[k % 4][:],
                                             start=(k == 0), stop=(k == KT - 1)))
                            eT.inc(te.matmul(ps[5][0:1, :], onesk[:], xsq[k % 2][:],
                                             start=(k == 0), stop=(k == KT - 1)))
                        te.wait_ge(eS.h, b_eS + 34)
                        te.wait_ge(eV.h, b_eV + 5)
                        eT.inc(te.matmul(ps[4][:], onesm[:], rows[0:1, 0:512],
                                         start=True, stop=True))
                        eT.inc(te.matmul(ps[5][:], onesm[:], rows[0:1, 1536:2048],
                                         start=True, stop=True))

                    @blk.vector
                    def _(ve):
                        ve.wait_ge(eT.h, b_eT + 64)
                        eV.inc(ve.tensor_scalar(rows[:, 0:512], ps[4][0:1, :],
                                                1.0 / D, None, ALU.mult))
                        eV.inc(ve.tensor_scalar(rows[:, 512:1024], ps[5][0:1, :],
                                                1.0 / D, None, ALU.mult))
                        eV.inc(ve.tensor_tensor(rows[:, 2048:2560], rows[:, 0:512],
                                                rows[:, 0:512], ALU.mult))
                        eV.inc(ve.tensor_tensor(rows[:, 1024:1536], rows[:, 512:1024],
                                                rows[:, 2048:2560], ALU.subtract))
                        eV.inc(ve.tensor_scalar(rows[:, 2560:3072], rows[:, 1024:1536],
                                                EPS, None, ALU.add))

                b2_eT, b2_eV, b2_eS = eT.n, eV.n, eS.n
                x2v = [s_x[k % 4].n + 16 * (k // 4 + 1) for k in range(32)]

                with nc.Block() as blk:
                    @blk.sync
                    def _(sy):
                        for k in range(KT):
                            if k >= 4:
                                sy.wait_ge(eV.h, b2_eV + 3 * (k - 3))
                            sy.dma_start(xb[k % 4][:],
                                         xsrc[k, :, col0:col0 + 512]).then_inc(
                                s_x[k % 4].h, 16)
                            s_x[k % 4].n += 16

                    @blk.vector
                    def _(ve):
                        for k in range(KT):
                            ve.wait_ge(s_x[k % 4].h, x2v[k])
                            if k >= 2:
                                ve.wait_ge(eS.h, b2_eS + k - 1)
                            eV.inc(ve.tensor_tensor(lnf[k % 2][:], xb[k % 4][:],
                                                    ps[4][:], ALU.subtract))
                            eV.inc(ve.tensor_tensor(lnf[k % 2][:], lnf[k % 2][:],
                                                    ps[5][:], ALU.mult))
                            ve.wait_ge(eS.h, b2_eS + k + 1)
                            col = 512 * k
                            eV.inc(ve.tensor_tensor(lnlo[:, col:col + 512], lnf[k % 2][:],
                                                    lnhi[:, col:col + 512], ALU.subtract))

                    @blk.scalar
                    def _(sc):
                        for k in range(KT):
                            sc.wait_ge(eV.h, b2_eV + 3 * k + 2)
                            col = 512 * k
                            eS.inc(sc.activation(lnhi[:, col:col + 512], lnf[k % 2][:],
                                                 AF.Copy))

            def jtype(tau):
                return "q" if tau < 32 else ("k" if tau < 40 else "v")

            def jsrc(tau):
                if tau < 32:
                    return wqs, tau
                if tau < 40:
                    return wks, tau - 32
                return wvs, tau - 40

            def sweep2(h):
                for tau in range(48):
                    t = jtype(tau)
                    dstd, lt = jsrc(tau)
                    b_eT, b_eV, b_eS = eT.n, eV.n, eS.n
                    wv = []
                    with nc.Block() as blk:
                        @blk.sync
                        def _(sy):
                            for kh in range(2):
                                for pj in range(2):
                                    sl = s_w[2 * kh + pj]
                                    sy.dma_start(
                                        wbuf[kh][:, 2048 * pj:2048 * (pj + 1)],
                                        bass.AP(dstd,
                                                (pj * dstd.shape[1] + lt) * 128 * D
                                                + 2048 * kh,
                                                [[D, 128], [1, 2048]])).then_inc(
                                        sl.h, 16)
                                    sl.n += 16
                                    wv.append((sl, sl.n))

                        @blk.tensor
                        def _(te):
                            for kh in range(2):
                                for sl, v in wv[2 * kh:2 * kh + 2]:
                                    te.wait_ge(sl.h, v)
                                wb = wbuf[kh]
                                for k16 in range(16):
                                    k = 16 * kh + k16
                                    colh = 512 * k
                                    whi_ = wb[:, 128 * k16:128 * (k16 + 1)]
                                    wlo_ = wb[:, 2048 + 128 * k16:2048 + 128 * (k16 + 1)]
                                    for tc in range(1):
                                        col = colh + 512 * tc
                                        bank = ps[tc]
                                        eT.inc(te.matmul(bank[:], whi_,
                                                         lnhi[:, col:col + 512],
                                                         start=(k == 0), stop=False))
                                        eT.inc(te.matmul(bank[:], whi_,
                                                         lnlo[:, col:col + 512],
                                                         start=False, stop=False))
                                        eT.inc(te.matmul(bank[:], wlo_,
                                                         lnhi[:, col:col + 512],
                                                         start=False,
                                                         stop=(k == KT - 1)))
                            if t in ("q", "k"):
                                soff = {"q": 2, "k": 4}[t]
                                for tc in range(1):
                                    te.wait_ge(eS.h, b_eS + soff * tc + 1)
                                    eT.inc(te.matmul(ps[4 + tc][:], rotT[:],
                                                     qraw[tc][:], start=True, stop=True))
                                    te.wait_ge(eS.h, b_eS + soff * tc + 2)
                                    eT.inc(te.matmul(ps[6][0:1, 0:512], onesk[:],
                                                     eb[tc][:], start=True, stop=True))
                                    te.wait_ge(eV.h, b_eV + 7 * tc + 6)
                                    eT.inc(te.matmul(ps[7][:], onesm[:],
                                                     rows[0:1, 3584:4096],
                                                     start=True, stop=True))

                        @blk.scalar
                        def _(sc):
                            for tc in range(1):
                                bank = ps[tc]
                                sc.wait_ge(eT.h, b_eT + 96)
                                if t == "v":
                                    eS.inc(sc.activation(vb[tc][:], bank[:], AF.Copy))
                                    sc.dma_start(
                                        v_s[tau - 40, :, 512 * h:512 * h + 512],
                                        vb[tc][:]).then_inc(s_sp[tc].h, 16)
                                    s_sp[tc].n += 16
                                else:
                                    eS.inc(sc.activation(qraw[tc][:], bank[:], AF.Copy))
                                    eS.inc(sc.activation(eb[tc][:], bank[:], AF.Exp))
                                    if t == "k":
                                        eS.inc(sc.activation(ub[tc][:], bank[:], AF.Exp,
                                                             scale=-1.0))
                                        eS.inc(sc.activation(ub[tc][:], ub[tc][:],
                                                             AF.Ln, bias=1.0))

                        if t != "v":
                            @blk.vector
                            def _(ve):
                                soff = {"q": 2, "k": 4}[t]
                                for tc in range(1):
                                    col0 = 512 * h
                                    ve.wait_ge(eT.h, b_eT + 96 + 3 * tc + 1)
                                    eV.inc(ve.tensor_tensor(t1b[tc][:], qraw[tc][:],
                                                            cosb[:],
                                                            ALU.mult))
                                    eV.inc(ve.tensor_tensor(t2b[tc][:], ps[4 + tc][:],
                                                            sinb[:],
                                                            ALU.mult))
                                    eV.inc(ve.tensor_tensor(rpf[tc][:], t1b[tc][:],
                                                            t2b[tc][:], ALU.add))
                                    eV.inc(ve.tensor_copy(rphi[tc][:], rpf[tc][:]))
                                    eV.inc(ve.tensor_tensor(rplo[tc][:], rpf[tc][:],
                                                            rphi[tc][:], ALU.subtract))
                                    ve.wait_ge(eT.h, b_eT + 96 + 3 * tc + 2)
                                    eV.inc(ve.reciprocal(rows[0:1, 3584:4096],
                                                         ps[6][0:1, 0:512]))
                                    ve.wait_ge(eT.h, b_eT + 96 + 3 * tc + 3)
                                    eV.inc(ve.tensor_tensor(qsf[tc][:], eb[tc][:],
                                                            ps[7][:], ALU.mult))
                                    if t == "k":
                                        ve.wait_ge(eS.h, b_eS + soff * tc + 4)
                                        eV.inc(ve.tensor_scalar(gsf[tc][:], ub[tc][:],
                                                                -1.0 / 16.0, None,
                                                                ALU.mult))

                            @blk.gpsimd
                            def _(gp):
                                voff = {"q": 7, "k": 8}[t]
                                for tc in range(1):
                                    col0 = 512 * h
                                    gp.wait_ge(eV.h, b_eV + voff * tc + 5)
                                    if t == "q":
                                        gp.dma_start(qr_s[0, tau, :, col0:col0 + 512],
                                                     rphi[tc][:]).then_inc(s_sp[0].h, 16)
                                        s_sp[0].n += 16
                                        gp.dma_start(qr_s[1, tau, :, col0:col0 + 512],
                                                     rplo[tc][:]).then_inc(s_sp[1].h, 16)
                                        s_sp[1].n += 16
                                    else:
                                        gp.dma_start(kr_s[0, tau - 32, :, col0:col0 + 512],
                                                     rphi[tc][:]).then_inc(s_sp[0].h, 16)
                                        s_sp[0].n += 16
                                        gp.dma_start(kr_s[1, tau - 32, :, col0:col0 + 512],
                                                     rplo[tc][:]).then_inc(s_sp[1].h, 16)
                                        s_sp[1].n += 16
                                    gp.wait_ge(eV.h, b_eV + voff * tc + 7)
                                    dst = (qs_s[tau, :, col0:col0 + 512] if t == "q"
                                           else ks_s[tau - 32, :, col0:col0 + 512])
                                    gp.dma_start(dst, qsf[tc][:]).then_inc(s_sp[2].h, 16)
                                    s_sp[2].n += 16
                                    if t == "k":
                                        gp.wait_ge(eV.h, b_eV + voff * tc + 8)
                                        gp.dma_start(g_s[tau - 32, :, col0:col0 + 512],
                                                     gsf[tc][:]).then_inc(s_sp[3].h, 16)
                                        s_sp[3].n += 16
                    barrier()

            for h in range(4):
                with nc.Block() as blk:
                    @blk.sync
                    def _(sy):
                        dI.inc(sy.dma_start(
                            cosb[:], ropec_d[:, 512 * h:512 * (h + 1)]), 16)
                        dI.inc(sy.dma_start(
                            sinb[:], ropes_d[:, 512 * h:512 * (h + 1)]), 16)
                barrier()
                ln_pass2(h, 0)
                barrier()
                sweep2(h)
                barrier()
            sst.close()

        if upto not in ("entry", "dequant"):
            emit_ffn(xA, xB, "1")

        if upto in ("proj", "mixer", "attn", "full"):
            emit_wsplit()
            emit_proj(xB)

        def emit_mixer():
            slot_reset()
            sst = contextlib.ExitStack()
            sen = sst.enter_context
            qsb = [sen(nc.sbuf_tensor(f"mx_qs{i}", [128, 4096], F32)) for i in range(2)]
            qrh = [sen(nc.sbuf_tensor(f"mx_qh{i}", [128, 4096], BF16)) for i in range(2)]
            qrl = [sen(nc.sbuf_tensor(f"mx_ql{i}", [128, 4096], BF16)) for i in range(2)]
            ksb = [sen(nc.sbuf_tensor(f"mx_ks{i}", [128, 1024], F32)) for i in range(2)]
            krh = [sen(nc.sbuf_tensor(f"mx_kh{i}", [128, 2048], BF16)) for i in range(2)]
            krl = [sen(nc.sbuf_tensor(f"mx_kl{i}", [128, 2048], BF16)) for i in range(2)]
            vsb = [sen(nc.sbuf_tensor(f"mx_vs{i}", [128, 2048], F32)) for i in range(2)]
            gsb = [sen(nc.sbuf_tensor(f"mx_gs{i}", [128, 1024], F32)) for i in range(2)]
            vteh = sen(nc.sbuf_tensor("mx_vteh", [128, 128], BF16))
            vtel = sen(nc.sbuf_tensor("mx_vtel", [128, 128], BF16))
            vtoh = sen(nc.sbuf_tensor("mx_vtoh", [128, 128], BF16))
            vtol = sen(nc.sbuf_tensor("mx_vtol", [128, 128], BF16))
            Sb = [sen(nc.sbuf_tensor(f"mx_S{i}", [128, 1024], F32)) for i in range(2)]
            Ab = sen(nc.sbuf_tensor("mx_A", [128, 128], F32))
            Ar = sen(nc.sbuf_tensor("mx_Ar", [128, 128], F32))
            Pb = sen(nc.sbuf_tensor("mx_P", [128, 128], F32))
            Pi = sen(nc.sbuf_tensor("mx_Pi", [128, 128], F32))
            ktb = sen(nc.sbuf_tensor("mx_kt", [128, 128], F32))
            kth = sen(nc.sbuf_tensor("mx_kth", [128, 128], BF16))
            ktl = sen(nc.sbuf_tensor("mx_ktl", [128, 128], BF16))
            kTh = sen(nc.sbuf_tensor("mx_kTh", [128, 128], BF16))
            kTl = sen(nc.sbuf_tensor("mx_kTl", [128, 128], BF16))
            s0s = sen(nc.sbuf_tensor("mx_s0s", [128, 128], F32))
            s0h = sen(nc.sbuf_tensor("mx_s0h", [128, 128], BF16))
            s0l = sen(nc.sbuf_tensor("mx_s0l", [128, 128], BF16))
            stmp = sen(nc.sbuf_tensor("mx_stmp", [128, 128], F32))
            r2 = sen(nc.sbuf_tensor("mx_r2", [128, 8], F32))
            qtb = sen(nc.sbuf_tensor("mx_qt", [128, 128], F32))
            qth = sen(nc.sbuf_tensor("mx_qth", [128, 128], BF16))
            qtl = sen(nc.sbuf_tensor("mx_qtl", [128, 128], BF16))
            smb = sen(nc.sbuf_tensor("mx_sm", [128, 128], F32))
            smh = sen(nc.sbuf_tensor("mx_smh", [128, 128], BF16))
            sml = sen(nc.sbuf_tensor("mx_sml", [128, 128], BF16))
            swb = sen(nc.sbuf_tensor("mx_sw", [W, 128], F32))
            swp = sen(nc.sbuf_tensor("mx_swp", [W, 128], F32))
            swsum = sen(nc.sbuf_tensor("mx_sws", [W, 1], F32))
            swr = sen(nc.sbuf_tensor("mx_swr", [W, 1], F32))
            swn = sen(nc.sbuf_tensor("mx_swn", [W, 128], F32))
            pth = sen(nc.sbuf_tensor("mx_pth", [128, W], BF16))
            ptl = sen(nc.sbuf_tensor("mx_ptl", [128, W], BF16))
            olh = sen(nc.sbuf_tensor("mx_olh", [128, 128], F32))
            ob = [sen(nc.sbuf_tensor(f"mx_ob{i}", [128, 128], BF16)) for i in range(2)]
            pTrA = sen(nc.psum_tensor("mx_pA", [128, 512], F32))
            pTrB = sen(nc.psum_tensor("mx_pB", [128, 512], F32))
            pU = sen(nc.psum_tensor("mx_pU", [128, 512], F32))
            pSCR = sen(nc.psum_tensor("mx_pS", [128, 512], F32))
            pO = sen(nc.psum_tensor("mx_pO", [128, 512], F32))
            pW = sen(nc.psum_tensor("mx_pW", [128, 512], F32))
            pY0 = sen(nc.psum_tensor("mx_pY0", [128, 512], F32))
            pY1 = sen(nc.psum_tensor("mx_pY1", [128, 512], F32))
            s_ld = slots(8)
            s_ob = slots(2)

            with nc.Block() as blk:
                @blk.vector
                def _(ve):
                    eV.inc(ve.memset(Sb[0][:], 0.0))
                    eV.inc(ve.memset(vsb[0][:], 0.0))
                    eV.inc(ve.memset(krh[0][:], 0.0))
                    eV.inc(ve.memset(krl[0][:], 0.0))
            barrier()

            for c in range(16):
                pc = c % 2
                with nc.Block() as blk:
                    @blk.sync
                    def _(sy):
                        def ld(i, dst, srcd, nh, plane, c0, cw):
                            if plane is None:
                                srcap = bass.AP(srcd, c0, [[T, 128], [128 * T, nh],
                                                           [1, cw]])
                            else:
                                srcap = bass.AP(srcd, plane * nh * 128 * T + c0,
                                                [[T, 128], [128 * T, nh], [1, cw]])
                            sy.dma_start(dst, srcap).then_inc(s_ld[i].h, 16)
                            s_ld[i].n += 16
                        ld(0, qsb[pc][:], qs_s, NH, None, 128 * c, 128)
                        ld(1, qrh[pc][:], qr_s, NH, 0, 128 * c, 128)
                        ld(2, qrl[pc][:], qr_s, NH, 1, 128 * c, 128)
                        ld(3, ksb[pc][:], ks_s, NKV, None, 128 * c, 128)
                        if c == 0:
                            kdst = bass.AP(krh[pc], 128,
                                           [[2048, 128], [256, 8], [1, 128]])
                            ld(4, kdst, kr_s, NKV, 0, 0, 128)
                            kdst = bass.AP(krl[pc], 128,
                                           [[2048, 128], [256, 8], [1, 128]])
                            ld(5, kdst, kr_s, NKV, 1, 0, 128)
                        else:
                            kdst = bass.AP(krh[pc], 0,
                                           [[2048, 128], [256, 8], [1, 256]])
                            ld(4, kdst, kr_s, NKV, 0, 128 * (c - 1), 256)
                            kdst = bass.AP(krl[pc], 0,
                                           [[2048, 128], [256, 8], [1, 256]])
                            ld(5, kdst, kr_s, NKV, 1, 128 * (c - 1), 256)
                        if c == 0:
                            vdst = bass.AP(vsb[pc], 128,
                                           [[2048, 128], [256, 8], [1, 128]])
                            ld(6, vdst, v_s, NKV, None, 0, 128)
                        else:
                            vdst = bass.AP(vsb[pc], 0,
                                           [[2048, 128], [256, 8], [1, 256]])
                            ld(6, vdst, v_s, NKV, None, 128 * (c - 1), 256)
                        ld(7, gsb[pc][:], g_s, NKV, None, 128 * c, 128)
                barrier()

                for j in range(NKV):
                    Ssrc, Sdst = Sb[c % 2], Sb[(c + 1) % 2]
                    bT, bV, bS = eT.n, eV.n, eS.n
                    obv = [s_ob[i].n for i in range(2)]
                    with nc.Block() as blk:
                        @blk.tensor
                        def _(te):
                            eT.inc(te.transpose(
                                pTrA[:, 0:128],
                                vsb[pc][:, 256 * j + 64:256 * j + 192], idn32[:]))
                            eT.inc(te.transpose(
                                pTrB[:, 128:256],
                                vsb[pc][:, 256 * j + 128:256 * j + 256], idn32[:]))
                            te.wait_ge(eV.h, bV + 4)
                            eT.inc(te.transpose(pSCR[:, 128:256], ktb[:], idn32[:]))
                            te.wait_ge(eV.h, bV + 12)
                            eT.inc(te.matmul(pU[:, 0:128], kTh[:], vtoh[:],
                                             start=True, stop=False))
                            eT.inc(te.matmul(pU[:, 0:128], kTh[:], vtol[:],
                                             start=False, stop=False))
                            eT.inc(te.matmul(pU[:, 0:128], kTl[:], vtoh[:],
                                             start=False, stop=True))
                            for il in range(4):
                                qb = bV + 17 + 18 * il
                                sb = bS + 5 + 7 * il
                                tb = bT + 6 + 23 * il
                                hh = 4 * j + il
                                te.wait_ge(eV.h, qb + 3)
                                eT.inc(te.matmul(pSCR[:, 0:128], kth[:],
                                                 qth[:], start=True, stop=False))
                                eT.inc(te.matmul(pSCR[:, 0:128], kth[:],
                                                 qtl[:], start=False, stop=False))
                                eT.inc(te.matmul(pSCR[:, 0:128], ktl[:],
                                                 qth[:], start=False, stop=True))
                                te.wait_ge(eV.h, qb + 6)
                                te.wait_ge(eS.h, sb)
                                eT.inc(te.matmul(pO[:, 0:128], vtoh[:], smh[:],
                                                 start=True, stop=False))
                                eT.inc(te.matmul(pO[:, 0:128], vtoh[:], sml[:],
                                                 start=False, stop=False))
                                eT.inc(te.matmul(pO[:, 0:128], vtol[:], smh[:],
                                                 start=False, stop=False))
                                eT.inc(te.matmul(pO[:, 0:128], s0h[:], qth[:],
                                                 start=False, stop=False))
                                eT.inc(te.matmul(pO[:, 0:128], s0h[:], qtl[:],
                                                 start=False, stop=False))
                                eT.inc(te.matmul(pO[:, 0:128], s0l[:], qth[:],
                                                 start=False, stop=True))
                                for b in range(2):
                                    qcol = 128 * hh + 64 * b
                                    kwin = (64 + 64 * b, 64 + 64 * b + 128)
                                    kwh = krh[pc][:, 256 * j + kwin[0]:
                                                  256 * j + kwin[1]]
                                    kwl = krl[pc][:, 256 * j + kwin[0]:
                                                  256 * j + kwin[1]]
                                    qbh = qrh[pc][:, qcol:qcol + 64]
                                    qbl = qrl[pc][:, qcol:qcol + 64]
                                    if b == 1:
                                        te.wait_ge(eV.h, qb + 7)
                                    eT.inc(te.matmul(pW[0:64, 0:128], qbh, kwh,
                                                     start=True, stop=False))
                                    eT.inc(te.matmul(pW[0:64, 0:128], qbh, kwl,
                                                     start=False, stop=False))
                                    eT.inc(te.matmul(pW[0:64, 0:128], qbl, kwh,
                                                     start=False, stop=True))
                                    te.wait_ge(eV.h, qb + 9 + 5 * b)
                                    pT = pTrA if b == 0 else pTrB
                                    eT.inc(te.transpose(pT[:, 128:192], swn[:],
                                                        idn32.ap()[0:64, 0:64]))
                                    te.wait_ge(eV.h, qb + 11 + 5 * b)
                                    pY = pY0 if b == 0 else pY1
                                    vh_ = vteh if b == 0 else vtoh
                                    vl_ = vtel if b == 0 else vtol
                                    eT.inc(te.matmul(pY[:, 0:64], vh_[:], pth[:],
                                                     start=True, stop=False))
                                    eT.inc(te.matmul(pY[:, 0:64], vh_[:], ptl[:],
                                                     start=False, stop=False))
                                    eT.inc(te.matmul(pY[:, 0:64], vl_[:], pth[:],
                                                     start=False, stop=True))

                        @blk.vector
                        def _(ve):
                            gsj = gsb[pc][:, 128 * j:128 * (j + 1)]
                            eV.inc(ve.tensor_tensor_scan(Ab[:], gsj, gsj, 0.0,
                                                         ALU.add, ALU.bypass))
                            eV.inc(ve.tensor_scalar(Ar[:], Ab[:], Ab[:, 63:64],
                                                    None, ALU.subtract))
                            eV.inc(ve.tensor_tensor(r2[:, 0:1], Ab[:, 127:128],
                                                    Ab[:, 63:64], ALU.subtract))
                            ve.wait_ge(eS.h, bS + 2)
                            eV.inc(ve.tensor_tensor(ktb[:],
                                                    ksb[pc][:, 128 * j:128 * (j + 1)],
                                                    Pi[:], ALU.mult))
                            eV.inc(ve.tensor_copy(kth[:], ktb[:]))
                            eV.inc(ve.tensor_tensor(ktl[:], ktb[:], kth[:],
                                                    ALU.subtract))
                            ve.wait_ge(eT.h, bT + 3)
                            eV.inc(ve.tensor_copy(kTh[:], pSCR[:, 128:256]))
                            eV.inc(ve.tensor_tensor(kTl[:], pSCR[:, 128:256],
                                                    kTh[:], ALU.subtract))
                            eV.inc(ve.tensor_copy(vteh[:], pTrA[:, 0:128]))
                            eV.inc(ve.tensor_tensor(vtel[:], pTrA[:, 0:128],
                                                    vteh[:], ALU.subtract))
                            eV.inc(ve.tensor_copy(vtoh[:], pTrB[:, 128:256]))
                            eV.inc(ve.tensor_tensor(vtol[:], pTrB[:, 128:256],
                                                    vtoh[:], ALU.subtract))
                            ve.wait_ge(eS.h, bS + 3)
                            Ssrcj = Ssrc[:, 128 * j:128 * (j + 1)]
                            eV.inc(ve.tensor_scalar(s0s[:], Ssrcj, r2[:, 2:3],
                                                    None, ALU.mult))
                            eV.inc(ve.tensor_copy(s0h[:], s0s[:]))
                            eV.inc(ve.tensor_tensor(s0l[:], s0s[:], s0h[:],
                                                    ALU.subtract))
                            ve.wait_ge(eS.h, bS + 4)
                            eV.inc(ve.tensor_scalar(stmp[:], Ssrcj, r2[:, 3:4],
                                                    None, ALU.mult))
                            ve.wait_ge(eT.h, bT + 6)
                            ve.wait_ge(eS.h, bS + 5)
                            eV.inc(ve.scalar_tensor_tensor(
                                Sdst[:, 128 * j:128 * (j + 1)], pU[:, 0:128],
                                r2[:, 4:5], stmp[:], ALU.mult, ALU.add))
                            for il in range(4):
                                hh = 4 * j + il
                                qb = bV + 17 + 18 * il
                                sb = bS + 5 + 7 * il
                                tb = bT + 6 + 23 * il
                                eV.inc(ve.tensor_tensor(
                                    qtb[:], qsb[pc][:, 128 * hh:128 * (hh + 1)],
                                    Pb[:], ALU.mult))
                                eV.inc(ve.tensor_copy(qth[:], qtb[:]))
                                eV.inc(ve.tensor_tensor(qtl[:], qtb[:], qth[:],
                                                        ALU.subtract))
                                ve.wait_ge(eT.h, tb + 3)
                                eV.inc(ve.tensor_tensor(smb[:], pSCR[:, 0:128],
                                                        ltm[:], ALU.mult))
                                eV.inc(ve.tensor_copy(smh[:], smb[:]))
                                eV.inc(ve.tensor_tensor(sml[:], smb[:], smh[:],
                                                        ALU.subtract))
                                for b in range(2):
                                    mask = swam0 if (c == 0 and b == 0) else swam
                                    ve.wait_ge(eT.h, tb + 12 + 7 * b)
                                    eV.inc(ve.tensor_tensor(swb[:], pW[0:64, 0:128],
                                                            mask[:], ALU.add))
                                    ve.wait_ge(eS.h, sb + 1 + 3 * b)
                                    eV.inc(ve.reduce_sum(swsum[:], swp[:],
                                                         axis=mybir.AxisListType.X))
                                    ve.wait_ge(eS.h, sb + 3 + 3 * b)
                                    eV.inc(ve.tensor_scalar(swn[:], swp[:],
                                                            swr[:, 0:1], None,
                                                            ALU.mult))
                                    ve.wait_ge(eT.h, tb + 13 + 7 * b)
                                    pT = pTrA if b == 0 else pTrB
                                    eV.inc(ve.tensor_copy(pth[:], pT[:, 128:192]))
                                    eV.inc(ve.tensor_tensor(ptl[:], pT[:, 128:192],
                                                            pth[:], ALU.subtract))
                                if il >= 2:
                                    ve.wait_ge(s_ob[il % 2].h, obv[il % 2] + 16)
                                ve.wait_ge(eT.h, tb + 16)
                                ve.wait_ge(eS.h, sb + 3)
                                eV.inc(ve.scalar_tensor_tensor(
                                    ob[il % 2][:, 0:64], pY0[:, 0:64], 0.5,
                                    olh[:, 0:64], ALU.mult, ALU.add))
                                ve.wait_ge(eT.h, tb + 23)
                                eV.inc(ve.scalar_tensor_tensor(
                                    ob[il % 2][:, 64:128], pY1[:, 0:64], 0.5,
                                    olh[:, 64:128], ALU.mult, ALU.add))

                        @blk.scalar
                        def _(sc):
                            sc.wait_ge(eV.h, bV + 2)
                            eS.inc(sc.activation(Pb[:], Ar[:], AF.Exp))
                            eS.inc(sc.activation(Pi[:], Ar[:], AF.Exp, scale=-1.0))
                            eS.inc(sc.activation(r2[:, 2:3], Ab[:, 63:64], AF.Exp))
                            eS.inc(sc.activation(r2[:, 3:4], Ab[:, 127:128], AF.Exp))
                            sc.wait_ge(eV.h, bV + 3)
                            eS.inc(sc.activation(r2[:, 4:5], r2[:, 0:1], AF.Exp))
                            for il in range(4):
                                qb = bV + 17 + 18 * il
                                tb = bT + 6 + 23 * il
                                for b in range(2):
                                    sc.wait_ge(eV.h, qb + 7 + 5 * b)
                                    eS.inc(sc.activation(swp[:], swb[:], AF.Exp,
                                                         scale=SQDI))
                                    sc.wait_ge(eV.h, qb + 8 + 5 * b)
                                    eS.inc(sc.activation(swr[:], swsum[:], AF.Ln))
                                    eS.inc(sc.activation(swr[:], swr[:], AF.Exp,
                                                         scale=-1.0))
                                sc.wait_ge(eT.h, tb + 9)
                                eS.inc(sc.activation(olh[:], pO[:, 0:128], AF.Copy,
                                                     scale=0.5))

                        @blk.gpsimd
                        def _(gp):
                            for il in range(4):
                                hh = 4 * j + il
                                qb = bV + 17 + 18 * il
                                if dbg and c == 1 and j == 0 and il == 3:
                                    gp.wait_ge(eV.h, qb + 15)
                                    gp.dma_start(dbg2_d[0, 0:64, 0:1], swr[:],
                                                 allow_slow_non_contiguous=True
                                                 ).then_inc(dI.h, 16)
                                    dI.n += 16
                                    gp.dma_start(dbg2_d[1, 0:64, 0:1], swsum[:],
                                                 allow_slow_non_contiguous=True
                                                 ).then_inc(dI.h, 16)
                                    dI.n += 16
                                    gp.wait_ge(eV.h, qb + 16)
                                    gp.dma_start(dbg2_d[2, 0:64, :],
                                                 swn[:]).then_inc(dI.h, 16)
                                    dI.n += 16
                                gp.wait_ge(eV.h, qb + 18)
                                gp.dma_start(o_s[hh, :, 128 * c:128 * (c + 1)],
                                             ob[il % 2][:]).then_inc(
                                    s_ob[il % 2].h, 16)
                                s_ob[il % 2].n += 16
                    barrier()
                    if dbg and c == 1 and j == 0:
                        with nc.allow_non_contiguous_dma(reason="dbg"), \
                                nc.Block() as blk:
                            @blk.sync
                            def _(sy):
                                for i_, t_ in enumerate((Ab, Pb, ktb, smb, s0s,
                                                         swb, swn, olh, qtb,
                                                         swp, swsum, swr)):
                                    dI.inc(sy.dma_start(
                                        dbg_d[i_, 0:t_.shape[0], 0:t_.shape[1]],
                                        t_[:]), 16)
                                dI.inc(sy.dma_start(
                                    dbg_d[12, :, :],
                                    Sb[(c + 1) % 2][:, 0:128]), 16)
                        barrier()
            sst.close()

        def emit_wo():
            slot_reset()
            sst = contextlib.ExitStack()
            sen = sst.enter_context
            ps2 = [sen(nc.psum_tensor(f"wo_p{i}", [128, 512], F32)) for i in range(2)]
            obuf = sen(nc.sbuf_tensor("wo_ob", [128, KT * 512], BF16))
            wbuf = [sen(nc.sbuf_tensor(f"wo_w{i}", [128, 2 * 2048], BF16))
                    for i in range(2)]
            x1b = [sen(nc.sbuf_tensor(f"wo_x{i}", [128, 512], F32)) for i in range(2)]
            xob = [sen(nc.sbuf_tensor(f"wo_o{i}", [128, 512], F32)) for i in range(2)]
            s_o = slots(1)
            s_w = slots(4)
            s_x1 = slots(2)
            s_wr = slots(2)
            for tc in range(4):
                bload = s_o.n if False else None
                with nc.Block() as blk:
                    @blk.sync
                    def _(sy):
                        src = bass.AP(o_s, 512 * tc,
                                      [[T, 128], [128 * T, KT], [1, 512]])
                        sy.dma_start(obuf[:], src).then_inc(s_o[0].h, 16)
                        s_o[0].n += 16
                barrier()
                for tau in range(32):
                    bT, bV = eT.n, eV.n
                    wv = []
                    x1v = []
                    with nc.Block() as blk:
                        @blk.sync
                        def _(sy):
                            for kh in range(2):
                                for pj in range(2):
                                    sl = s_w[2 * kh + pj]
                                    sy.dma_start(
                                        wbuf[kh][:, 2048 * pj:2048 * (pj + 1)],
                                        bass.AP(wos,
                                                (pj * 32 + tau) * 128 * D + 2048 * kh,
                                                [[D, 128], [1, 2048]])).then_inc(
                                        sl.h, 16)
                                    sl.n += 16
                                    wv.append((sl, sl.n))
                            sy.dma_start(x1b[0][:],
                                         xB[tau, :, 512 * tc:512 * (tc + 1)]
                                         ).then_inc(s_x1[0].h, 16)
                            s_x1[0].n += 16
                            x1v.append(s_x1[0].n)

                        @blk.tensor
                        def _(te):
                            for kh in range(2):
                                for sl, v in wv[2 * kh:2 * kh + 2]:
                                    te.wait_ge(sl.h, v)
                                wb = wbuf[kh]
                                for k16 in range(16):
                                    k = 16 * kh + k16
                                    whi_ = wb[:, 128 * k16:128 * (k16 + 1)]
                                    wlo_ = wb[:, 2048 + 128 * k16:
                                              2048 + 128 * (k16 + 1)]
                                    rhs = obuf[:, 512 * k:512 * (k + 1)]
                                    eT.inc(te.matmul(ps2[0][:], whi_, rhs,
                                                     start=(k == 0), stop=False))
                                    eT.inc(te.matmul(ps2[0][:], wlo_, rhs,
                                                     start=False,
                                                     stop=(k == KT - 1)))

                        @blk.vector
                        def _(ve):
                            ve.wait_ge(eT.h, bT + 64)
                            ve.wait_ge(s_x1[0].h, x1v[0])
                            eV.inc(ve.tensor_tensor(xob[0][:], ps2[0][:],
                                                    x1b[0][:], ALU.add))

                        @blk.scalar
                        def _(sc):
                            sc.wait_ge(eV.h, bV + 1)
                            sc.dma_start(xC[tau, :, 512 * tc:512 * (tc + 1)],
                                         xob[0][:]).then_inc(s_wr[0].h, 16)
                            s_wr[0].n += 16
                    barrier()
            sst.close()

        if upto in ("mixer", "attn", "full"):
            emit_mixer()
        if upto in ("attn", "full"):
            emit_wo()

        x_final = {"entry": xA, "dequant": xA, "ffn1": xB, "proj": xB,
                   "mixer": xB, "attn": xC, "full": xA}[upto]
        if upto == "full":
            emit_ffn(xC, xA, "2")

        dO = Sem(nc, "s_dO")

        def emit_exit(src_dram, dst_dram, tag):
            slot_reset()
            sst = contextlib.ExitStack()
            sen = sst.enter_context
            tb = [sen(nc.sbuf_tensor(f"ex_{tag}_t{i}", [128, 128], F32)) for i in range(4)]
            ob = [sen(nc.sbuf_tensor(f"ex_{tag}_o{i}", [128, D], F32)) for i in range(2)]
            xpp = [sen(nc.psum_tensor(f"ex_{tag}_p{i}", [128, 512], F32)) for i in range(4)]
            s_t = slots(4)
            b_eT, b_eV, b_dO = eT.n, eV.n, dO.n
            tv = [s_t[i % 4].n + 16 * (i // 4 + 1) for i in range(512)]

            with nc.Block() as blk:
                @blk.sync
                def _(sy):
                    for tt in range(16):
                        for k in range(KT):
                            idx = tt * KT + k
                            if idx >= 4:
                                sy.wait_ge(eT.h, b_eT + idx - 3)
                            sy.dma_start(tb[idx % 4][:],
                                         src_dram[k, :, 128 * tt:128 * (tt + 1)]
                                         ).then_inc(s_t[idx % 4].h, 16)
                            s_t[idx % 4].n += 16

                @blk.tensor
                def _(te):
                    for tt in range(16):
                        for k in range(KT):
                            idx = tt * KT + k
                            te.wait_ge(s_t[idx % 4].h, tv[idx])
                            if idx >= 4:
                                te.wait_ge(eV.h, b_eV + idx - 3)
                            eT.inc(te.transpose(xpp[idx % 4][:, 0:128], tb[idx % 4][:],
                                                idn32[:]))

                @blk.vector
                def _(ve):
                    for tt in range(16):
                        if tt >= 2:
                            ve.wait_ge(dO.h, b_dO + 16 * (tt - 1))
                        for k in range(KT):
                            idx = tt * KT + k
                            ve.wait_ge(eT.h, b_eT + idx + 1)
                            eV.inc(ve.tensor_copy(ob[tt % 2][:, 128 * k:128 * (k + 1)],
                                                  xpp[idx % 4][:, 0:128]))

                @blk.scalar
                def _(sc):
                    for tt in range(16):
                        sc.wait_ge(eV.h, b_eV + 32 * (tt + 1))
                        dO.inc(sc.dma_start(dst_dram[128 * tt:128 * (tt + 1), :],
                                            ob[tt % 2][:]), 16)
            sst.close()

        for name in taps:
            src = {"x0": xA, "x1": xB, "x2": xC}[name]
            barrier()
            emit_exit(src, tap_d[name], name)

        barrier()
        emit_exit(x_final, out_d, "out")

        with nc.Block() as blk:
            @blk.sync
            def _(sy):
                sy.wait_ge(dO.h, dO.n)

        nc.compile()
    return nc


_consts = make_consts()


def _in_map(core_inputs):
    m = dict(core_inputs)
    m.update({"idn32c": _consts["idn32"], "idn16c": _consts["idn16"],
              "ones_kc": _consts["ones_k"], "ones_mc": _consts["ones_m"],
              "ropecc": _consts["ropec"], "ropesc": _consts["ropes"],
              "rotTc": _consts["rotT"], "ltmaskc": _consts["ltmask"],
              "swamc": _consts["swam"], "swam0c": _consts["swam0"],
              "sh16c": _consts["sh16"]})
    return m


def kernel(L0_attn_hidden_states, L0_attn_Wq, L0_attn_Wk, L0_attn_Wv, L0_attn_Wo,
           L0_ffn_qweight, L0_ffn_lut4, L0_ffn_lut8, L0_ffn_bias):
    x = np.ascontiguousarray(np.asarray(L0_attn_hidden_states, dtype=np.float32))
    biasr = np.ascontiguousarray(np.asarray(L0_ffn_bias, dtype=np.float32)
                                 .reshape(32, 128).T)
    shared = {"wq": np.ascontiguousarray(L0_attn_Wq),
              "wk": np.ascontiguousarray(L0_attn_Wk),
              "wv": np.ascontiguousarray(L0_attn_Wv),
              "wo": np.ascontiguousarray(L0_attn_Wo),
              "qw": np.ascontiguousarray(L0_ffn_qweight),
              "lut4": np.ascontiguousarray(L0_ffn_lut4),
              "lut8": np.ascontiguousarray(L0_ffn_lut8), "biasr": biasr}
    nc = build()
    in_maps = [_in_map({"x_in": np.ascontiguousarray(x[c]), **shared})
               for c in range(8)]
    res = run_bass_kernel_spmd(nc, in_maps, core_ids=list(range(8)))
    return np.stack([r["out"] for r in res.results], axis=0)


# version 06
# speedup vs baseline: 2.7219x; 1.0797x over previous
import contextlib
import numpy as np
import ml_dtypes
import concourse.bass as bass
import concourse.bacc as bacc
import concourse.mybir as mybir
from concourse.bass_utils import run_bass_kernel_spmd

import concourse.dve_ops as _dvo
from concourse.dve_op import DveOp as _DveOp
from concourse.dve_spec import (Spec as _Spec, select as _dsel, eq as _deq,
                                Src0 as _S0, Src1 as _S1, C0 as _C0, C1 as _C1,
                                C2 as _C2, One as _DOne, Zero as _DZero,
                                lower as _dlower)
from concourse.dve_table_gen import dve_ver_for as _dve_ver_for

F32 = mybir.dt.float32
BF16 = mybir.dt.bfloat16
I16 = mybir.dt.int16
I32 = mybir.dt.int32
AF = mybir.ActivationFunctionType
ALU = mybir.AluOpType


def _reg_dve_op(name, body, ref, row):
    if name in _dvo._SUB_OPCODE_FOR_NAME:
        return next(o for o in _dvo.OPS if o.name == name)
    spec = _Spec(body=body)
    ver = _dve_ver_for("TRN2")
    sha = _dlower(spec, ver=ver, perf_en=False, name=name).sha(ver)
    op = _DveOp(name, spec, subdim=False, uops_sha={ver: sha}, reference=ref)
    _dvo.OPS.append(op)
    _dvo.CUSTOM_DVE_SPECS[name] = spec
    _dvo.CUSTOM_DVE_REFS[name] = ref
    _dvo._SUB_OPCODE_FOR_NAME[name] = row
    assert row < 0x20
    return op


def _lut_sel2_ref(in0, in1, s0, s1, imm2):
    f = np.float64 if np.asarray(in0).dtype == np.float64 else np.float32
    return np.where(in0 == imm2, s0, np.where(in0 == imm2 + 1.0, s1, in1)).astype(f)


def _lut_seed2_ref(in0, in1, s0, s1, imm2):
    f = np.float64 if np.asarray(in0).dtype == np.float64 else np.float32
    return np.where(in0 == imm2, s0, np.where(in0 == imm2 + 1.0, s1, 0.0)).astype(f)


LUT_SEL2 = _reg_dve_op(
    "LUT_SEL2",
    _dsel(_deq(_S0, _C2), _C0, _dsel(_deq(_S0 - _C2, _DOne), _C1, _S1)),
    _lut_sel2_ref, 30)
LUT_SEED2 = _reg_dve_op(
    "LUT_SEED2",
    _dsel(_deq(_S0, _C2), _C0, _dsel(_deq(_S0 - _C2, _DOne), _C1, _DZero)),
    _lut_seed2_ref, 31)

T = 2048
D = 4096
KT = 32
NH = 32
NKV = 8
W = 64
EPS = 1e-5
NEG = -1e30
SQDI = float(1.0 / np.sqrt(128.0))


class Sem:
    def __init__(self, nc, name):
        self.h = nc.alloc_semaphore(name)
        self.n = 0

    def inc(self, inst, k=1):
        inst.then_inc(self.h, k)
        self.n += k
        return self.n


def make_consts():
    c = {}
    c["idn32"] = np.eye(128, dtype=np.float32)
    c["idn16"] = np.eye(128, dtype=ml_dtypes.bfloat16)
    c["ones_k"] = np.ones((128, 1), dtype=np.float32)
    c["ones_m"] = np.ones((1, 128), dtype=np.float32)
    half = np.arange(64, dtype=np.float64)
    inv = 1.0 / (10000.0 ** (2.0 * half / 128.0))
    pos = np.arange(T, dtype=np.float64)
    fr = pos[None, :] * inv[:, None]
    emb = np.concatenate([fr, fr], axis=0)
    c["ropec"] = np.cos(emb).astype(np.float32)
    c["ropes"] = np.sin(emb).astype(np.float32)
    R = np.zeros((128, 128), dtype=np.float32)
    for m in range(64):
        R[m + 64, m] = -1.0
        R[m, m + 64] = 1.0
    c["rotT"] = R
    c["ltmask"] = (np.arange(128)[:, None] <= np.arange(128)[None, :]).astype(np.float32)
    a = np.arange(W)[:, None]
    cc = np.arange(2 * W)[None, :]
    rel = a + W - cc
    base = (rel >= 0) & (rel <= W)
    mb = np.where(base, 0.0, NEG).astype(np.float32)
    c["swam"] = mb
    mb0 = mb.copy()
    mb0[:, :W] = NEG
    c["swam0"] = mb0
    c["sh16"] = np.tile(np.arange(16, dtype=np.int16), (128, 1))
    return c


def build(taps=(), upto="full", dbg=False):
    nc = bacc.Bacc("TRN2", target_bir_lowering=False)
    dk = {"kind": "ExternalOutput"} if dbg else {}

    x_in = nc.dram_tensor("x_in", [T, D], F32, kind="ExternalInput")
    wq_d = nc.dram_tensor("wq", [D, D], F32, kind="ExternalInput")
    wk_d = nc.dram_tensor("wk", [D, 1024], F32, kind="ExternalInput")
    wv_d = nc.dram_tensor("wv", [D, 1024], F32, kind="ExternalInput")
    wo_d = nc.dram_tensor("wo", [D, D], F32, kind="ExternalInput")
    qw_d = nc.dram_tensor("qw", [8, D, 128], I32, kind="ExternalInput")
    lut4_d = nc.dram_tensor("lut4", [D, 16], F32, kind="ExternalInput")
    lut8_d = nc.dram_tensor("lut8", [D, 256], F32, kind="ExternalInput")
    bias_d = nc.dram_tensor("biasr", [128, 32], F32, kind="ExternalInput")
    idn32_d = nc.dram_tensor("idn32c", [128, 128], F32, kind="ExternalInput")
    idn16_d = nc.dram_tensor("idn16c", [128, 128], BF16, kind="ExternalInput")
    onesk_d = nc.dram_tensor("ones_kc", [128, 1], F32, kind="ExternalInput")
    onesm_d = nc.dram_tensor("ones_mc", [1, 128], F32, kind="ExternalInput")
    ropec_d = nc.dram_tensor("ropecc", [128, T], F32, kind="ExternalInput")
    ropes_d = nc.dram_tensor("ropesc", [128, T], F32, kind="ExternalInput")
    rotT_d = nc.dram_tensor("rotTc", [128, 128], F32, kind="ExternalInput")
    ltm_d = nc.dram_tensor("ltmaskc", [128, 128], F32, kind="ExternalInput")
    swam_d = nc.dram_tensor("swamc", [W, 2 * W], F32, kind="ExternalInput")
    swam0_d = nc.dram_tensor("swam0c", [W, 2 * W], F32, kind="ExternalInput")
    sh16_d = nc.dram_tensor("sh16c", [128, 16], I16, kind="ExternalInput")

    out_d = nc.dram_tensor("out", [T, D], F32, kind="ExternalOutput")
    tap_d = {name: nc.dram_tensor(f"tap_{name}", [T, D], F32, kind="ExternalOutput")
             for name in taps}

    xA = nc.dram_tensor("xA", [KT, 128, T], F32)
    xB = nc.dram_tensor("xB", [KT, 128, T], F32)
    xC = nc.dram_tensor("xC", [KT, 128, T], F32)
    wf = nc.dram_tensor("wf", [4, 32, 128, D], BF16, **dk)
    wqs = nc.dram_tensor("wqs", [2, 32, 128, D], BF16)
    wks = nc.dram_tensor("wks", [2, 8, 128, D], BF16)
    wvs = nc.dram_tensor("wvs", [2, 8, 128, D], BF16)
    wos = nc.dram_tensor("wos", [2, 32, 128, D], BF16)
    qs_s = nc.dram_tensor("qs_s", [NH, 128, T], F32, **dk)
    qr_s = nc.dram_tensor("qr_s", [2, NH, 128, T], BF16, **dk)
    ks_s = nc.dram_tensor("ks_s", [NKV, 128, T], F32, **dk)
    kr_s = nc.dram_tensor("kr_s", [2, NKV, 128, T], BF16, **dk)
    v_s = nc.dram_tensor("v_s", [NKV, 128, T], F32, **dk)
    g_s = nc.dram_tensor("g_s", [NKV, 128, T], F32, **dk)
    o_s = nc.dram_tensor("o_s", [NH, 128, T], BF16, **dk)
    dbg_d = nc.dram_tensor("dbg_d", [13, 128, 128], F32,
                           kind="ExternalOutput") if dbg else None
    dbg2_d = nc.dram_tensor("dbg2_d", [3, 64, 128], F32,
                            kind="ExternalOutput") if dbg else None

    st = contextlib.ExitStack()
    en = st.enter_context
    with st:
        idn32 = en(nc.sbuf_tensor("c_idn32", [128, 128], F32))
        idn16 = en(nc.sbuf_tensor("c_idn16", [128, 128], BF16))
        onesk = en(nc.sbuf_tensor("c_onesk", [128, 1], F32))
        onesm = en(nc.sbuf_tensor("c_onesm", [1, 128], F32))
        rotT = en(nc.sbuf_tensor("c_rotT", [128, 128], F32))
        ltm = en(nc.sbuf_tensor("c_ltm", [128, 128], F32))
        swam = en(nc.sbuf_tensor("c_swam", [W, 2 * W], F32))
        swam0 = en(nc.sbuf_tensor("c_swam0", [W, 2 * W], F32))
        sh16 = en(nc.sbuf_tensor("c_sh16", [128, 16], I16))
        biasr = en(nc.sbuf_tensor("c_biasr", [128, 32], F32))
        hbias = en(nc.sbuf_tensor("c_hbias", [128, 32], F32))

        eT = Sem(nc, "s_eT")
        eV = Sem(nc, "s_eV")
        eS = Sem(nc, "s_eS")
        eG = Sem(nc, "s_eG")
        SL = [Sem(nc, f"sl{i}") for i in range(56)]
        _next_sl = [0]

        def slots(n):
            i = _next_sl[0]
            _next_sl[0] += n
            assert _next_sl[0] <= len(SL), "out of slot semaphores"
            return SL[i:i + n]

        def slot_reset():
            _next_sl[0] = 0

        dI = Sem(nc, "s_dI")

        with nc.Block() as blk:
            @blk.sync
            def _(sy):
                for dst, src in ((idn32, idn32_d), (idn16, idn16_d), (onesk, onesk_d),
                                 (onesm, onesm_d),
                                 (rotT, rotT_d), (ltm, ltm_d), (swam, swam_d),
                                 (swam0, swam0_d), (sh16, sh16_d), (biasr, bias_d)):
                    dI.inc(sy.dma_start(dst[:], src.ap()), 16)

            @blk.vector
            def _(ve):
                ve.wait_ge(dI.h, dI.n)
                eV.inc(ve.tensor_scalar(hbias[:], biasr[:], 0.5, None, ALU.mult))

        def barrier():
            marks = [(s.h, s.n) for s in (eT, eV, eS, eG, dI)] + \
                    [(s.h, s.n) for s in SL if s.n > 0]
            with nc.Block() as blk:
                for engname in ("sync", "tensor", "vector", "scalar", "gpsimd"):
                    def mk(_eng):
                        for h, v in marks:
                            _eng.wait_ge(h, v)
                    getattr(blk, engname)(mk)

        def pap(t, p0, pstride, pcount, flist):
            ps = t.ap().ap[0][0]
            return bass.AP(t, p0 * ps, [[pstride * ps, pcount]] + flist)

        with contextlib.ExitStack() as sst, nc.Block() as blk:
            sen = sst.enter_context
            ebuf = [sen(nc.sbuf_tensor(f"en_b{i}", [128, D], F32)) for i in range(2)]
            evac = [sen(nc.sbuf_tensor(f"en_e{i}", [128, 128], F32)) for i in range(4)]
            epp = [sen(nc.psum_tensor(f"en_p{i}", [128, 512], F32)) for i in range(4)]
            ld = slots(2)
            wr = slots(4)
            b_eT, b_eV = eT.n, eV.n
            ldv = [ld[tt % 2].n + 16 * (tt // 2 + 1) for tt in range(16)]
            wrv0 = [w.n for w in wr]

            @blk.sync
            def _(sy):
                sy.wait_ge(dI.h, dI.n)
                for tt in range(16):
                    if tt >= 2:
                        sy.wait_ge(eT.h, b_eT + 32 * (tt - 1))
                    ld[tt % 2].inc(sy.dma_start(ebuf[tt % 2][:],
                                                x_in[128 * tt:128 * (tt + 1), :]), 16)

            @blk.tensor
            def _(te):
                for tt in range(16):
                    te.wait_ge(ld[tt % 2].h, ldv[tt])
                    for k in range(KT):
                        idx = tt * KT + k
                        if idx >= 4:
                            te.wait_ge(eV.h, b_eV + idx - 3)
                        eT.inc(te.transpose(epp[idx % 4][:, 0:128],
                                            ebuf[tt % 2][:, 128 * k:128 * (k + 1)],
                                            idn32[:]))

            @blk.vector
            def _(ve):
                for tt in range(16):
                    for k in range(KT):
                        idx = tt * KT + k
                        ve.wait_ge(eT.h, b_eT + idx + 1)
                        if idx >= 4:
                            ve.wait_ge(wr[idx % 4].h, wrv0[idx % 4] + 16 * (idx // 4))
                        eV.inc(ve.tensor_copy(evac[idx % 4][:], epp[idx % 4][:, 0:128]))

            @blk.scalar
            def _(sc):
                for tt in range(16):
                    for k in range(KT):
                        idx = tt * KT + k
                        sc.wait_ge(eV.h, b_eV + idx + 1)
                        wr[idx % 4].inc(sc.dma_start(
                            xA[k, :, 128 * tt:128 * (tt + 1)], evac[idx % 4][:]), 16)

        barrier()
        slot_reset()

        if upto != "entry":
            with contextlib.ExitStack() as sst, nc.Block() as blk:
                sen = sst.enter_context
                lutb = [sen(nc.sbuf_tensor(f"dq_lut{i}", [128, 272], F32)) for i in range(2)]
                qwb = [sen(nc.sbuf_tensor(f"dq_qw{i}", [128, 256], I16)) for i in range(8)]
                shexp = sen(nc.sbuf_tensor("dq_shx", [128, 4096], I16))
                cA = sen(nc.sbuf_tensor("dq_cA", [128, 4096], I16))
                cB = sen(nc.sbuf_tensor("dq_cB", [128, 4096], I16))
                ci32 = sen(nc.sbuf_tensor("dq_c32", [128, 4096], I32))
                c8f = sen(nc.sbuf_tensor("dq_c8f", [128, 4096], F32))
                c4f = sen(nc.sbuf_tensor("dq_c4f", [128, 4096], F32))
                acc = [sen(nc.sbuf_tensor(f"dq_ac{i}", [128, 4096], F32)) for i in range(2)]
                wsp = [sen(nc.sbuf_tensor(f"dq_ws{i}", [128, 4096], BF16)) for i in range(4)]
                ebuf = [sen(nc.sbuf_tensor(f"dq_eb{i}", [128, 1024], BF16)) for i in range(4)]
                wps = [sen(nc.psum_tensor(f"dq_p{i}", [128, 1024], BF16)) for i in range(8)]

                s_lut = slots(2)
                s_qw = slots(8)
                s_wr = slots(4)
                b_eV, b_eT, b_eS = eV.n, eT.n, eS.n
                VPT = 175

                def vm(tau, ph):
                    return b_eV + VPT * tau + {1: 35, 2: 171, 3: VPT}[ph]

                lut_v = [[s_lut[j].n + 16 * (tau + 1) for j in range(2)] for tau in range(32)]
                qw_v = [[s_qw[p].n + 16 * (tau + 1) for p in range(8)] for tau in range(32)]
                wr_b = [s_wr[p].n for p in range(4)]

                @blk.sync
                def _(sy):
                    for tau in range(32):
                        if tau >= 2:
                            sy.wait_ge(eV.h, vm(tau - 2, 2))
                        sy.dma_start(lutb[tau % 2][:, 0:256],
                                     lut8_d[128 * tau:128 * (tau + 1), :]).then_inc(
                            s_lut[0].h, 16)
                        s_lut[0].n += 16
                        sy.dma_start(lutb[tau % 2][:, 256:272],
                                     lut4_d[128 * tau:128 * (tau + 1), :]).then_inc(
                            s_lut[1].h, 16)
                        s_lut[1].n += 16
                        if tau >= 1:
                            sy.wait_ge(eV.h, vm(tau - 1, 1))
                        for p in range(8):
                            sy.dma_start(qwb[p][:],
                                         qw_d[p, 128 * tau:128 * (tau + 1), :]
                                         .bitcast(I16)).then_inc(s_qw[p].h, 16)
                            s_qw[p].n += 16

                @blk.vector
                def _(ve):
                    for tau in range(32):
                        for p in range(8):
                            ve.wait_ge(s_qw[p].h, qw_v[tau][p])
                            qbc = bass.AP(qwb[p], 0,
                                          [list(qwb[p].ap().ap[0]), [1, 256], [0, 16]])
                            shbc = bass.AP(sh16, 0,
                                           [list(sh16.ap().ap[0]), [0, 256], [1, 16]])
                            eV.inc(ve.tensor_tensor(shexp[:], qbc, shbc,
                                                    ALU.logical_shift_right))
                            if p == 0:
                                eV.inc(ve.tensor_scalar(cB[:], shexp[:], 1,
                                                        None, ALU.bitwise_and))
                            else:
                                eV.inc(ve.tensor_scalar(shexp[:], shexp[:], 1, None,
                                                        ALU.bitwise_and))
                                eV.inc(ve.tensor_scalar(shexp[:], shexp[:], p, None,
                                                        ALU.logical_shift_left))
                                src_acc = cB[:] if p % 2 == 1 else cA[:]
                                dst_acc = cA[:] if p % 2 == 1 else cB[:]
                                eV.inc(ve.tensor_tensor(dst_acc, shexp[:], src_acc,
                                                        ALU.bitwise_or))
                        eV.inc(ve.tensor_scalar(cB[:], cA[:], 15, None,
                                                ALU.bitwise_and))
                        eV.inc(ve.tensor_copy(ci32[:], cA[:]))
                        eV.inc(ve.tensor_copy(c8f[:], ci32[:]))
                        eV.inc(ve.tensor_copy(ci32[:], cB[:]))
                        eV.inc(ve.tensor_copy(c4f[:], ci32[:]))
                        ve.wait_ge(s_lut[0].h, lut_v[tau][0])
                        ve.wait_ge(s_lut[1].h, lut_v[tau][1])
                        pk = lutb[tau % 2]
                        eV.inc(ve._custom_dve(LUT_SEED2, out=acc[0][:], in0=c8f[:],
                                              s0=pk[:, 0:1], s1=pk[:, 1:2], imm2=0.0))
                        for v in range(2, 256, 2):
                            eV.inc(ve._custom_dve(
                                LUT_SEL2, out=acc[0][:], in0=c8f[:], in1=acc[0][:],
                                s0=pk[:, v:v + 1], s1=pk[:, v + 1:v + 2],
                                imm2=float(v)))
                        eV.inc(ve._custom_dve(LUT_SEED2, out=acc[1][:], in0=c4f[:],
                                              s0=pk[:, 256:257], s1=pk[:, 257:258],
                                              imm2=0.0))
                        for v in range(2, 16, 2):
                            eV.inc(ve._custom_dve(
                                LUT_SEL2, out=acc[1][:], in0=c4f[:], in1=acc[1][:],
                                s0=pk[:, 256 + v:257 + v], s1=pk[:, 257 + v:258 + v],
                                imm2=float(v)))
                        if tau >= 1:
                            ve.wait_ge(eT.h, b_eT + 128 * tau)
                        eV.inc(ve.tensor_copy(wsp[0][:], acc[0][:]))
                        eV.inc(ve.tensor_tensor(wsp[1][:], acc[0][:], wsp[0][:],
                                                ALU.subtract))
                        eV.inc(ve.tensor_copy(wsp[2][:], acc[1][:]))
                        eV.inc(ve.tensor_tensor(wsp[3][:], acc[1][:], wsp[2][:],
                                                ALU.subtract))

                @blk.tensor
                def _(te):
                    for tau in range(32):
                        te.wait_ge(eV.h, vm(tau, 3))
                        for r in range(4):
                            for p in range(4):
                                for k8 in range(8):
                                    g = 128 * tau + 32 * r + 8 * p + k8
                                    if g >= 8:
                                        te.wait_ge(eS.h, b_eS + g - 7)
                                    k = 8 * r + k8
                                    eT.inc(te.transpose(
                                        wps[g % 8][:, 0:128],
                                        wsp[p][:, 128 * k:128 * (k + 1)],
                                        idn16[:]))

                @blk.scalar
                def _(sc):
                    for tau in range(32):
                        for r in range(4):
                            for p in range(4):
                                i = tau * 4 + r
                                if i >= 1:
                                    sc.wait_ge(s_wr[p].h, wr_b[p] + 16 * i)
                                for k8 in range(8):
                                    g = 128 * tau + 32 * r + 8 * p + k8
                                    sc.wait_ge(eT.h, b_eT + g + 1)
                                    eS.inc(sc.activation(
                                        ebuf[p][:, 128 * k8:128 * (k8 + 1)],
                                        wps[g % 8][:, 0:128], AF.Copy))

                @blk.gpsimd
                def _(gp):
                    for tau in range(32):
                        for r in range(4):
                            for p in range(4):
                                gp.wait_ge(eS.h,
                                           b_eS + 128 * tau + 32 * r + 8 * (p + 1))
                                gp.dma_start(wf[p, tau, :, 1024 * r:1024 * (r + 1)],
                                             ebuf[p][:]).then_inc(s_wr[p].h, 16)
                                s_wr[p].n += 16

            barrier()

        def emit_ffn(xsrc, xdst, tag):
            slot_reset()
            sst = contextlib.ExitStack()
            sen = sst.enter_context
            ps = [sen(nc.psum_tensor(f"f{tag}_p{i}", [128, 512], F32)) for i in range(8)]
            xb = [sen(nc.sbuf_tensor(f"f{tag}_x{i}", [128, 512], F32)) for i in range(4)]
            xsq = [sen(nc.sbuf_tensor(f"f{tag}_q{i}", [128, 512], F32)) for i in range(2)]
            rows = sen(nc.sbuf_tensor(f"f{tag}_rw", [1, 512 * 8], F32))
            lnf = [sen(nc.sbuf_tensor(f"f{tag}_lf{i}", [128, 512], F32)) for i in range(2)]
            lnhi = sen(nc.sbuf_tensor(f"f{tag}_lh", [128, KT * 1024], BF16))
            lnlo = sen(nc.sbuf_tensor(f"f{tag}_ll", [128, KT * 1024], BF16))
            mbuf = [sen(nc.sbuf_tensor(f"f{tag}_mb{i}", [128, 512], I32)) for i in range(2)]
            wbuf = [sen(nc.sbuf_tensor(f"f{tag}_w{i}", [128, 2 * 2048], BF16))
                    for i in range(2)]
            ysel = [sen(nc.sbuf_tensor(f"f{tag}_ys{i}", [128, 512], F32)) for i in range(2)]
            yh = [sen(nc.sbuf_tensor(f"f{tag}_yh{i}", [128, 512], F32)) for i in range(2)]
            xr = [sen(nc.sbuf_tensor(f"f{tag}_xr{i}", [128, 512], F32)) for i in range(4)]
            xo = [sen(nc.sbuf_tensor(f"f{tag}_xo{i}", [128, 512], F32)) for i in range(2)]
            s_x = slots(4)
            s_w = slots(8)
            s_xr = slots(4)
            s_wr = slots(2)

            def ln_pass(h, tc):
                col0 = 1024 * h + 512 * tc
                b_eT, b_eV, b_eS = eT.n, eV.n, eS.n
                xv = [s_x[k % 4].n + 16 * (k // 4 + 1) for k in range(32)]

                with nc.Block() as blk:
                    @blk.sync
                    def _(sy):
                        for k in range(KT):
                            if k >= 4:
                                sy.wait_ge(eT.h, b_eT + 2 * (k - 3))
                            sy.dma_start(xb[k % 4][:],
                                         xsrc[k, :, col0:col0 + 512]).then_inc(
                                s_x[k % 4].h, 16)
                            s_x[k % 4].n += 16

                    @blk.scalar
                    def _(sc):
                        for k in range(KT):
                            sc.wait_ge(s_x[k % 4].h, xv[k])
                            if k >= 2:
                                sc.wait_ge(eT.h, b_eT + 2 * (k - 1))
                            eS.inc(sc.activation(xsq[k % 2][:], xb[k % 4][:], AF.Square))
                        sc.wait_ge(eV.h, b_eV + 5)
                        eS.inc(sc.activation(rows[:, 1536:2048], rows[:, 2560:3072],
                                             AF.Ln))
                        eS.inc(sc.activation(rows[:, 1536:2048], rows[:, 1536:2048],
                                             AF.Exp, scale=-0.5))

                    @blk.tensor
                    def _(te):
                        for k in range(KT):
                            te.wait_ge(s_x[k % 4].h, xv[k])
                            te.wait_ge(eS.h, b_eS + k + 1)
                            eT.inc(te.matmul(ps[4][0:1, :], onesk[:], xb[k % 4][:],
                                             start=(k == 0), stop=(k == KT - 1)))
                            eT.inc(te.matmul(ps[5][0:1, :], onesk[:], xsq[k % 2][:],
                                             start=(k == 0), stop=(k == KT - 1)))
                        te.wait_ge(eS.h, b_eS + 34)
                        te.wait_ge(eV.h, b_eV + 8)
                        eT.inc(te.matmul(ps[4][:], onesm[:], rows[0:1, 0:512],
                                         start=True, stop=True))
                        eT.inc(te.matmul(ps[5][:], onesm[:], rows[0:1, 1536:2048],
                                         start=True, stop=True))
                        eT.inc(te.matmul(ps[6][:], onesm[:], rows[0:1, 3584:4096],
                                         start=True, stop=True))

                    @blk.vector
                    def _(ve):
                        ve.wait_ge(eT.h, b_eT + 64)
                        eV.inc(ve.tensor_scalar(rows[:, 0:512], ps[4][0:1, :],
                                                1.0 / D, None, ALU.mult))
                        eV.inc(ve.tensor_scalar(rows[:, 512:1024], ps[5][0:1, :],
                                                1.0 / D, None, ALU.mult))
                        eV.inc(ve.tensor_tensor(rows[:, 2048:2560], rows[:, 0:512],
                                                rows[:, 0:512], ALU.mult))
                        eV.inc(ve.tensor_tensor(rows[:, 1024:1536], rows[:, 512:1024],
                                                rows[:, 2048:2560], ALU.subtract))
                        eV.inc(ve.tensor_scalar(rows[:, 2560:3072], rows[:, 1024:1536],
                                                EPS, None, ALU.add))
                        eV.inc(ve.reciprocal(rows[:, 3072:3584], rows[:, 2560:3072]))
                        eV.inc(ve.tensor_tensor(rows[:, 2048:2560], rows[:, 1024:1536],
                                                rows[:, 3072:3584], ALU.mult))
                        eV.inc(ve.tensor_scalar(rows[:, 3584:4096], rows[:, 2048:2560],
                                                1.0, None, ALU.is_gt))
                        ve.wait_ge(eT.h, b_eT + 67)
                        eV.inc(ve.tensor_copy(mbuf[tc][:], ps[6][:]))

                b2_eT, b2_eV, b2_eS = eT.n, eV.n, eS.n
                x2v = [s_x[k % 4].n + 16 * (k // 4 + 1) for k in range(32)]

                with nc.Block() as blk:
                    @blk.sync
                    def _(sy):
                        for k in range(KT):
                            if k >= 4:
                                sy.wait_ge(eV.h, b2_eV + 3 * (k - 3))
                            sy.dma_start(xb[k % 4][:],
                                         xsrc[k, :, col0:col0 + 512]).then_inc(
                                s_x[k % 4].h, 16)
                            s_x[k % 4].n += 16

                    @blk.vector
                    def _(ve):
                        for k in range(KT):
                            ve.wait_ge(s_x[k % 4].h, x2v[k])
                            if k >= 2:
                                ve.wait_ge(eS.h, b2_eS + k - 1)
                            eV.inc(ve.tensor_tensor(lnf[k % 2][:], xb[k % 4][:],
                                                    ps[4][:], ALU.subtract))
                            eV.inc(ve.tensor_tensor(lnf[k % 2][:], lnf[k % 2][:],
                                                    ps[5][:], ALU.mult))
                            ve.wait_ge(eS.h, b2_eS + k + 1)
                            col = 1024 * k + 512 * tc
                            eV.inc(ve.tensor_tensor(lnlo[:, col:col + 512], lnf[k % 2][:],
                                                    lnhi[:, col:col + 512], ALU.subtract))

                    @blk.scalar
                    def _(sc):
                        for k in range(KT):
                            sc.wait_ge(eV.h, b2_eV + 3 * k + 2)
                            col = 1024 * k + 512 * tc
                            eS.inc(sc.activation(lnhi[:, col:col + 512], lnf[k % 2][:],
                                                 AF.Copy))

            def sweep(h):
                b_eT, b_eV, b_eS = eT.n, eV.n, eS.n
                w_v = {}
                cnt = [0] * 8
                for tau in range(32):
                    for path in range(2):
                        for kh in range(2):
                            gi = (tau * 2 + path) * 2 + kh
                            for pj in range(2):
                                sl = 4 * (gi % 2) + pj
                                cnt[sl] += 1
                                w_v[(tau, path, kh, pj)] = (sl, s_w[sl].n + 16 * cnt[sl])
                xr_v = [s_xr[i % 4].n + 16 * (i // 4 + 1) for i in range(64)]
                wr_vv = [s_wr[i % 2].n + 16 * (i // 2 + 1) for i in range(64)]

                with nc.Block() as blk:
                    @blk.sync
                    def _(sy):
                        for tau in range(32):
                            for path in range(2):
                                for kh in range(2):
                                    gi = (tau * 2 + path) * 2 + kh
                                    for pj in range(2):
                                        sl, _v = w_v[(tau, path, kh, pj)]
                                        if gi >= 2:
                                            sy.wait_ge(eT.h, b_eT + 96 * (gi - 1))
                                        plane = path * 2 + pj
                                        sy.dma_start(
                                            wbuf[gi % 2][:, 2048 * pj:2048 * (pj + 1)],
                                            bass.AP(wf,
                                                    (plane * 32 + tau) * 128 * D + 2048 * kh,
                                                    [[D, 128], [1, 2048]])).then_inc(
                                            s_w[sl].h, 16)
                                        s_w[sl].n += 16


                    @blk.tensor
                    def _(te):
                        for tau in range(32):
                            for path in range(2):
                                for kh in range(2):
                                    gi = (tau * 2 + path) * 2 + kh
                                    for pj in range(2):
                                        sl, v = w_v[(tau, path, kh, pj)]
                                        te.wait_ge(s_w[sl].h, v)
                                    if tau >= 2 and path == 0 and kh == 0:
                                        te.wait_ge(eV.h,
                                                   b_eV + 4 * (2 * (tau - 2) + 2))
                                    wb = wbuf[gi % 2]
                                    for k16 in range(16):
                                        k = 16 * kh + k16
                                        colh = 1024 * k
                                        whi = wb[:, 128 * k16:128 * (k16 + 1)]
                                        wlo = wb[:, 2048 + 128 * k16:
                                                 2048 + 128 * (k16 + 1)]
                                        for tc in range(2):
                                            col = colh + 512 * tc
                                            bank = ps[4 * (tau % 2) + path * 2 + tc]
                                            first = (k == 0)
                                            eT.inc(te.matmul(
                                                bank[:], whi,
                                                lnhi[:, col:col + 512],
                                                start=first, stop=False))
                                            eT.inc(te.matmul(
                                                bank[:], whi,
                                                lnlo[:, col:col + 512],
                                                start=False, stop=False))
                                            eT.inc(te.matmul(
                                                bank[:], wlo,
                                                lnhi[:, col:col + 512],
                                                start=False,
                                                stop=(k == KT - 1)))

                    @blk.vector
                    def _(ve):
                        for tau in range(32):
                            for tc in range(2):
                                ti = 2 * tau + tc
                                ve.wait_ge(eT.h, b_eT + 384 * (tau + 1))
                                ve.wait_ge(s_xr[ti % 4].h, xr_v[ti])
                                if ti >= 2:
                                    ve.wait_ge(s_wr[ti % 2].h, wr_vv[ti - 2])
                                pbank_l = ps[4 * (tau % 2) + 2 + tc]
                                pbank_h = ps[4 * (tau % 2) + tc]
                                eV.inc(ve.tensor_scalar(ysel[ti % 2][:], pbank_l[:],
                                                        0.5, hbias[:, tau:tau + 1],
                                                        ALU.mult, ALU.add))
                                eV.inc(ve.tensor_scalar(yh[ti % 2][:], pbank_h[:],
                                                        0.5, hbias[:, tau:tau + 1],
                                                        ALU.mult, ALU.add))
                                eV.inc(ve.copy_predicated(ysel[ti % 2][:], mbuf[tc][:],
                                                          yh[ti % 2][:]))
                                eV.inc(ve.tensor_tensor(xo[ti % 2][:], ysel[ti % 2][:],
                                                        xr[ti % 4][:], ALU.add))

                    @blk.gpsimd
                    def _(gp):
                        for tau in range(32):
                            for tc in range(2):
                                ti = 2 * tau + tc
                                col0 = 1024 * h + 512 * tc
                                if ti >= 4:
                                    gp.wait_ge(eV.h, b_eV + 4 * (ti - 3))
                                gp.dma_start(xr[ti % 4][:],
                                             xsrc[tau, :, col0:col0 + 512]).then_inc(
                                    s_xr[ti % 4].h, 16)
                                s_xr[ti % 4].n += 16
                                gp.wait_ge(eV.h, b_eV + 4 * (ti + 1))
                                gp.dma_start(xdst[tau, :, col0:col0 + 512],
                                             xo[ti % 2][:]).then_inc(s_wr[ti % 2].h, 16)
                                s_wr[ti % 2].n += 16

            for h in range(2):
                ln_pass(h, 0)
                barrier()
                ln_pass(h, 1)
                barrier()
                sweep(h)
                barrier()
            sst.close()

        def emit_wsplit():
            slot_reset()
            sst = contextlib.ExitStack()
            sen = sst.enter_context
            wt = [sen(nc.sbuf_tensor(f"ws_t{i}", [128, D], F32)) for i in range(2)]
            whi = [sen(nc.sbuf_tensor(f"ws_h{i}", [128, D], BF16)) for i in range(2)]
            wlo = [sen(nc.sbuf_tensor(f"ws_l{i}", [128, D], BF16)) for i in range(2)]
            s_ld = slots(2)
            s_wr = slots(4)
            jobs = []
            for tau in range(32):
                jobs.append((wq_d, wqs, tau))
            for tau in range(8):
                jobs.append((wk_d, wks, tau))
            for tau in range(8):
                jobs.append((wv_d, wvs, tau))
            for tau in range(32):
                jobs.append((wo_d, wos, tau))
            NJ = len(jobs)
            b_eV, b_eS = eV.n, eS.n
            ld_v = [s_ld[i % 2].n + 16 * (i // 2 + 1) for i in range(NJ)]
            wr_v = [[s_wr[2 * (i % 2) + j].n + 16 * (i // 2 + 1) for j in range(2)]
                    for i in range(NJ)]

            with nc.Block() as blk:
                @blk.sync
                def _(sy):
                    for i, (srcd, dstd, tau) in enumerate(jobs):
                        if i >= 2:
                            sy.wait_ge(eV.h, b_eV + (i - 1))
                        ncol = srcd.shape[1]
                        src = bass.AP(srcd, 128 * tau,
                                      [[ncol, 128], [128 * ncol, 32], [1, 128]])
                        sy.dma_start(wt[i % 2][:], src).then_inc(s_ld[i % 2].h, 16)
                        s_ld[i % 2].n += 16

                @blk.scalar
                def _(sc):
                    for i, (srcd, dstd, tau) in enumerate(jobs):
                        sc.wait_ge(s_ld[i % 2].h, ld_v[i])
                        if i >= 2:
                            sc.wait_ge(s_wr[2 * (i % 2)].h, wr_v[i - 2][0])
                        eS.inc(sc.activation(whi[i % 2][:], wt[i % 2][:], AF.Copy))
                        sc.wait_ge(eV.h, b_eV + i + 1)
                        sc.dma_start(dstd[0, tau, :, :], whi[i % 2][:]).then_inc(
                            s_wr[2 * (i % 2)].h, 16)
                        s_wr[2 * (i % 2)].n += 16
                        sc.dma_start(dstd[1, tau, :, :], wlo[i % 2][:]).then_inc(
                            s_wr[2 * (i % 2) + 1].h, 16)
                        s_wr[2 * (i % 2) + 1].n += 16

                @blk.vector
                def _(ve):
                    for i, (srcd, dstd, tau) in enumerate(jobs):
                        ve.wait_ge(eS.h, b_eS + i + 1)
                        if i >= 2:
                            ve.wait_ge(s_wr[2 * (i % 2) + 1].h, wr_v[i - 2][1])
                        eV.inc(ve.tensor_tensor(wlo[i % 2][:], wt[i % 2][:],
                                                whi[i % 2][:], ALU.subtract))
            sst.close()
            barrier()

        def emit_proj(xsrc):
            slot_reset()
            sst = contextlib.ExitStack()
            sen = sst.enter_context
            ps = [sen(nc.psum_tensor(f"pr_p{i}", [128, 512], F32)) for i in range(8)]
            xb = [sen(nc.sbuf_tensor(f"pr_x{i}", [128, 512], F32)) for i in range(4)]
            xsq = [sen(nc.sbuf_tensor(f"pr_q{i}", [128, 512], F32)) for i in range(2)]
            rows = sen(nc.sbuf_tensor("pr_rw", [1, 512 * 8], F32))
            lnf = [sen(nc.sbuf_tensor(f"pr_lf{i}", [128, 512], F32)) for i in range(2)]
            lnhi = sen(nc.sbuf_tensor("pr_lh", [128, KT * 512], BF16))
            lnlo = sen(nc.sbuf_tensor("pr_ll", [128, KT * 512], BF16))
            wbuf = [sen(nc.sbuf_tensor(f"pr_w{i}", [128, 2 * 2048], BF16))
                    for i in range(2)]
            cosb = sen(nc.sbuf_tensor("pr_cos", [128, 512], F32))
            sinb = sen(nc.sbuf_tensor("pr_sin", [128, 512], F32))
            qraw = [sen(nc.sbuf_tensor(f"pr_qr{i}", [128, 512], F32)) for i in range(2)]
            t1b = [sen(nc.sbuf_tensor(f"pr_t1{i}", [128, 512], F32)) for i in range(2)]
            t2b = [sen(nc.sbuf_tensor(f"pr_t2{i}", [128, 512], F32)) for i in range(2)]
            rpf = [sen(nc.sbuf_tensor(f"pr_rp{i}", [128, 512], F32)) for i in range(2)]
            rphi = [sen(nc.sbuf_tensor(f"pr_rh{i}", [128, 512], BF16)) for i in range(2)]
            rplo = [sen(nc.sbuf_tensor(f"pr_rl{i}", [128, 512], BF16)) for i in range(2)]
            eb = [sen(nc.sbuf_tensor(f"pr_e{i}", [128, 512], F32)) for i in range(2)]
            qsf = [sen(nc.sbuf_tensor(f"pr_qs{i}", [128, 512], F32)) for i in range(2)]
            ub = [sen(nc.sbuf_tensor(f"pr_u{i}", [128, 512], F32)) for i in range(2)]
            vb = [sen(nc.sbuf_tensor(f"pr_v{i}", [128, 512], F32)) for i in range(2)]
            gsf = [sen(nc.sbuf_tensor(f"pr_gs{i}", [128, 512], F32)) for i in range(2)]
            s_x = slots(4)
            s_w = slots(8)
            s_sp = slots(4)

            def ln_pass2(h, tc):
                col0 = 512 * h
                b_eT, b_eV, b_eS = eT.n, eV.n, eS.n
                xv = [s_x[k % 4].n + 16 * (k // 4 + 1) for k in range(32)]

                with nc.Block() as blk:
                    @blk.sync
                    def _(sy):
                        for k in range(KT):
                            if k >= 4:
                                sy.wait_ge(eT.h, b_eT + 2 * (k - 3))
                            sy.dma_start(xb[k % 4][:],
                                         xsrc[k, :, col0:col0 + 512]).then_inc(
                                s_x[k % 4].h, 16)
                            s_x[k % 4].n += 16

                    @blk.scalar
                    def _(sc):
                        for k in range(KT):
                            sc.wait_ge(s_x[k % 4].h, xv[k])
                            if k >= 2:
                                sc.wait_ge(eT.h, b_eT + 2 * (k - 1))
                            eS.inc(sc.activation(xsq[k % 2][:], xb[k % 4][:], AF.Square))
                        sc.wait_ge(eV.h, b_eV + 5)
                        eS.inc(sc.activation(rows[:, 1536:2048], rows[:, 2560:3072],
                                             AF.Ln))
                        eS.inc(sc.activation(rows[:, 1536:2048], rows[:, 1536:2048],
                                             AF.Exp, scale=-0.5))

                    @blk.tensor
                    def _(te):
                        for k in range(KT):
                            te.wait_ge(s_x[k % 4].h, xv[k])
                            te.wait_ge(eS.h, b_eS + k + 1)
                            eT.inc(te.matmul(ps[4][0:1, :], onesk[:], xb[k % 4][:],
                                             start=(k == 0), stop=(k == KT - 1)))
                            eT.inc(te.matmul(ps[5][0:1, :], onesk[:], xsq[k % 2][:],
                                             start=(k == 0), stop=(k == KT - 1)))
                        te.wait_ge(eS.h, b_eS + 34)
                        te.wait_ge(eV.h, b_eV + 5)
                        eT.inc(te.matmul(ps[4][:], onesm[:], rows[0:1, 0:512],
                                         start=True, stop=True))
                        eT.inc(te.matmul(ps[5][:], onesm[:], rows[0:1, 1536:2048],
                                         start=True, stop=True))

                    @blk.vector
                    def _(ve):
                        ve.wait_ge(eT.h, b_eT + 64)
                        eV.inc(ve.tensor_scalar(rows[:, 0:512], ps[4][0:1, :],
                                                1.0 / D, None, ALU.mult))
                        eV.inc(ve.tensor_scalar(rows[:, 512:1024], ps[5][0:1, :],
                                                1.0 / D, None, ALU.mult))
                        eV.inc(ve.tensor_tensor(rows[:, 2048:2560], rows[:, 0:512],
                                                rows[:, 0:512], ALU.mult))
                        eV.inc(ve.tensor_tensor(rows[:, 1024:1536], rows[:, 512:1024],
                                                rows[:, 2048:2560], ALU.subtract))
                        eV.inc(ve.tensor_scalar(rows[:, 2560:3072], rows[:, 1024:1536],
                                                EPS, None, ALU.add))

                b2_eT, b2_eV, b2_eS = eT.n, eV.n, eS.n
                x2v = [s_x[k % 4].n + 16 * (k // 4 + 1) for k in range(32)]

                with nc.Block() as blk:
                    @blk.sync
                    def _(sy):
                        for k in range(KT):
                            if k >= 4:
                                sy.wait_ge(eV.h, b2_eV + 3 * (k - 3))
                            sy.dma_start(xb[k % 4][:],
                                         xsrc[k, :, col0:col0 + 512]).then_inc(
                                s_x[k % 4].h, 16)
                            s_x[k % 4].n += 16

                    @blk.vector
                    def _(ve):
                        for k in range(KT):
                            ve.wait_ge(s_x[k % 4].h, x2v[k])
                            if k >= 2:
                                ve.wait_ge(eS.h, b2_eS + k - 1)
                            eV.inc(ve.tensor_tensor(lnf[k % 2][:], xb[k % 4][:],
                                                    ps[4][:], ALU.subtract))
                            eV.inc(ve.tensor_tensor(lnf[k % 2][:], lnf[k % 2][:],
                                                    ps[5][:], ALU.mult))
                            ve.wait_ge(eS.h, b2_eS + k + 1)
                            col = 512 * k
                            eV.inc(ve.tensor_tensor(lnlo[:, col:col + 512], lnf[k % 2][:],
                                                    lnhi[:, col:col + 512], ALU.subtract))

                    @blk.scalar
                    def _(sc):
                        for k in range(KT):
                            sc.wait_ge(eV.h, b2_eV + 3 * k + 2)
                            col = 512 * k
                            eS.inc(sc.activation(lnhi[:, col:col + 512], lnf[k % 2][:],
                                                 AF.Copy))

            def jtype(tau):
                return "q" if tau < 32 else ("k" if tau < 40 else "v")

            def jsrc(tau):
                if tau < 32:
                    return wqs, tau
                if tau < 40:
                    return wks, tau - 32
                return wvs, tau - 40

            def sweep2(h):
                for tau in range(48):
                    t = jtype(tau)
                    dstd, lt = jsrc(tau)
                    b_eT, b_eV, b_eS = eT.n, eV.n, eS.n
                    wv = []
                    with nc.Block() as blk:
                        @blk.sync
                        def _(sy):
                            for kh in range(2):
                                for pj in range(2):
                                    sl = s_w[2 * kh + pj]
                                    sy.dma_start(
                                        wbuf[kh][:, 2048 * pj:2048 * (pj + 1)],
                                        bass.AP(dstd,
                                                (pj * dstd.shape[1] + lt) * 128 * D
                                                + 2048 * kh,
                                                [[D, 128], [1, 2048]])).then_inc(
                                        sl.h, 16)
                                    sl.n += 16
                                    wv.append((sl, sl.n))

                        @blk.tensor
                        def _(te):
                            for kh in range(2):
                                for sl, v in wv[2 * kh:2 * kh + 2]:
                                    te.wait_ge(sl.h, v)
                                wb = wbuf[kh]
                                for k16 in range(16):
                                    k = 16 * kh + k16
                                    colh = 512 * k
                                    whi_ = wb[:, 128 * k16:128 * (k16 + 1)]
                                    wlo_ = wb[:, 2048 + 128 * k16:2048 + 128 * (k16 + 1)]
                                    for tc in range(1):
                                        col = colh + 512 * tc
                                        bank = ps[tc]
                                        eT.inc(te.matmul(bank[:], whi_,
                                                         lnhi[:, col:col + 512],
                                                         start=(k == 0), stop=False))
                                        eT.inc(te.matmul(bank[:], whi_,
                                                         lnlo[:, col:col + 512],
                                                         start=False, stop=False))
                                        eT.inc(te.matmul(bank[:], wlo_,
                                                         lnhi[:, col:col + 512],
                                                         start=False,
                                                         stop=(k == KT - 1)))
                            if t in ("q", "k"):
                                soff = {"q": 2, "k": 4}[t]
                                for tc in range(1):
                                    te.wait_ge(eS.h, b_eS + soff * tc + 1)
                                    eT.inc(te.matmul(ps[4 + tc][:], rotT[:],
                                                     qraw[tc][:], start=True, stop=True))
                                    te.wait_ge(eS.h, b_eS + soff * tc + 2)
                                    eT.inc(te.matmul(ps[6][0:1, 0:512], onesk[:],
                                                     eb[tc][:], start=True, stop=True))
                                    te.wait_ge(eV.h, b_eV + 7 * tc + 6)
                                    eT.inc(te.matmul(ps[7][:], onesm[:],
                                                     rows[0:1, 3584:4096],
                                                     start=True, stop=True))

                        @blk.scalar
                        def _(sc):
                            for tc in range(1):
                                bank = ps[tc]
                                sc.wait_ge(eT.h, b_eT + 96)
                                if t == "v":
                                    eS.inc(sc.activation(vb[tc][:], bank[:], AF.Copy))
                                    sc.dma_start(
                                        v_s[tau - 40, :, 512 * h:512 * h + 512],
                                        vb[tc][:]).then_inc(s_sp[tc].h, 16)
                                    s_sp[tc].n += 16
                                else:
                                    eS.inc(sc.activation(qraw[tc][:], bank[:], AF.Copy))
                                    eS.inc(sc.activation(eb[tc][:], bank[:], AF.Exp))
                                    if t == "k":
                                        eS.inc(sc.activation(ub[tc][:], bank[:], AF.Exp,
                                                             scale=-1.0))
                                        eS.inc(sc.activation(ub[tc][:], ub[tc][:],
                                                             AF.Ln, bias=1.0))

                        if t != "v":
                            @blk.vector
                            def _(ve):
                                soff = {"q": 2, "k": 4}[t]
                                for tc in range(1):
                                    col0 = 512 * h
                                    ve.wait_ge(eT.h, b_eT + 96 + 3 * tc + 1)
                                    eV.inc(ve.tensor_tensor(t1b[tc][:], qraw[tc][:],
                                                            cosb[:],
                                                            ALU.mult))
                                    eV.inc(ve.tensor_tensor(t2b[tc][:], ps[4 + tc][:],
                                                            sinb[:],
                                                            ALU.mult))
                                    eV.inc(ve.tensor_tensor(rpf[tc][:], t1b[tc][:],
                                                            t2b[tc][:], ALU.add))
                                    eV.inc(ve.tensor_copy(rphi[tc][:], rpf[tc][:]))
                                    eV.inc(ve.tensor_tensor(rplo[tc][:], rpf[tc][:],
                                                            rphi[tc][:], ALU.subtract))
                                    ve.wait_ge(eT.h, b_eT + 96 + 3 * tc + 2)
                                    eV.inc(ve.reciprocal(rows[0:1, 3584:4096],
                                                         ps[6][0:1, 0:512]))
                                    ve.wait_ge(eT.h, b_eT + 96 + 3 * tc + 3)
                                    eV.inc(ve.tensor_tensor(qsf[tc][:], eb[tc][:],
                                                            ps[7][:], ALU.mult))
                                    if t == "k":
                                        ve.wait_ge(eS.h, b_eS + soff * tc + 4)
                                        eV.inc(ve.tensor_scalar(gsf[tc][:], ub[tc][:],
                                                                -1.0 / 16.0, None,
                                                                ALU.mult))

                            @blk.gpsimd
                            def _(gp):
                                voff = {"q": 7, "k": 8}[t]
                                for tc in range(1):
                                    col0 = 512 * h
                                    gp.wait_ge(eV.h, b_eV + voff * tc + 5)
                                    if t == "q":
                                        gp.dma_start(qr_s[0, tau, :, col0:col0 + 512],
                                                     rphi[tc][:]).then_inc(s_sp[0].h, 16)
                                        s_sp[0].n += 16
                                        gp.dma_start(qr_s[1, tau, :, col0:col0 + 512],
                                                     rplo[tc][:]).then_inc(s_sp[1].h, 16)
                                        s_sp[1].n += 16
                                    else:
                                        gp.dma_start(kr_s[0, tau - 32, :, col0:col0 + 512],
                                                     rphi[tc][:]).then_inc(s_sp[0].h, 16)
                                        s_sp[0].n += 16
                                        gp.dma_start(kr_s[1, tau - 32, :, col0:col0 + 512],
                                                     rplo[tc][:]).then_inc(s_sp[1].h, 16)
                                        s_sp[1].n += 16
                                    gp.wait_ge(eV.h, b_eV + voff * tc + 7)
                                    dst = (qs_s[tau, :, col0:col0 + 512] if t == "q"
                                           else ks_s[tau - 32, :, col0:col0 + 512])
                                    gp.dma_start(dst, qsf[tc][:]).then_inc(s_sp[2].h, 16)
                                    s_sp[2].n += 16
                                    if t == "k":
                                        gp.wait_ge(eV.h, b_eV + voff * tc + 8)
                                        gp.dma_start(g_s[tau - 32, :, col0:col0 + 512],
                                                     gsf[tc][:]).then_inc(s_sp[3].h, 16)
                                        s_sp[3].n += 16
                    barrier()

            for h in range(4):
                with nc.Block() as blk:
                    @blk.sync
                    def _(sy):
                        dI.inc(sy.dma_start(
                            cosb[:], ropec_d[:, 512 * h:512 * (h + 1)]), 16)
                        dI.inc(sy.dma_start(
                            sinb[:], ropes_d[:, 512 * h:512 * (h + 1)]), 16)
                barrier()
                ln_pass2(h, 0)
                barrier()
                sweep2(h)
                barrier()
            sst.close()

        if upto not in ("entry", "dequant"):
            emit_ffn(xA, xB, "1")

        if upto in ("proj", "mixer", "attn", "full"):
            emit_wsplit()
            emit_proj(xB)

        def emit_mixer():
            slot_reset()
            sst = contextlib.ExitStack()
            sen = sst.enter_context
            qsb = [sen(nc.sbuf_tensor(f"mx_qs{i}", [128, 4096], F32)) for i in range(2)]
            qrh = [sen(nc.sbuf_tensor(f"mx_qh{i}", [128, 4096], BF16)) for i in range(2)]
            qrl = [sen(nc.sbuf_tensor(f"mx_ql{i}", [128, 4096], BF16)) for i in range(2)]
            ksb = [sen(nc.sbuf_tensor(f"mx_ks{i}", [128, 1024], F32)) for i in range(2)]
            krh = [sen(nc.sbuf_tensor(f"mx_kh{i}", [128, 2048], BF16)) for i in range(2)]
            krl = [sen(nc.sbuf_tensor(f"mx_kl{i}", [128, 2048], BF16)) for i in range(2)]
            vsb = [sen(nc.sbuf_tensor(f"mx_vs{i}", [128, 2048], F32)) for i in range(2)]
            gsb = [sen(nc.sbuf_tensor(f"mx_gs{i}", [128, 1024], F32)) for i in range(2)]
            vteh = sen(nc.sbuf_tensor("mx_vteh", [128, 128], BF16))
            vtel = sen(nc.sbuf_tensor("mx_vtel", [128, 128], BF16))
            vtoh = sen(nc.sbuf_tensor("mx_vtoh", [128, 128], BF16))
            vtol = sen(nc.sbuf_tensor("mx_vtol", [128, 128], BF16))
            Sb = [sen(nc.sbuf_tensor(f"mx_S{i}", [128, 1024], F32)) for i in range(2)]
            Ab = sen(nc.sbuf_tensor("mx_A", [128, 128], F32))
            Ar = sen(nc.sbuf_tensor("mx_Ar", [128, 128], F32))
            Pb = sen(nc.sbuf_tensor("mx_P", [128, 128], F32))
            Pi = sen(nc.sbuf_tensor("mx_Pi", [128, 128], F32))
            ktb = sen(nc.sbuf_tensor("mx_kt", [128, 128], F32))
            kth = sen(nc.sbuf_tensor("mx_kth", [128, 128], BF16))
            ktl = sen(nc.sbuf_tensor("mx_ktl", [128, 128], BF16))
            kTh = sen(nc.sbuf_tensor("mx_kTh", [128, 128], BF16))
            kTl = sen(nc.sbuf_tensor("mx_kTl", [128, 128], BF16))
            s0s = sen(nc.sbuf_tensor("mx_s0s", [128, 128], F32))
            s0h = sen(nc.sbuf_tensor("mx_s0h", [128, 128], BF16))
            s0l = sen(nc.sbuf_tensor("mx_s0l", [128, 128], BF16))
            stmp = sen(nc.sbuf_tensor("mx_stmp", [128, 128], F32))
            r2 = sen(nc.sbuf_tensor("mx_r2", [128, 8], F32))
            qtb = sen(nc.sbuf_tensor("mx_qt", [128, 128], F32))
            qth = sen(nc.sbuf_tensor("mx_qth", [128, 128], BF16))
            qtl = sen(nc.sbuf_tensor("mx_qtl", [128, 128], BF16))
            smb = sen(nc.sbuf_tensor("mx_sm", [128, 128], F32))
            smh = sen(nc.sbuf_tensor("mx_smh", [128, 128], BF16))
            sml = sen(nc.sbuf_tensor("mx_sml", [128, 128], BF16))
            swb = sen(nc.sbuf_tensor("mx_sw", [W, 128], F32))
            swp = sen(nc.sbuf_tensor("mx_swp", [W, 128], F32))
            swsum = sen(nc.sbuf_tensor("mx_sws", [W, 1], F32))
            swr = sen(nc.sbuf_tensor("mx_swr", [W, 1], F32))
            swr2 = sen(nc.sbuf_tensor("mx_sw2", [W, 1], F32))
            ajnk = sen(nc.sbuf_tensor("mx_ajk", [128, 1], F32))
            swn = sen(nc.sbuf_tensor("mx_swn", [W, 128], F32))
            pth = sen(nc.sbuf_tensor("mx_pth", [128, W], BF16))
            ptl = sen(nc.sbuf_tensor("mx_ptl", [128, W], BF16))
            olh = sen(nc.sbuf_tensor("mx_olh", [128, 128], F32))
            ob = [sen(nc.sbuf_tensor(f"mx_ob{i}", [128, 128], BF16)) for i in range(2)]
            pTrA = sen(nc.psum_tensor("mx_pA", [128, 512], F32))
            pTrB = sen(nc.psum_tensor("mx_pB", [128, 512], F32))
            pU = sen(nc.psum_tensor("mx_pU", [128, 512], F32))
            pSCR = sen(nc.psum_tensor("mx_pS", [128, 512], F32))
            pO = sen(nc.psum_tensor("mx_pO", [128, 512], F32))
            pW = sen(nc.psum_tensor("mx_pW", [128, 512], F32))
            pY0 = sen(nc.psum_tensor("mx_pY0", [128, 512], F32))
            pY1 = sen(nc.psum_tensor("mx_pY1", [128, 512], F32))
            s_ld = slots(8)
            s_ob = slots(2)

            with nc.Block() as blk:
                @blk.vector
                def _(ve):
                    eV.inc(ve.memset(Sb[0][:], 0.0))
                    eV.inc(ve.memset(vsb[0][:], 0.0))
                    eV.inc(ve.memset(krh[0][:], 0.0))
                    eV.inc(ve.memset(krl[0][:], 0.0))
            barrier()

            for c in range(16):
                pc = c % 2
                with nc.Block() as blk:
                    @blk.sync
                    def _(sy):
                        def ld(i, dst, srcd, nh, plane, c0, cw):
                            if plane is None:
                                srcap = bass.AP(srcd, c0, [[T, 128], [128 * T, nh],
                                                           [1, cw]])
                            else:
                                srcap = bass.AP(srcd, plane * nh * 128 * T + c0,
                                                [[T, 128], [128 * T, nh], [1, cw]])
                            sy.dma_start(dst, srcap).then_inc(s_ld[i].h, 16)
                            s_ld[i].n += 16
                        ld(0, qsb[pc][:], qs_s, NH, None, 128 * c, 128)
                        ld(1, qrh[pc][:], qr_s, NH, 0, 128 * c, 128)
                        ld(2, qrl[pc][:], qr_s, NH, 1, 128 * c, 128)
                        ld(3, ksb[pc][:], ks_s, NKV, None, 128 * c, 128)
                        if c == 0:
                            kdst = bass.AP(krh[pc], 128,
                                           [[2048, 128], [256, 8], [1, 128]])
                            ld(4, kdst, kr_s, NKV, 0, 0, 128)
                            kdst = bass.AP(krl[pc], 128,
                                           [[2048, 128], [256, 8], [1, 128]])
                            ld(5, kdst, kr_s, NKV, 1, 0, 128)
                        else:
                            kdst = bass.AP(krh[pc], 0,
                                           [[2048, 128], [256, 8], [1, 256]])
                            ld(4, kdst, kr_s, NKV, 0, 128 * (c - 1), 256)
                            kdst = bass.AP(krl[pc], 0,
                                           [[2048, 128], [256, 8], [1, 256]])
                            ld(5, kdst, kr_s, NKV, 1, 128 * (c - 1), 256)
                        if c == 0:
                            vdst = bass.AP(vsb[pc], 128,
                                           [[2048, 128], [256, 8], [1, 128]])
                            ld(6, vdst, v_s, NKV, None, 0, 128)
                        else:
                            vdst = bass.AP(vsb[pc], 0,
                                           [[2048, 128], [256, 8], [1, 256]])
                            ld(6, vdst, v_s, NKV, None, 128 * (c - 1), 256)
                        ld(7, gsb[pc][:], g_s, NKV, None, 128 * c, 128)
                barrier()

                for j in range(NKV):
                    Ssrc, Sdst = Sb[c % 2], Sb[(c + 1) % 2]
                    bT, bV, bS = eT.n, eV.n, eS.n
                    obv = [s_ob[i].n for i in range(2)]
                    with nc.Block() as blk:
                        @blk.tensor
                        def _(te):
                            eT.inc(te.transpose(
                                pTrA[:, 0:128],
                                vsb[pc][:, 256 * j + 64:256 * j + 192], idn32[:]))
                            eT.inc(te.transpose(
                                pTrB[:, 128:256],
                                vsb[pc][:, 256 * j + 128:256 * j + 256], idn32[:]))
                            te.wait_ge(eV.h, bV + 4)
                            eT.inc(te.transpose(pSCR[:, 128:256], ktb[:], idn32[:]))
                            te.wait_ge(eV.h, bV + 12)
                            eT.inc(te.matmul(pU[:, 0:128], kTh[:], vtoh[:],
                                             start=True, stop=False))
                            eT.inc(te.matmul(pU[:, 0:128], kTh[:], vtol[:],
                                             start=False, stop=False))
                            eT.inc(te.matmul(pU[:, 0:128], kTl[:], vtoh[:],
                                             start=False, stop=True))
                            for il in range(4):
                                qb = bV + 17 + 28 * il
                                sb = bS + 5 + 3 * il
                                tb = bT + 6 + 23 * il
                                hh = 4 * j + il
                                te.wait_ge(eV.h, qb + 3)
                                eT.inc(te.matmul(pSCR[:, 0:128], kth[:],
                                                 qth[:], start=True, stop=False))
                                eT.inc(te.matmul(pSCR[:, 0:128], kth[:],
                                                 qtl[:], start=False, stop=False))
                                eT.inc(te.matmul(pSCR[:, 0:128], ktl[:],
                                                 qth[:], start=False, stop=True))
                                te.wait_ge(eV.h, qb + 6)
                                te.wait_ge(eS.h, sb)
                                eT.inc(te.matmul(pO[:, 0:128], vtoh[:], smh[:],
                                                 start=True, stop=False))
                                eT.inc(te.matmul(pO[:, 0:128], vtoh[:], sml[:],
                                                 start=False, stop=False))
                                eT.inc(te.matmul(pO[:, 0:128], vtol[:], smh[:],
                                                 start=False, stop=False))
                                eT.inc(te.matmul(pO[:, 0:128], s0h[:], qth[:],
                                                 start=False, stop=False))
                                eT.inc(te.matmul(pO[:, 0:128], s0h[:], qtl[:],
                                                 start=False, stop=False))
                                eT.inc(te.matmul(pO[:, 0:128], s0l[:], qth[:],
                                                 start=False, stop=True))
                                for b in range(2):
                                    qcol = 128 * hh + 64 * b
                                    kwin = (64 + 64 * b, 64 + 64 * b + 128)
                                    kwh = krh[pc][:, 256 * j + kwin[0]:
                                                  256 * j + kwin[1]]
                                    kwl = krl[pc][:, 256 * j + kwin[0]:
                                                  256 * j + kwin[1]]
                                    qbh = qrh[pc][:, qcol:qcol + 64]
                                    qbl = qrl[pc][:, qcol:qcol + 64]
                                    if b == 1:
                                        te.wait_ge(eV.h, qb + 7)
                                    eT.inc(te.matmul(pW[0:64, 0:128], qbh, kwh,
                                                     start=True, stop=False))
                                    eT.inc(te.matmul(pW[0:64, 0:128], qbh, kwl,
                                                     start=False, stop=False))
                                    eT.inc(te.matmul(pW[0:64, 0:128], qbl, kwh,
                                                     start=False, stop=True))
                                    te.wait_ge(eV.h, qb + 14 + 10 * b)
                                    pT = pTrA if b == 0 else pTrB
                                    eT.inc(te.transpose(pT[:, 128:192], swn[:],
                                                        idn32.ap()[0:64, 0:64]))
                                    te.wait_ge(eV.h, qb + 16 + 10 * b)
                                    pY = pY0 if b == 0 else pY1
                                    vh_ = vteh if b == 0 else vtoh
                                    vl_ = vtel if b == 0 else vtol
                                    eT.inc(te.matmul(pY[:, 0:64], vh_[:], pth[:],
                                                     start=True, stop=False))
                                    eT.inc(te.matmul(pY[:, 0:64], vh_[:], ptl[:],
                                                     start=False, stop=False))
                                    eT.inc(te.matmul(pY[:, 0:64], vl_[:], pth[:],
                                                     start=False, stop=True))

                        @blk.vector
                        def _(ve):
                            gsj = gsb[pc][:, 128 * j:128 * (j + 1)]
                            eV.inc(ve.tensor_tensor_scan(Ab[:], gsj, gsj, 0.0,
                                                         ALU.add, ALU.bypass))
                            eV.inc(ve.tensor_scalar(Ar[:], Ab[:], Ab[:, 63:64],
                                                    None, ALU.subtract))
                            eV.inc(ve.tensor_tensor(r2[:, 0:1], Ab[:, 127:128],
                                                    Ab[:, 63:64], ALU.subtract))
                            ve.wait_ge(eS.h, bS + 2)
                            eV.inc(ve.tensor_tensor(ktb[:],
                                                    ksb[pc][:, 128 * j:128 * (j + 1)],
                                                    Pi[:], ALU.mult))
                            eV.inc(ve.tensor_copy(kth[:], ktb[:]))
                            eV.inc(ve.tensor_tensor(ktl[:], ktb[:], kth[:],
                                                    ALU.subtract))
                            ve.wait_ge(eT.h, bT + 3)
                            eV.inc(ve.tensor_copy(kTh[:], pSCR[:, 128:256]))
                            eV.inc(ve.tensor_tensor(kTl[:], pSCR[:, 128:256],
                                                    kTh[:], ALU.subtract))
                            eV.inc(ve.tensor_copy(vteh[:], pTrA[:, 0:128]))
                            eV.inc(ve.tensor_tensor(vtel[:], pTrA[:, 0:128],
                                                    vteh[:], ALU.subtract))
                            eV.inc(ve.tensor_copy(vtoh[:], pTrB[:, 128:256]))
                            eV.inc(ve.tensor_tensor(vtol[:], pTrB[:, 128:256],
                                                    vtoh[:], ALU.subtract))
                            ve.wait_ge(eS.h, bS + 3)
                            Ssrcj = Ssrc[:, 128 * j:128 * (j + 1)]
                            eV.inc(ve.tensor_scalar(s0s[:], Ssrcj, r2[:, 2:3],
                                                    None, ALU.mult))
                            eV.inc(ve.tensor_copy(s0h[:], s0s[:]))
                            eV.inc(ve.tensor_tensor(s0l[:], s0s[:], s0h[:],
                                                    ALU.subtract))
                            ve.wait_ge(eS.h, bS + 4)
                            eV.inc(ve.tensor_scalar(stmp[:], Ssrcj, r2[:, 3:4],
                                                    None, ALU.mult))
                            ve.wait_ge(eT.h, bT + 6)
                            ve.wait_ge(eS.h, bS + 5)
                            eV.inc(ve.scalar_tensor_tensor(
                                Sdst[:, 128 * j:128 * (j + 1)], pU[:, 0:128],
                                r2[:, 4:5], stmp[:], ALU.mult, ALU.add))
                            for il in range(4):
                                hh = 4 * j + il
                                qb = bV + 17 + 28 * il
                                sb = bS + 5 + 3 * il
                                tb = bT + 6 + 23 * il
                                eV.inc(ve.tensor_tensor(
                                    qtb[:], qsb[pc][:, 128 * hh:128 * (hh + 1)],
                                    Pb[:], ALU.mult))
                                eV.inc(ve.tensor_copy(qth[:], qtb[:]))
                                eV.inc(ve.tensor_tensor(qtl[:], qtb[:], qth[:],
                                                        ALU.subtract))
                                ve.wait_ge(eT.h, tb + 3)
                                eV.inc(ve.tensor_tensor(smb[:], pSCR[:, 0:128],
                                                        ltm[:], ALU.mult))
                                eV.inc(ve.tensor_copy(smh[:], smb[:]))
                                eV.inc(ve.tensor_tensor(sml[:], smb[:], smh[:],
                                                        ALU.subtract))
                                for b in range(2):
                                    mask = swam0 if (c == 0 and b == 0) else swam
                                    ve.wait_ge(eT.h, tb + 12 + 7 * b)
                                    eV.inc(ve.tensor_tensor(swb[:], pW[0:64, 0:128],
                                                            mask[:], ALU.add))
                                    ve.wait_ge(eS.h, sb + 1 + b)
                                    eV.inc(ve.reduce_sum(swsum[:], swp[:],
                                                         axis=mybir.AxisListType.X))
                                    eV.inc(ve.tensor_copy(swr2[:],
                                                          swam[0:64, 0:1]))
                                    eV.inc(ve.tensor_copy(swr2[:],
                                                          swam[0:64, 0:1]))
                                    eV.inc(ve.reciprocal(out=swr[:],
                                                         in_=swsum[:]))
                                    eV.inc(ve.tensor_copy(swr2[:],
                                                          swam[0:64, 0:1]))
                                    eV.inc(ve.tensor_copy(swr2[:],
                                                          swam[0:64, 0:1]))
                                    eV.inc(ve.tensor_scalar(swn[:], swp[:],
                                                            swr[:, 0:1], None,
                                                            ALU.mult))
                                    ve.wait_ge(eT.h, tb + 13 + 7 * b)
                                    pT = pTrA if b == 0 else pTrB
                                    eV.inc(ve.tensor_copy(pth[:], pT[:, 128:192]))
                                    eV.inc(ve.tensor_tensor(ptl[:], pT[:, 128:192],
                                                            pth[:], ALU.subtract))
                                if il >= 2:
                                    ve.wait_ge(s_ob[il % 2].h, obv[il % 2] + 16)
                                ve.wait_ge(eT.h, tb + 16)
                                ve.wait_ge(eS.h, sb + 3)
                                eV.inc(ve.scalar_tensor_tensor(
                                    ob[il % 2][:, 0:64], pY0[:, 0:64], 0.5,
                                    olh[:, 0:64], ALU.mult, ALU.add))
                                ve.wait_ge(eT.h, tb + 23)
                                eV.inc(ve.scalar_tensor_tensor(
                                    ob[il % 2][:, 64:128], pY1[:, 0:64], 0.5,
                                    olh[:, 64:128], ALU.mult, ALU.add))

                        @blk.scalar
                        def _(sc):
                            sc.wait_ge(eV.h, bV + 2)
                            eS.inc(sc.activation(Pb[:], Ar[:], AF.Exp))
                            eS.inc(sc.activation(Pi[:], Ar[:], AF.Exp, scale=-1.0))
                            eS.inc(sc.activation(r2[:, 2:3], Ab[:, 63:64], AF.Exp))
                            eS.inc(sc.activation(r2[:, 3:4], Ab[:, 127:128], AF.Exp))
                            sc.wait_ge(eV.h, bV + 3)
                            eS.inc(sc.activation(r2[:, 4:5], r2[:, 0:1], AF.Exp))
                            for il in range(4):
                                qb = bV + 17 + 28 * il
                                tb = bT + 6 + 23 * il
                                for b in range(2):
                                    sc.wait_ge(eV.h, qb + 7 + 10 * b)
                                    eS.inc(sc.activation(swp[:], swb[:], AF.Exp,
                                                         scale=SQDI))
                                sc.wait_ge(eT.h, tb + 9)
                                eS.inc(sc.activation(olh[:], pO[:, 0:128], AF.Copy,
                                                     scale=0.5))

                        @blk.gpsimd
                        def _(gp):
                            for il in range(4):
                                hh = 4 * j + il
                                qb = bV + 17 + 28 * il
                                if dbg and c == 1 and j == 0 and il == 3:
                                    gp.wait_ge(eV.h, qb + 21)
                                    gp.dma_start(dbg2_d[0, 0:64, 0:1], swr[:],
                                                 allow_slow_non_contiguous=True
                                                 ).then_inc(dI.h, 16)
                                    dI.n += 16
                                    gp.dma_start(dbg2_d[1, 0:64, 0:1], swsum[:],
                                                 allow_slow_non_contiguous=True
                                                 ).then_inc(dI.h, 16)
                                    dI.n += 16
                                    gp.wait_ge(eV.h, qb + 24)
                                    gp.dma_start(dbg2_d[2, 0:64, :],
                                                 swn[:]).then_inc(dI.h, 16)
                                    dI.n += 16
                                gp.wait_ge(eV.h, qb + 28)
                                gp.dma_start(o_s[hh, :, 128 * c:128 * (c + 1)],
                                             ob[il % 2][:]).then_inc(
                                    s_ob[il % 2].h, 16)
                                s_ob[il % 2].n += 16
                    barrier()
                    if dbg and c == 1 and j == 0:
                        with nc.allow_non_contiguous_dma(reason="dbg"), \
                                nc.Block() as blk:
                            @blk.sync
                            def _(sy):
                                for i_, t_ in enumerate((Ab, Pb, ktb, smb, s0s,
                                                         swb, swn, olh, qtb,
                                                         swp, swsum, swr)):
                                    dI.inc(sy.dma_start(
                                        dbg_d[i_, 0:t_.shape[0], 0:t_.shape[1]],
                                        t_[:]), 16)
                                dI.inc(sy.dma_start(
                                    dbg_d[12, :, :],
                                    Sb[(c + 1) % 2][:, 0:128]), 16)
                        barrier()
            sst.close()

        def emit_wo():
            slot_reset()
            sst = contextlib.ExitStack()
            sen = sst.enter_context
            ps2 = [sen(nc.psum_tensor(f"wo_p{i}", [128, 512], F32)) for i in range(2)]
            obuf = sen(nc.sbuf_tensor("wo_ob", [128, KT * 512], BF16))
            wbuf = [sen(nc.sbuf_tensor(f"wo_w{i}", [128, 2 * 2048], BF16))
                    for i in range(2)]
            x1b = [sen(nc.sbuf_tensor(f"wo_x{i}", [128, 512], F32)) for i in range(2)]
            xob = [sen(nc.sbuf_tensor(f"wo_o{i}", [128, 512], F32)) for i in range(2)]
            s_o = slots(1)
            s_w = slots(4)
            s_x1 = slots(2)
            s_wr = slots(2)
            for tc in range(4):
                bload = s_o.n if False else None
                with nc.Block() as blk:
                    @blk.sync
                    def _(sy):
                        src = bass.AP(o_s, 512 * tc,
                                      [[T, 128], [128 * T, KT], [1, 512]])
                        sy.dma_start(obuf[:], src).then_inc(s_o[0].h, 16)
                        s_o[0].n += 16
                barrier()
                for tau in range(32):
                    bT, bV = eT.n, eV.n
                    wv = []
                    x1v = []
                    with nc.Block() as blk:
                        @blk.sync
                        def _(sy):
                            for kh in range(2):
                                for pj in range(2):
                                    sl = s_w[2 * kh + pj]
                                    sy.dma_start(
                                        wbuf[kh][:, 2048 * pj:2048 * (pj + 1)],
                                        bass.AP(wos,
                                                (pj * 32 + tau) * 128 * D + 2048 * kh,
                                                [[D, 128], [1, 2048]])).then_inc(
                                        sl.h, 16)
                                    sl.n += 16
                                    wv.append((sl, sl.n))
                            sy.dma_start(x1b[0][:],
                                         xB[tau, :, 512 * tc:512 * (tc + 1)]
                                         ).then_inc(s_x1[0].h, 16)
                            s_x1[0].n += 16
                            x1v.append(s_x1[0].n)

                        @blk.tensor
                        def _(te):
                            for kh in range(2):
                                for sl, v in wv[2 * kh:2 * kh + 2]:
                                    te.wait_ge(sl.h, v)
                                wb = wbuf[kh]
                                for k16 in range(16):
                                    k = 16 * kh + k16
                                    whi_ = wb[:, 128 * k16:128 * (k16 + 1)]
                                    wlo_ = wb[:, 2048 + 128 * k16:
                                              2048 + 128 * (k16 + 1)]
                                    rhs = obuf[:, 512 * k:512 * (k + 1)]
                                    eT.inc(te.matmul(ps2[0][:], whi_, rhs,
                                                     start=(k == 0), stop=False))
                                    eT.inc(te.matmul(ps2[0][:], wlo_, rhs,
                                                     start=False,
                                                     stop=(k == KT - 1)))

                        @blk.vector
                        def _(ve):
                            ve.wait_ge(eT.h, bT + 64)
                            ve.wait_ge(s_x1[0].h, x1v[0])
                            eV.inc(ve.tensor_tensor(xob[0][:], ps2[0][:],
                                                    x1b[0][:], ALU.add))

                        @blk.scalar
                        def _(sc):
                            sc.wait_ge(eV.h, bV + 1)
                            sc.dma_start(xC[tau, :, 512 * tc:512 * (tc + 1)],
                                         xob[0][:]).then_inc(s_wr[0].h, 16)
                            s_wr[0].n += 16
                    barrier()
            sst.close()

        if upto in ("mixer", "attn", "full"):
            emit_mixer()
        if upto in ("attn", "full"):
            emit_wo()

        x_final = {"entry": xA, "dequant": xA, "ffn1": xB, "proj": xB,
                   "mixer": xB, "attn": xC, "full": xA}[upto]
        if upto == "full":
            emit_ffn(xC, xA, "2")

        dO = Sem(nc, "s_dO")

        def emit_exit(src_dram, dst_dram, tag):
            slot_reset()
            sst = contextlib.ExitStack()
            sen = sst.enter_context
            tb = [sen(nc.sbuf_tensor(f"ex_{tag}_t{i}", [128, 128], F32)) for i in range(4)]
            ob = [sen(nc.sbuf_tensor(f"ex_{tag}_o{i}", [128, D], F32)) for i in range(2)]
            xpp = [sen(nc.psum_tensor(f"ex_{tag}_p{i}", [128, 512], F32)) for i in range(4)]
            s_t = slots(4)
            b_eT, b_eV, b_dO = eT.n, eV.n, dO.n
            tv = [s_t[i % 4].n + 16 * (i // 4 + 1) for i in range(512)]

            with nc.Block() as blk:
                @blk.sync
                def _(sy):
                    for tt in range(16):
                        for k in range(KT):
                            idx = tt * KT + k
                            if idx >= 4:
                                sy.wait_ge(eT.h, b_eT + idx - 3)
                            sy.dma_start(tb[idx % 4][:],
                                         src_dram[k, :, 128 * tt:128 * (tt + 1)]
                                         ).then_inc(s_t[idx % 4].h, 16)
                            s_t[idx % 4].n += 16

                @blk.tensor
                def _(te):
                    for tt in range(16):
                        for k in range(KT):
                            idx = tt * KT + k
                            te.wait_ge(s_t[idx % 4].h, tv[idx])
                            if idx >= 4:
                                te.wait_ge(eV.h, b_eV + idx - 3)
                            eT.inc(te.transpose(xpp[idx % 4][:, 0:128], tb[idx % 4][:],
                                                idn32[:]))

                @blk.vector
                def _(ve):
                    for tt in range(16):
                        if tt >= 2:
                            ve.wait_ge(dO.h, b_dO + 16 * (tt - 1))
                        for k in range(KT):
                            idx = tt * KT + k
                            ve.wait_ge(eT.h, b_eT + idx + 1)
                            eV.inc(ve.tensor_copy(ob[tt % 2][:, 128 * k:128 * (k + 1)],
                                                  xpp[idx % 4][:, 0:128]))

                @blk.scalar
                def _(sc):
                    for tt in range(16):
                        sc.wait_ge(eV.h, b_eV + 32 * (tt + 1))
                        dO.inc(sc.dma_start(dst_dram[128 * tt:128 * (tt + 1), :],
                                            ob[tt % 2][:]), 16)
            sst.close()

        for name in taps:
            src = {"x0": xA, "x1": xB, "x2": xC}[name]
            barrier()
            emit_exit(src, tap_d[name], name)

        barrier()
        emit_exit(x_final, out_d, "out")

        with nc.Block() as blk:
            @blk.sync
            def _(sy):
                sy.wait_ge(dO.h, dO.n)

        nc.compile()
    return nc


_consts = make_consts()


def _in_map(core_inputs):
    m = dict(core_inputs)
    m.update({"idn32c": _consts["idn32"], "idn16c": _consts["idn16"],
              "ones_kc": _consts["ones_k"], "ones_mc": _consts["ones_m"],
              "ropecc": _consts["ropec"], "ropesc": _consts["ropes"],
              "rotTc": _consts["rotT"], "ltmaskc": _consts["ltmask"],
              "swamc": _consts["swam"], "swam0c": _consts["swam0"],
              "sh16c": _consts["sh16"]})
    return m


def kernel(L0_attn_hidden_states, L0_attn_Wq, L0_attn_Wk, L0_attn_Wv, L0_attn_Wo,
           L0_ffn_qweight, L0_ffn_lut4, L0_ffn_lut8, L0_ffn_bias):
    x = np.ascontiguousarray(np.asarray(L0_attn_hidden_states, dtype=np.float32))
    biasr = np.ascontiguousarray(np.asarray(L0_ffn_bias, dtype=np.float32)
                                 .reshape(32, 128).T)
    shared = {"wq": np.ascontiguousarray(L0_attn_Wq),
              "wk": np.ascontiguousarray(L0_attn_Wk),
              "wv": np.ascontiguousarray(L0_attn_Wv),
              "wo": np.ascontiguousarray(L0_attn_Wo),
              "qw": np.ascontiguousarray(L0_ffn_qweight),
              "lut4": np.ascontiguousarray(L0_ffn_lut4),
              "lut8": np.ascontiguousarray(L0_ffn_lut8), "biasr": biasr}
    nc = build()
    in_maps = [_in_map({"x_in": np.ascontiguousarray(x[c]), **shared})
               for c in range(8)]
    res = run_bass_kernel_spmd(nc, in_maps, core_ids=list(range(8)))
    return np.stack([r["out"] for r in res.results], axis=0)
